# version 10
# baseline (speedup 1.0000x reference)
import numpy as np
import ml_dtypes
from contextlib import ExitStack

import concourse.bass as bass
import concourse.mybir as mybir
from concourse.bass_utils import run_bass_kernel_spmd

F32 = mybir.dt.float32
BF16 = mybir.dt.bfloat16
AF = mybir.ActivationFunctionType
ALU = mybir.AluOpType
AX = mybir.AxisListType

B, S, D, H, DH = 4, 4096, 1024, 16, 64
T = 2048
ND = D // 128
NT = T // 128
NG = T // 512
FF = 4096
NFF = FF // 128
NE = 8
CCAP = 1537
GLOBALS = [0, 585, 1170, 1755, 2340, 2925, 3510, 4095]
VW = H * 65
MAGIC = 12582912.0
BIG = 1e30
EPS = 1e-5

_TAPS = ()


def _bf(a):
    return np.ascontiguousarray(np.asarray(a, np.float32).astype(ml_dtypes.bfloat16))


def build_nc(taps=(), nstage=99):
    nc = bass.Bass("TRN2", target_bir_lowering=False)

    def din(nm, shp, dt=F32):
        return nc.dram_tensor(nm, shp, dt, kind="ExternalInput")

    DI = {}
    for nm, shp, dt in [
        ("xf", [D, T], F32), ("xg", [8, D], F32),
        ("l0wq", [D, D], F32), ("l0wk", [D, D], F32), ("l0wv", [D, D], F32),
        ("l0wo", [D, D], F32), ("l0bq", [D], F32), ("l0bk", [D], F32),
        ("l0bv", [D], F32), ("l0bo", [D], F32),
        ("l0winT", [FF, D], F32), ("l0bin", [FF], F32),
        ("l0wout", [FF, D], F32), ("l0bout", [D], F32),
        ("l1wq", [D, D], F32), ("l1wk", [D, D], F32), ("l1wv", [D, D], F32),
        ("l1wo", [D, D], F32), ("l1bq", [D], F32), ("l1bk", [D], F32),
        ("l1bv", [D], F32), ("l1bo", [D], F32),
        ("l1wrT", [D, NE], F32), ("l1we", [NE, D, D], F32),
        ("identf", [128, 128], F32), ("identb", [128, 128], BF16),
        ("ones1b", [1, 128], BF16), ("ones1f", [1, 128], F32),
        ("onescol", [128, 1], BF16), ("onespair", [128, 2], BF16),
        ("maskl_lhs", [128, 128], BF16), ("maskl_rhs", [128, T], BF16),
        ("maskg_lhs", [128, 128], BF16), ("maskg_rhs", [128, T], BF16),
        ("scat_s", [8, T], BF16),
        ("iota_e", [128, NE], F32), ("revta", [128, NE], F32),
        ("odd1", [1, 1], F32),
    ]:
        DI[nm] = din(nm, shp, dt)

    out_d = nc.dram_tensor("out", [D, T], F32, kind="ExternalOutput")
    tapspec = {
        "xn": ([D, T], BF16), "x1": ([D, T], F32), "hidden": ([FF, T], BF16),
        "x2": ([D, T], F32), "xn2": ([D, T], BF16), "moe": ([D, T], F32),
        "amax": ([1, 2], F32), "hamax": ([1, 2], F32),
        "gc": ([NE, T], F32), "ctx0": ([D, T], BF16), "ctx1": ([D, T], BF16),
        "cnt": ([2, 40], F32),
    }
    tap_d = {tn: nc.dram_tensor("tap_" + tn, tapspec[tn][0], tapspec[tn][1],
                                kind="ExternalOutput") for tn in taps if tn in tapspec}

    def dint(nm, shp, dt=F32):
        if nm in taps:
            return nc.dram_tensor(nm, shp, dt, kind="ExternalOutput")
        return nc.dram_tensor(nm, shp, dt)

    amax_io = (dint("amax_in", [1, 2]), dint("amax_out", [1, 2]))
    hamax_io = (dint("hamax_in", [1, 2]), dint("hamax_out", [1, 2]))
    grow_in = dint("grow_in", [1, 8 * 16 * 66])
    grow_out = dint("grow_out", [2, 8 * 16 * 66])
    cnt_in = dint("cnt_in", [1, 24])
    cnt_out = dint("cnt_out", [2, 24])
    KVH = ND * 128 * T + NT * 128 * VW
    kv_in = dint("kv_in", [1, KVH], BF16)
    kv_out = dint("kv_out", [2, KVH], BF16)
    k0_d = dint("k0_d", [D, T], BF16)
    q0_d = dint("q0_d", [D, T], BF16)
    v0_d = dint("v0_d", [NT * 128, VW], BF16)
    ctx_d = dint("ctx_d", [16 * 65, T], BF16)
    ctf_d = dint("ctf_d", [D, T], BF16)
    x1_d = dint("x1_d", [D, T])
    x2_d = dint("x2_d", [D, T])
    q1_d = dint("q1_d", [D, T], BF16)
    moe_d = dint("moe_d", [D, T])
    ctxb_d = dint("ctxb_d", [16 * 65, T], BF16)
    RG8 = [list(range(8))]
    RG2 = [[0, 1], [2, 3], [4, 5], [6, 7]]

    with ExitStack() as ctx:
        E_ = ctx.enter_context
        sem = {}
        cnt = {}
        for s_ in ["ld", "lg", "lsc", "st", "sg", "t", "v", "s", "cc",
                   "p0", "p1", "p2", "p3", "r0", "r1", "r2", "r3", "k0", "k1",
                   "g0", "g1"]:
            sem[s_] = E_(nc.semaphore("sem_" + s_))
            cnt[s_] = 0

        IDF = E_(nc.sbuf_tensor("IDF", [128, 128], F32))
        IDB = E_(nc.sbuf_tensor("IDB", [128, 128], BF16))
        ONE1B = E_(nc.sbuf_tensor("ONE1B", [1, 128], BF16))
        ONE1F = E_(nc.sbuf_tensor("ONE1F", [1, 128], F32))
        ONECOL = E_(nc.sbuf_tensor("ONECOL", [128, 1], BF16))
        ONEPAIR = E_(nc.sbuf_tensor("ONEPAIR", [128, 2], BF16))
        XN = E_(nc.sbuf_tensor("XN", [128, ND * T], BF16))

        with nc.Block() as blk:
            @blk.sync
            def _(sy):
                for tdst, tsrc in [(IDF, "identf"), (IDB, "identb"), (ONE1B, "ones1b"),
                                   (ONE1F, "ones1f"), (ONECOL, "onescol"),
                                   (ONEPAIR, "onespair")]:
                    sy.dma_start(out=tdst[:, :], in_=DI[tsrc].ap()).then_inc(sem["ld"], 16)
                    cnt["ld"] += 16
                sy.wait_ge(sem["ld"], cnt["ld"])

        def layernorm(x_dram, tag, amax_io_=None):
            TW = NT
            with ExitStack() as c2:
                E2 = c2.enter_context
                Xl = E2(nc.sbuf_tensor(tag + "X", [128, ND * T], F32))
                Xb = E2(nc.sbuf_tensor(tag + "Xb", [128, 2 * T], BF16))
                XSQ = E2(nc.sbuf_tensor(tag + "XSQ", [128, 2 * T], BF16))
                SR = E2(nc.sbuf_tensor(tag + "SR", [1, 2 * T], F32))
                STT_ = E2(nc.sbuf_tensor(tag + "ST", [128, 4 * TW], F32))
                RR = E2(nc.sbuf_tensor(tag + "RR", [1, 2 * T], F32))
                AB = E2(nc.sbuf_tensor(tag + "AB", [128, T], F32))
                BB = E2(nc.sbuf_tensor(tag + "BB", [128, T], F32))
                AM = E2(nc.sbuf_tensor(tag + "AM", [128, 2], F32))
                AM1 = E2(nc.sbuf_tensor(tag + "AM1", [1, 1], F32))
                TMP = E2(nc.sbuf_tensor(tag + "TMP", [128, T], F32))
                PZ = E2(nc.psum_tensor(tag + "PZ", [128, T], F32))
                PZS = E2(nc.psum_tensor(tag + "PZS", [1, T], F32))

                with nc.Block() as blk:
                    @blk.sync
                    def _(sy):
                        sy.wait_ge(sem["ld"], cnt["ld"])
                        for c in range(ND):
                            sy.dma_start(out=Xl[:, c * T:(c + 1) * T],
                                         in_=x_dram[c * 128:(c + 1) * 128, :]).then_inc(sem["ld"], 16)
                    cnt["ld"] += 16 * ND
                    ld1 = cnt["ld"]
                    v0 = cnt["v"]; t0 = cnt["t"]
                    lgx = {0: cnt["g0"], 1: cnt["g1"]}

                    @blk.gpsimd
                    def _(g):
                        for p_ in ["g0", "g1"]:
                            g.wait_ge(sem[p_], cnt[p_])
                        for c in range(ND):
                            p = c % 2
                            g.wait_ge(sem["g%d" % p], lgx[p] + 16 * (c // 2))
                            if c >= 2:
                                g.wait_ge(sem["t"], t0 + c - 1)
                            g.dma_start(out=Xb[:, p * T:(p + 1) * T],
                                        in_=x_dram[c * 128:(c + 1) * 128, :]
                                        ).then_inc(sem["g%d" % p], 16)
                        for p_ in [0, 1]:
                            g.wait_ge(sem["g%d" % p_], lgx[p_] + 16 * 4)
                    cnt["g0"] += 16 * 4
                    cnt["g1"] += 16 * 4

                    @blk.vector
                    def _(v):
                        for c in range(ND):
                            p = c % 2
                            v.wait_ge(sem["g%d" % p], lgx[p] + 16 * (c // 2 + 1))
                            if c >= 2:
                                v.wait_ge(sem["t"], t0 + c - 1)
                            v.tensor_tensor(XSQ[:, p * T:(p + 1) * T], Xb[:, p * T:(p + 1) * T],
                                            Xb[:, p * T:(p + 1) * T],
                                            ALU.mult).then_inc(sem["v"], 1)
                    cnt["v"] += ND

                    @blk.tensor
                    def _(t):
                        for c in range(ND):
                            p = c % 2
                            t.wait_ge(sem["v"], v0 + c + 1)
                            for g4 in range(NG):
                                t.matmul(PZS[0:1, g4 * 512:(g4 + 1) * 512], ONECOL[:, :],
                                         Xb[:, p * T + g4 * 512:p * T + (g4 + 1) * 512],
                                         start=(c == 0), stop=(c == ND - 1))
                                mm = t.matmul(PZ[0:1, g4 * 512:(g4 + 1) * 512], ONECOL[:, :],
                                              XSQ[:, p * T + g4 * 512:p * T + (g4 + 1) * 512],
                                              start=(c == 0), stop=(c == ND - 1))
                                if g4 == NG - 1:
                                    mm.then_inc(sem["t"], 1)
                    cnt["t"] += ND
                    t2 = cnt["t"]

                    @blk.vector
                    def _(v):
                        v.wait_ge(sem["t"], t2)
                        v.tensor_copy(SR[:, 0:T], PZS[0:1, :])
                        v.drain()
                        v.tensor_copy(SR[:, T:2 * T], PZ[0:1, :])
                        v.drain()
                        v.nop().then_inc(sem["v"], 1)
                    cnt["v"] += 1
                    v3 = cnt["v"]

                    @blk.sync
                    def _(sy):
                        sy.wait_ge(sem["v"], v3)
                        sy.wait_ge(sem["ld"], cnt["ld"])
                        sy.dma_start(out=STT_[:, 0:TW], in_=SR[:, 0:T]).then_inc(sem["ld"], 16)
                        sy.dma_start(out=STT_[:, TW:2 * TW], in_=SR[:, T:2 * T]).then_inc(sem["ld"], 16)
                    cnt["ld"] += 32
                    ld2 = cnt["ld"]

                    @blk.vector
                    def _(v):
                        v.wait_ge(sem["ld"], ld2)
                        v.tensor_scalar(STT_[:, 0:TW], STT_[:, 0:TW], 1.0 / D, None, ALU.mult)
                        v.tensor_scalar(STT_[:, TW:2 * TW], STT_[:, TW:2 * TW], 1.0 / D, None, ALU.mult)
                        v.drain()
                        v.tensor_tensor(STT_[:, 2 * TW:3 * TW], STT_[:, 0:TW], STT_[:, 0:TW], ALU.mult)
                        v.drain()
                        v.tensor_tensor(STT_[:, 2 * TW:3 * TW], STT_[:, TW:2 * TW],
                                        STT_[:, 2 * TW:3 * TW], ALU.subtract)
                        v.drain()
                        v.tensor_scalar(STT_[:, 2 * TW:3 * TW], STT_[:, 2 * TW:3 * TW], EPS, None, ALU.add)
                        v.drain()
                        v.nop().then_inc(sem["v"], 1)
                    cnt["v"] += 1
                    v4 = cnt["v"]

                    @blk.scalar
                    def _(s):
                        s.wait_ge(sem["v"], v4)
                        s.activation(STT_[:, 3 * TW:4 * TW], STT_[:, 2 * TW:3 * TW], AF.Sqrt)
                        s.drain()
                        s.nop().then_inc(sem["s"], 1)
                    cnt["s"] += 1
                    s1 = cnt["s"]

                    @blk.vector
                    def _(v):
                        v.wait_ge(sem["s"], s1)
                        v.reciprocal(STT_[:, 2 * TW:3 * TW], STT_[:, 3 * TW:4 * TW])
                        v.drain()
                        v.tensor_tensor(STT_[:, 3 * TW:4 * TW], STT_[:, 0:TW],
                                        STT_[:, 2 * TW:3 * TW], ALU.mult)
                        v.drain()
                        v.tensor_scalar(STT_[:, 3 * TW:4 * TW], STT_[:, 3 * TW:4 * TW],
                                        -1.0, None, ALU.mult)
                        v.drain()
                        v.nop().then_inc(sem["v"], 1)
                    cnt["v"] += 1
                    v5 = cnt["v"]

                    @blk.sync
                    def _(sy):
                        sy.wait_ge(sem["v"], v5)
                        sy.wait_ge(sem["ld"], cnt["ld"])
                        sy.dma_start(out=RR[:, 0:T], in_=STT_[:, 2 * TW:3 * TW]).then_inc(sem["ld"], 16)
                        sy.dma_start(out=RR[:, T:2 * T], in_=STT_[:, 3 * TW:4 * TW]).then_inc(sem["ld"], 16)
                    cnt["ld"] += 32
                    ld3 = cnt["ld"]

                    @blk.tensor
                    def _(t):
                        t.wait_ge(sem["ld"], ld3)
                        for g4 in range(NG):
                            mm = t.matmul(PZ[:, g4 * 512:(g4 + 1) * 512], ONE1F[:, :],
                                          RR[:, g4 * 512:(g4 + 1) * 512], start=True, stop=True)
                        mm.then_inc(sem["t"], 1)
                    cnt["t"] += 1
                    t3 = cnt["t"]

                    @blk.vector
                    def _(v):
                        v.wait_ge(sem["t"], t3)
                        v.tensor_copy(AB[:, :], PZ[:, :])
                        v.drain()
                        v.nop().then_inc(sem["v"], 1)
                    cnt["v"] += 1
                    v6 = cnt["v"]

                    @blk.tensor
                    def _(t):
                        t.wait_ge(sem["v"], v6)
                        for g4 in range(NG):
                            mm = t.matmul(PZ[:, g4 * 512:(g4 + 1) * 512], ONE1F[:, :],
                                          RR[:, T + g4 * 512:T + (g4 + 1) * 512], start=True, stop=True)
                        mm.then_inc(sem["t"], 1)
                    cnt["t"] += 1
                    t4 = cnt["t"]

                    @blk.vector
                    def _(v):
                        v.wait_ge(sem["t"], t4)
                        v.tensor_copy(BB[:, :], PZ[:, :])
                        v.drain()
                        for c in range(ND):
                            v.tensor_tensor(TMP[:, :], Xl[:, c * T:(c + 1) * T], AB[:, :], ALU.mult)
                            v.drain()
                            v.tensor_tensor(XN[:, c * T:(c + 1) * T], TMP[:, :], BB[:, :], ALU.add)
                            v.drain()
                        if amax_io_ is not None:
                            for c in range(ND):
                                v.tensor_reduce(AM[:, 1:2], XN[:, c * T:(c + 1) * T], AX.X,
                                                ALU.max, apply_absolute_value=True)
                                v.drain()
                                if c == 0:
                                    v.tensor_copy(AM[:, 0:1], AM[:, 1:2])
                                else:
                                    v.tensor_tensor(AM[:, 0:1], AM[:, 0:1], AM[:, 1:2], ALU.max)
                                v.drain()
                        v.nop().then_inc(sem["v"], 1)
                    cnt["v"] += 1
                    v7 = cnt["v"]

                    if amax_io_ is not None:
                        @blk.tensor
                        def _(t):
                            t.wait_ge(sem["v"], v7)
                            t.transpose(PZ[0:1, 0:128], AM[:, 0:1], IDF[:, :]).then_inc(sem["t"], 1)
                        cnt["t"] += 1
                        t5 = cnt["t"]

                        @blk.vector
                        def _(v):
                            v.wait_ge(sem["t"], t5)
                            v.tensor_reduce(AM1[:, :], PZ[0:1, 0:128], AX.X, ALU.max)
                            v.drain()
                            v.nop().then_inc(sem["v"], 1)
                        cnt["v"] += 1
                        v8 = cnt["v"]

                        @blk.gpsimd
                        def _(g):
                            g.wait_ge(sem["v"], v8)
                            g.dma_start(out=amax_io_[0].ap()[0:1, 0:1],
                                        in_=AM1[:, :]).then_inc(sem["lg"], 16)
                            g.wait_ge(sem["lg"], cnt["lg"] + 16)
                            g.collective_compute(
                                "AllReduce", ALU.max, replica_groups=RG8,
                                ins=[amax_io_[0].ap()[0:1, 0:1].opt()],
                                outs=[amax_io_[1].ap()[0:1, 0:1].opt()],
                            ).then_inc(sem["cc"], 1)
                        cnt["lg"] += 16
                        cnt["cc"] += 1

        def proj_fm(w_dram, b_dram, dst_dram, tag, dst_sb=None, src_sb=None,
                    res_dram=None, res2_dram=None, out_f32=False):
            src = src_sb if src_sb is not None else XN
            with ExitStack() as c2:
                E2 = c2.enter_context
                WB = E2(nc.sbuf_tensor(tag + "WB", [128, ND * D], BF16))
                BC = E2(nc.sbuf_tensor(tag + "BC", [128, ND], F32))
                OT = E2(nc.sbuf_tensor(tag + "OT", [128, 4 * 512], F32 if out_f32 else BF16))
                RES = E2(nc.sbuf_tensor(tag + "RES", [128, 4 * 512], F32)) \
                    if res_dram is not None else None
                RES2 = E2(nc.sbuf_tensor(tag + "RES2", [128, 4 * 512], F32)) \
                    if res2_dram is not None else None
                PP = E2(nc.psum_tensor(tag + "PP", [128, T], F32))
                with nc.Block() as blk:
                    @blk.sync
                    def _(sy):
                        sy.wait_ge(sem["ld"], cnt["ld"])
                        sy.dma_start(out=BC[:, :],
                                     in_=bass.AP(b_dram, 0, [[1, 128], [128, ND]]),
                                     allow_slow_non_contiguous=True).then_inc(sem["ld"], 16)
                    cnt["ld"] += 16
                    ldb = cnt["ld"]

                    @blk.gpsimd
                    def _(g):
                        g.wait_ge(sem["lg"], cnt["lg"])
                        for c in range(ND):
                            g.dma_start(out=WB[:, c * D:(c + 1) * D],
                                        in_=w_dram[c * 128:(c + 1) * 128, :]).then_inc(sem["lg"], 16)
                    cnt["lg"] += 16 * ND
                    lgw = cnt["lg"]

                    t0 = cnt["t"]; s0 = cnt["s"]
                    pp0 = {p: cnt["p%d" % p] for p in range(4)}
                    rr0 = {p: cnt["r%d" % p] for p in range(4)}

                    nres = (1 if res_dram is not None else 0) + (1 if res2_dram is not None else 0)
                    if res_dram is not None:
                        @blk.scalar
                        def _(s):
                            for p_ in ["r0", "r1", "r2", "r3"]:
                                s.wait_ge(sem[p_], cnt[p_])
                            for i in range(ND * NG):
                                do, g4 = divmod(i, NG)
                                p = i % 4
                                s.wait_ge(sem["r%d" % p], rr0[p] + 16 * nres * (i // 4))
                                if i >= 4:
                                    s.wait_ge(sem["s"], s0 + i - 3)
                                s.dma_start(out=RES[:, p * 512:(p + 1) * 512],
                                            in_=res_dram[do * 128:(do + 1) * 128,
                                                         g4 * 512:(g4 + 1) * 512]
                                            ).then_inc(sem["r%d" % p], 16)
                                if res2_dram is not None:
                                    s.dma_start(out=RES2[:, p * 512:(p + 1) * 512],
                                                in_=res2_dram[do * 128:(do + 1) * 128,
                                                              g4 * 512:(g4 + 1) * 512]
                                                ).then_inc(sem["r%d" % p], 16)
                            for p_ in range(4):
                                s.wait_ge(sem["r%d" % p_], rr0[p_] + 16 * nres * len(
                                    [i for i in range(ND * NG) if i % 4 == p_]))
                        for p_ in range(4):
                            cnt["r%d" % p_] += 16 * nres * len(
                                [i for i in range(ND * NG) if i % 4 == p_])

                    @blk.tensor
                    def _(t):
                        t.wait_ge(sem["lg"], lgw)
                        for i in range(ND * NG):
                            do, g4 = divmod(i, NG)
                            if i >= 4:
                                t.wait_ge(sem["s"], s0 + i - 3)
                            for c in range(ND):
                                mm = t.matmul(PP[:, (i % 4) * 512:((i % 4) + 1) * 512],
                                              WB[:, c * D + do * 128:c * D + (do + 1) * 128],
                                              src[:, c * T + g4 * 512:c * T + (g4 + 1) * 512],
                                              start=(c == 0), stop=(c == ND - 1))
                            mm.then_inc(sem["t"], 1)
                    cnt["t"] += ND * NG

                    @blk.vector
                    def _(v):
                        v.wait_ge(sem["ld"], ldb)
                        for i in range(ND * NG):
                            do, g4 = divmod(i, NG)
                            p = i % 4
                            v.wait_ge(sem["t"], t0 + i + 1)
                            if i >= 4:
                                v.wait_ge(sem["p%d" % p], pp0[p] + 16 * (i // 4))
                            if res_dram is not None:
                                v.wait_ge(sem["r%d" % p], rr0[p] + 16 * nres * (i // 4 + 1))
                                mmv = v.scalar_tensor_tensor(OT[:, p * 512:(p + 1) * 512],
                                                             PP[:, p * 512:(p + 1) * 512],
                                                             BC[:, do:do + 1],
                                                             RES[:, p * 512:(p + 1) * 512],
                                                             ALU.add, ALU.add)
                                if res2_dram is not None:
                                    v.drain()
                                    mmv = v.tensor_tensor(OT[:, p * 512:(p + 1) * 512],
                                                          OT[:, p * 512:(p + 1) * 512],
                                                          RES2[:, p * 512:(p + 1) * 512], ALU.add)
                            else:
                                mmv = v.tensor_scalar(OT[:, (i % 4) * 512:((i % 4) + 1) * 512],
                                                      PP[:, (i % 4) * 512:((i % 4) + 1) * 512],
                                                      BC[:, do:do + 1], None, ALU.add)
                            if dst_sb is not None:
                                mmv = v.tensor_scalar(dst_sb[:, do * T + g4 * 512:do * T + (g4 + 1) * 512],
                                                      PP[:, (i % 4) * 512:((i % 4) + 1) * 512],
                                                      BC[:, do:do + 1], None, ALU.add)
                            mmv.then_inc(sem["s"], 1)
                    cnt["s"] += ND * NG

                    @blk.sync
                    def _(sy):
                        for p in range(4):
                            sy.wait_ge(sem["p%d" % p], pp0[p])
                        for i in range(ND * NG):
                            do, g4 = divmod(i, NG)
                            p = i % 4
                            sy.wait_ge(sem["s"], s0 + i + 1)
                            sy.dma_start(out=dst_dram[do * 128:(do + 1) * 128, g4 * 512:(g4 + 1) * 512],
                                         in_=OT[:, p * 512:(p + 1) * 512]
                                         ).then_inc(sem["p%d" % p], 16)
                        for p_ in range(4):
                            sy.wait_ge(sem["p%d" % p_], pp0[p_] + 16 * len(
                                [i for i in range(ND * NG) if i % 4 == p_]))
                    for p_ in range(4):
                        cnt["p%d" % p_] += 16 * len([i for i in range(ND * NG) if i % 4 == p_])

        def tap_copy(name, src_dram, rows, cols, dt_rows=128):
            if name in tap_d:
                with nc.Block() as blk:
                    @blk.sync
                    def _(sy):
                        n = 0
                        for r0 in range(0, rows, dt_rows):
                            sy.dma_start(out=tap_d[name].ap()[r0:r0 + dt_rows, :],
                                         in_=src_dram[r0:r0 + dt_rows, :]).then_inc(sem["st"], 16)
                            n += 16
                        cnt["st"] += n
                        sy.wait_ge(sem["st"], cnt["st"])

        layernorm(DI["xf"].ap(), "ln1", amax_io if nstage >= 2 else None)
        cc_amax = cnt["cc"]

        if "xn" in tap_d:
            with nc.Block() as blk:
                @blk.sync
                def _(sy):
                    for c in range(ND):
                        sy.dma_start(out=tap_d["xn"].ap()[c * 128:(c + 1) * 128, :],
                                     in_=XN[:, c * T:(c + 1) * T]).then_inc(sem["st"], 16)
                    cnt["st"] += 16 * ND
                    sy.wait_ge(sem["st"], cnt["st"])

        if nstage >= 2:
            proj_fm(DI["l0wk"].ap(), DI["l0bk"], k0_d.ap(), "p2k")
            proj_fm(DI["l0wq"].ap(), DI["l0bq"], q0_d.ap(), "p2q")

        def proj_tm(w_dram, bv_dram, dst_dram, tag, src_sb=None):
            src = src_sb if src_sb is not None else XN
            with ExitStack() as c2:
                E2 = c2.enter_context
                WB = E2(nc.sbuf_tensor(tag + "WB", [128, ND * D], BF16))
                BR = E2(nc.sbuf_tensor(tag + "BR", [1, D], BF16))
                OT = E2(nc.sbuf_tensor(tag + "OT", [128, 4 * VW // 2], BF16))
                ONES = E2(nc.sbuf_tensor(tag + "ONES", [128, 16], BF16))
                PP = E2(nc.psum_tensor(tag + "PP", [128, T], F32))
                with nc.Block() as blk:
                    @blk.gpsimd
                    def _(g):
                        g.wait_ge(sem["lg"], cnt["lg"])
                        g.dma_start(out=BR[:, :],
                                    in_=bass.AP(bv_dram, 0, [[D, 1], [1, D]])).then_inc(sem["lg"], 16)
                        for c in range(ND):
                            g.dma_start(out=WB[:, c * D:(c + 1) * D],
                                        in_=w_dram[c * 128:(c + 1) * 128, :]).then_inc(sem["lg"], 16)
                    cnt["lg"] += 16 * (ND + 1)
                    lgw = cnt["lg"]

                    @blk.vector
                    def _(v):
                        v.memset(ONES[:, :], 1.0)
                        v.drain()
                        v.nop().then_inc(sem["v"], 1)
                    cnt["v"] += 1
                    vON = cnt["v"]

                    t0 = cnt["t"]; v0 = cnt["v"]
                    pp0 = {p: cnt["p%d" % p] for p in range(4)}

                    @blk.tensor
                    def _(t):
                        t.wait_ge(sem["lg"], lgw)
                        for i in range(NT * 2):
                            tt, g2 = divmod(i, 2)
                            if i >= 4:
                                t.wait_ge(sem["v"], v0 + i - 3)
                            for c in range(ND):
                                t.matmul(PP[:, (i % 4) * 512:((i % 4) + 1) * 512],
                                         src[:, c * T + tt * 128:c * T + (tt + 1) * 128],
                                         WB[:, c * D + g2 * 512:c * D + (g2 + 1) * 512],
                                         start=(c == 0), stop=False)
                            t.matmul(PP[:, (i % 4) * 512:((i % 4) + 1) * 512],
                                     ONE1B[0:1, :],
                                     BR[:, g2 * 512:(g2 + 1) * 512],
                                     start=False, stop=True).then_inc(sem["t"], 1)
                    cnt["t"] += NT * 2

                    @blk.vector
                    def _(v):
                        v.wait_ge(sem["v"], vON)
                        for i in range(NT * 2):
                            tt, g2 = divmod(i, 2)
                            p = i % 4
                            v.wait_ge(sem["t"], t0 + i + 1)
                            if i >= 4:
                                v.wait_ge(sem["p%d" % p], pp0[p] + 16 * (i // 4))
                            v.tensor_copy(
                                OT[:, (i % 4) * 520:(i % 4) * 520 + 520].rearrange(
                                    "p (h w) -> p h w", h=8)[:, :, 0:64],
                                PP[:, (i % 4) * 512:((i % 4) + 1) * 512].rearrange(
                                    "p (h w) -> p h w", h=8))
                            v.tensor_copy(OT[:, (i % 4) * 520:(i % 4) * 520 + 520].rearrange(
                                "p (h w) -> p h w", h=8)[:, :, 64:65], ONES[:, 0:8].rearrange(
                                "p (h w) -> p h w", h=8)).then_inc(sem["v"], 1)
                    cnt["v"] += NT * 2

                    @blk.sync
                    def _(sy):
                        for p in range(4):
                            sy.wait_ge(sem["p%d" % p], pp0[p])
                        for i in range(NT * 2):
                            tt, g2 = divmod(i, 2)
                            p = i % 4
                            sy.wait_ge(sem["v"], v0 + i + 1)
                            sy.dma_start(
                                out=dst_dram[tt * 128:(tt + 1) * 128, g2 * 520:(g2 + 1) * 520],
                                in_=OT[:, p * 520:p * 520 + 520]).then_inc(sem["p%d" % p], 16)
                        for p_ in range(4):
                            sy.wait_ge(sem["p%d" % p_], pp0[p_] + 16 * len(
                                [i for i in range(NT * 2) if i % 4 == p_]))
                    for p_ in range(4):
                        cnt["p%d" % p_] += 16 * len([i for i in range(NT * 2) if i % 4 == p_])

        if nstage >= 3:
            proj_tm(DI["l0wv"].ap(), DI["l0bv"], v0_d.ap(), "p2v")

        QGB = E_(nc.sbuf_tensor("QGB", [128, 64], BF16))
        KGB = E_(nc.sbuf_tensor("KGB", [128, 64], BF16))
        VGB = E_(nc.sbuf_tensor("VGB", [8, VW], BF16))
        NMneg = E_(nc.sbuf_tensor("NMneg", [128, 16], F32))
        MHP = E_(nc.sbuf_tensor("MHP", [128, 16], F32))

        def globals_prep(wq_nm, wk_nm, wv_nm, bq_nm, bk_nm, bv_nm, k_dram, q_dram,
                         xg_src, scale_inv, tag):
            with ExitStack() as c2:
                E2 = c2.enter_context
                XGl = E2(nc.sbuf_tensor(tag + "XG", [8, D], F32))
                XNG = E2(nc.sbuf_tensor(tag + "XNG", [8, D], F32))
                GS = E2(nc.sbuf_tensor(tag + "GS", [8, 8], F32))
                XGF = E2(nc.sbuf_tensor(tag + "XGF", [128, 64], BF16))
                WG = E2(nc.sbuf_tensor(tag + "WG", [128, ND * D], BF16))
                BCg = E2(nc.sbuf_tensor(tag + "BCg", [128, ND], F32))
                BRg = E2(nc.sbuf_tensor(tag + "BRg", [1, D], BF16))
                KCH = E2(nc.sbuf_tensor(tag + "KCH", [128, 2 * T], BF16))
                KSQ = E2(nc.sbuf_tensor(tag + "KSQ", [128, T], BF16))
                N2 = E2(nc.sbuf_tensor(tag + "N2", [2, 64], F32))
                NR = E2(nc.sbuf_tensor(tag + "NR", [1, 16], F32))
                PX = E2(nc.psum_tensor(tag + "PX", [128, 512], F32))
                PX2 = E2(nc.psum_tensor(tag + "PX2", [128, 64], F32))
                PV = E2(nc.psum_tensor(tag + "PV", [8, 1024], F32))
                PN = E2(nc.psum_tensor(tag + "PN", [2, T], F32))

                with nc.Block() as blk:
                    @blk.sync
                    def _(sy):
                        sy.wait_ge(sem["ld"], cnt["ld"])
                        sy.dma_start(out=XGl[:, :], in_=xg_src).then_inc(sem["ld"], 16)
                    cnt["ld"] += 16
                    ldg = cnt["ld"]

                    @blk.vector
                    def _(v):
                        v.wait_ge(sem["ld"], ldg)
                        v.tensor_reduce(GS[:, 0:1], XGl[:, :], AX.X, ALU.add)
                        v.drain()
                        v.tensor_scalar(GS[:, 0:1], GS[:, 0:1], 1.0 / D, None, ALU.mult)
                        v.drain()
                        v.tensor_tensor(XNG[:, :], XGl[:, :], XGl[:, :], ALU.mult)
                        v.drain()
                        v.tensor_reduce(GS[:, 1:2], XNG[:, :], AX.X, ALU.add)
                        v.drain()
                        v.tensor_scalar(GS[:, 1:2], GS[:, 1:2], 1.0 / D, None, ALU.mult)
                        v.drain()
                        v.tensor_tensor(GS[:, 2:3], GS[:, 0:1], GS[:, 0:1], ALU.mult)
                        v.drain()
                        v.tensor_tensor(GS[:, 2:3], GS[:, 1:2], GS[:, 2:3], ALU.subtract)
                        v.drain()
                        v.tensor_scalar(GS[:, 2:3], GS[:, 2:3], EPS, None, ALU.add)
                        v.drain()
                        v.nop().then_inc(sem["v"], 1)
                    cnt["v"] += 1
                    vg1 = cnt["v"]

                    @blk.scalar
                    def _(s):
                        s.wait_ge(sem["v"], vg1)
                        s.activation(GS[:, 3:4], GS[:, 2:3], AF.Sqrt)
                        s.drain()
                        s.nop().then_inc(sem["s"], 1)
                    cnt["s"] += 1
                    sg1 = cnt["s"]

                    @blk.vector
                    def _(v):
                        v.wait_ge(sem["s"], sg1)
                        v.reciprocal(GS[:, 4:5], GS[:, 3:4])
                        v.drain()
                        v.tensor_tensor(GS[:, 5:6], GS[:, 0:1], GS[:, 4:5], ALU.mult)
                        v.drain()
                        v.tensor_scalar(GS[:, 5:6], GS[:, 5:6], -1.0, None, ALU.mult)
                        v.drain()
                        v.tensor_scalar(XNG[:, :], XGl[:, :], GS[:, 4:5], GS[:, 5:6],
                                        ALU.mult, ALU.add)
                        v.drain()
                        v.nop().then_inc(sem["v"], 1)
                    cnt["v"] += 1
                    vg2 = cnt["v"]

                    @blk.tensor
                    def _(t):
                        t.wait_ge(sem["v"], vg2)
                        for c in range(ND):
                            t.transpose(PX2[:, c * 8:(c + 1) * 8],
                                        XNG[0:8, c * 128:(c + 1) * 128],
                                        IDF[0:8, 0:8]).then_inc(sem["t"], 1)
                    cnt["t"] += ND
                    tg1 = cnt["t"]

                    @blk.vector
                    def _(v):
                        v.wait_ge(sem["t"], tg1)
                        v.tensor_copy(XGF[:, :], PX2[:, :])
                        v.drain()
                        v.nop().then_inc(sem["v"], 1)
                    cnt["v"] += 1
                    vg3 = cnt["v"]

                def mini_fm(w_nm, b_nm, dstb):
                    with nc.Block() as blk:
                        @blk.gpsimd
                        def _(g):
                            g.wait_ge(sem["lg"], cnt["lg"])
                            for c in range(ND):
                                g.dma_start(out=WG[:, c * D:(c + 1) * D],
                                            in_=DI[w_nm].ap()[c * 128:(c + 1) * 128, :]
                                            ).then_inc(sem["lg"], 16)
                        cnt["lg"] += 16 * ND
                        lgw = cnt["lg"]

                        @blk.sync
                        def _(sy):
                            sy.wait_ge(sem["ld"], cnt["ld"])
                            sy.dma_start(out=BCg[:, :],
                                         in_=bass.AP(DI[b_nm], 0, [[1, 128], [128, ND]]),
                                         allow_slow_non_contiguous=True).then_inc(sem["ld"], 16)
                        cnt["ld"] += 16
                        ldb = cnt["ld"]

                        @blk.tensor
                        def _(t):
                            t.wait_ge(sem["lg"], lgw)
                            for do in range(ND):
                                for c in range(ND):
                                    mm = t.matmul(PX2[:, do * 8:(do + 1) * 8],
                                                  WG[:, c * D + do * 128:c * D + (do + 1) * 128],
                                                  XGF[:, c * 8:(c + 1) * 8],
                                                  start=(c == 0), stop=(c == ND - 1))
                            mm.then_inc(sem["t"], 1)
                        cnt["t"] += 1
                        tq = cnt["t"]

                        @blk.vector
                        def _(v):
                            v.wait_ge(sem["ld"], ldb)
                            v.wait_ge(sem["t"], tq)
                            for do in range(ND):
                                v.tensor_scalar(dstb[:, do * 8:(do + 1) * 8],
                                                PX2[:, do * 8:(do + 1) * 8],
                                                BCg[:, do:do + 1], None, ALU.add)
                            v.drain()
                            v.nop().then_inc(sem["v"], 1)
                        cnt["v"] += 1

                mini_fm(wq_nm, bq_nm, QGB)
                mini_fm(wk_nm, bk_nm, KGB)

                with nc.Block() as blk:
                    @blk.gpsimd
                    def _(g):
                        g.wait_ge(sem["lg"], cnt["lg"])
                        g.dma_start(out=BRg[:, :],
                                    in_=bass.AP(DI[bv_nm], 0, [[D, 1], [1, D]])
                                    ).then_inc(sem["lg"], 16)
                        for c in range(ND):
                            g.dma_start(out=WG[:, c * D:(c + 1) * D],
                                        in_=DI[wv_nm].ap()[c * 128:(c + 1) * 128, :]
                                        ).then_inc(sem["lg"], 16)
                    cnt["lg"] += 16 * (ND + 1)
                    lgv = cnt["lg"]

                    @blk.tensor
                    def _(t):
                        t.wait_ge(sem["lg"], lgv)
                        for g2 in range(2):
                            for c in range(ND):
                                t.matmul(PV[0:8, g2 * 512:(g2 + 1) * 512],
                                         XGF[:, c * 8:(c + 1) * 8],
                                         WG[:, c * D + g2 * 512:c * D + (g2 + 1) * 512],
                                         start=(c == 0), stop=False)
                            t.matmul(PV[0:8, g2 * 512:(g2 + 1) * 512],
                                     ONE1B[0:1, 0:8],
                                     BRg[:, g2 * 512:(g2 + 1) * 512],
                                     start=False, stop=True).then_inc(sem["t"], 1)
                    cnt["t"] += 2
                    tv = cnt["t"]

                    @blk.vector
                    def _(v):
                        v.wait_ge(sem["t"], tv)
                        for g2 in range(2):
                            v.tensor_copy(
                                VGB[:, g2 * 520:(g2 + 1) * 520].rearrange(
                                    "p (h w) -> p h w", h=8)[:, :, 0:64],
                                PV[0:8, g2 * 512:(g2 + 1) * 512].rearrange(
                                    "p (h w) -> p h w", h=8))
                            v.drain()
                        v.memset(VGB[:, :].rearrange("p (h w) -> p h w", h=16)[:, :, 64:65], 1.0)
                        v.drain()
                        v.nop().then_inc(sem["v"], 1)
                    cnt["v"] += 1

                def norms(src_dram, col0):
                    with nc.Block() as blk:
                        ks0 = cnt["ld"]
                        v0_ = cnt["v"]; t0_ = cnt["t"]

                        @blk.sync
                        def _(sy):
                            sy.wait_ge(sem["ld"], cnt["ld"])
                            for c in range(ND):
                                if c >= 2:
                                    sy.wait_ge(sem["v"], v0_ + 2 * (c - 2) + 1)
                                sy.dma_start(out=KCH[:, (c % 2) * T:(c % 2 + 1) * T],
                                             in_=src_dram[c * 128:(c + 1) * 128, :]
                                             ).then_inc(sem["ld"], 16)
                                sy.wait_ge(sem["ld"], ks0 + 16 * (c + 1))
                        cnt["ld"] += 16 * ND

                        @blk.vector
                        def _(v):
                            for c in range(ND):
                                p = c % 2
                                v.wait_ge(sem["ld"], ks0 + 16 * (c + 1))
                                if c >= 1:
                                    v.wait_ge(sem["t"], t0_ + c)
                                v.tensor_tensor(KSQ[:, :], KCH[:, p * T:(p + 1) * T],
                                                KCH[:, p * T:(p + 1) * T], ALU.mult)
                                v.drain()
                                v.nop().then_inc(sem["v"], 1)
                                v.wait_ge(sem["t"], t0_ + c + 1)
                                v.tensor_reduce(N2[0:2, col0 + c:col0 + c + 1], PN[0:2, :],
                                                AX.X, ALU.max)
                                v.drain()
                                v.nop().then_inc(sem["v"], 1)
                        cnt["v"] += 2 * ND

                        @blk.tensor
                        def _(t):
                            for c in range(ND):
                                t.wait_ge(sem["v"], v0_ + 2 * c + 1)
                                for g4 in range(NG):
                                    mm = t.matmul(PN[0:2, g4 * 512:(g4 + 1) * 512],
                                                  ONEPAIR[:, :],
                                                  KSQ[:, g4 * 512:(g4 + 1) * 512],
                                                  start=True, stop=True)
                                mm.then_inc(sem["t"], 1)
                        cnt["t"] += ND

                norms(k_dram, 0)
                norms(q_dram, 8)

                with nc.Block() as blk:
                    @blk.vector
                    def _(v):
                        v.tensor_tensor(KSQ[:, 0:64], QGB[:, :], QGB[:, :], ALU.mult)
                        v.drain()
                        v.tensor_tensor(KSQ[:, 64:128], KGB[:, :], KGB[:, :], ALU.mult)
                        v.drain()
                        v.nop().then_inc(sem["v"], 1)
                    cnt["v"] += 1
                    vq = cnt["v"]

                    @blk.tensor
                    def _(t):
                        t.wait_ge(sem["v"], vq)
                        t.matmul(PN[0:2, 0:64], ONEPAIR[:, :], KSQ[:, 0:64],
                                 start=True, stop=True)
                        t.matmul(PN[0:2, 64:128], ONEPAIR[:, :], KSQ[:, 64:128],
                                 start=True, stop=True).then_inc(sem["t"], 1)
                    cnt["t"] += 1
                    tq2 = cnt["t"]

                    @blk.vector
                    def _(v):
                        v.wait_ge(sem["t"], tq2)
                        for do in range(ND):
                            v.tensor_reduce(N2[0:2, 16 + do:17 + do],
                                            PN[0:2, do * 8:(do + 1) * 8], AX.X, ALU.max)
                            v.drain()
                            v.tensor_reduce(N2[0:2, 24 + do:25 + do],
                                            PN[0:2, 64 + do * 8:64 + (do + 1) * 8], AX.X, ALU.max)
                            v.drain()
                        v.tensor_tensor(N2[0:2, 32:40], N2[0:2, 8:16], N2[0:2, 16:24], ALU.max)
                        v.drain()
                        v.tensor_tensor(N2[0:2, 40:48], N2[0:2, 0:8], N2[0:2, 24:32], ALU.max)
                        v.drain()
                        v.tensor_tensor(N2[0:2, 48:56], N2[0:2, 32:40], N2[0:2, 40:48], ALU.mult)
                        v.drain()
                        v.nop().then_inc(sem["v"], 1)
                    cnt["v"] += 1
                    vq3 = cnt["v"]

                    @blk.scalar
                    def _(s):
                        s.wait_ge(sem["v"], vq3)
                        s.activation(N2[0:2, 56:64], N2[0:2, 48:56], AF.Sqrt)
                        s.drain()
                        s.mul(N2[0:2, 56:64], N2[0:2, 56:64], scale_inv)
                        s.drain()
                        s.nop().then_inc(sem["s"], 1)
                    cnt["s"] += 1
                    sq2 = cnt["s"]

                    @blk.tensor
                    def _(t):
                        t.wait_ge(sem["s"], sq2)
                        t.transpose(PX2[0:8, 0:2], N2[0:2, 56:64],
                                    IDF[0:2, 0:2]).then_inc(sem["t"], 1)
                    cnt["t"] += 1
                    tq25 = cnt["t"]

                    @blk.vector
                    def _(v):
                        v.wait_ge(sem["t"], tq25)
                        v.tensor_copy(GS[:, 6:8], PX2[0:8, 0:2])
                        v.drain()
                        v.nop().then_inc(sem["v"], 1)
                    cnt["v"] += 1
                    vq25 = cnt["v"]

                    @blk.sync
                    def _(sy):
                        sy.wait_ge(sem["v"], vq25)
                        sy.wait_ge(sem["ld"], cnt["ld"])
                        sy.dma_start(out=NR[0:1, 0:16],
                                     in_=GS[0:8, 6:8]).then_inc(sem["ld"], 16)
                    cnt["ld"] += 16
                    ldn = cnt["ld"]

                    @blk.tensor
                    def _(t):
                        t.wait_ge(sem["ld"], ldn)
                        t.matmul(PX[:, 0:16], ONE1F[:, :], NR[:, :],
                                 start=True, stop=True).then_inc(sem["t"], 1)
                    cnt["t"] += 1
                    tq3 = cnt["t"]

                    @blk.vector
                    def _(v):
                        v.wait_ge(sem["t"], tq3)
                        v.tensor_copy(MHP[:, :], PX[:, 0:16])
                        v.drain()
                        v.tensor_scalar(NMneg[:, :], PX[:, 0:16], -1.0, None, ALU.mult)
                        v.drain()
                        v.nop().then_inc(sem["v"], 1)
                    cnt["v"] += 1

        if nstage >= 4:
            globals_prep("l0wq", "l0wk", "l0wv", "l0bq", "l0bk", "l0bv",
                         k0_d.ap(), q0_d.ap(), DI["xg"].ap(), 0.125, "g0")

        def l0_attn_passA(tag):
            with ExitStack() as c2:
                E2 = c2.enter_context
                MRL = E2(nc.sbuf_tensor(tag + "MRL", [128, 128], BF16))
                MRR = E2(nc.sbuf_tensor(tag + "MRR", [128, T], BF16))
                MGL = E2(nc.sbuf_tensor(tag + "MGL", [128, 128], BF16))
                MGR = E2(nc.sbuf_tensor(tag + "MGR", [128, T], BF16))
                KC = E2(nc.sbuf_tensor(tag + "KC", [128, 2 * T], BF16))
                QC = E2(nc.sbuf_tensor(tag + "QC", [128, 2 * T], BF16))
                VH = E2(nc.sbuf_tensor(tag + "VH", [128, 2 * VW], BF16))
                EXT = E2(nc.sbuf_tensor(tag + "EXT", [128, T], BF16))
                EXG = E2(nc.sbuf_tensor(tag + "EXG", [128, T], BF16))
                GEX = E2(nc.sbuf_tensor(tag + "GEX", [128, 128], BF16))
                GRP = E2(nc.sbuf_tensor(tag + "GRP", [128, 8], F32))
                CTG = E2(nc.sbuf_tensor(tag + "CTG", [128, 2 * T], BF16))
                SHb = E2(nc.sbuf_tensor(tag + "SH", [1, 2 * T], F32))
                PS = E2(nc.psum_tensor(tag + "PS", [128, T], F32))
                PG = E2(nc.psum_tensor(tag + "PG", [128, T], F32))

                with nc.Block() as blk:
                    @blk.sync
                    def _(sy):
                        sy.wait_ge(sem["ld"], cnt["ld"])
                        for tdst, tsrc in [(MRL, "maskl_lhs"), (MRR, "maskl_rhs"),
                                           (MGL, "maskg_lhs"), (MGR, "maskg_rhs")]:
                            sy.dma_start(out=tdst[:, :], in_=DI[tsrc].ap()).then_inc(sem["ld"], 16)
                        cnt["ld"] += 16 * 4
                        sy.wait_ge(sem["ld"], cnt["ld"])
                    ldm = cnt["ld"]

                t0 = cnt["t"]; v0 = cnt["v"]; s0 = cnt["s"]
                kq0 = {0: cnt["k0"], 1: cnt["k1"]}
                vh0 = {0: cnt["r2"], 1: cnt["r3"]}
                pc0 = {0: cnt["p0"], 1: cnt["p1"]}
                sf0 = {0: cnt["p2"], 1: cnt["p3"]}

                with nc.Block() as blk:
                    @blk.sync
                    def _(sy):
                        for p_ in ["k0", "k1", "r2", "r3", "p0", "p1", "p2", "p3"]:
                            sy.wait_ge(sem[p_], cnt[p_])
                        for h in range(H):
                            base = 64 * (h % 2)
                            c = h // 2
                            cp = c % 2
                            hp = h % 2
                            if h % 2 == 0:
                                ksem = "k%d" % cp
                                sy.wait_ge(sem[ksem], kq0[cp] + 32 * (c // 2))
                                if c >= 2:
                                    sy.wait_ge(sem["t"], t0 + 5 * (2 * (c - 2) + 1) + 5)
                                sy.dma_start(out=KC[:, cp * T:(cp + 1) * T],
                                             in_=k0_d.ap()[c * 128:(c + 1) * 128, :]
                                             ).then_inc(sem[ksem], 16)
                                sy.dma_start(out=QC[:, cp * T:(cp + 1) * T],
                                             in_=q0_d.ap()[c * 128:(c + 1) * 128, :]
                                             ).then_inc(sem[ksem], 16)
                            vsem = "r%d" % (2 + hp)
                            sy.wait_ge(sem[vsem], vh0[hp] + 16 * (h // 2))
                            if h >= 2:
                                sy.wait_ge(sem["t"], t0 + 5 * (h - 2) + 5)
                            sy.dma_start(out=VH[:, hp * VW:(hp + 1) * VW],
                                         in_=bass.AP(v0_d, h * 65,
                                                     [[VW, 128], [128 * VW, NT], [1, 65]])
                                         ).then_inc(sem[vsem], 16)
                            psem = "p%d" % hp
                            sy.wait_ge(sem[psem], pc0[hp] + 16 * (h // 2))
                            sy.wait_ge(sem["v"], v0 + 3 * h + 3)
                            sy.dma_start(out=ctx_d.ap()[h * 65:(h + 1) * 65, :],
                                         in_=CTG[0:65, hp * T:(hp + 1) * T]
                                         ).then_inc(sem[psem], 16)
                            fsem = "p%d" % (2 + hp)
                            sy.wait_ge(sem[fsem], sf0[hp] + 16 * (h // 2))
                            sy.dma_start(out=bass.AP(S128, h * 16, [[256, 128], [1, 16]]),
                                         in_=SHb[0:1, hp * T:(hp + 1) * T]
                                         ).then_inc(sem[fsem], 16)
                        sy.wait_ge(sem["p0"], pc0[0] + 16 * 8)
                        sy.wait_ge(sem["p1"], pc0[1] + 16 * 8)
                        sy.wait_ge(sem["p2"], sf0[0] + 16 * 8)
                        sy.wait_ge(sem["p3"], sf0[1] + 16 * 8)
                        sy.wait_ge(sem["k0"], kq0[0] + 32 * 4)
                        sy.wait_ge(sem["k1"], kq0[1] + 32 * 4)
                        sy.wait_ge(sem["r2"], vh0[0] + 16 * 8)
                        sy.wait_ge(sem["r3"], vh0[1] + 16 * 8)
                    for hh in range(H):
                        cnt["k%d" % ((hh // 2) % 2)] += 32 if hh % 2 == 0 else 0
                        cnt["r%d" % (2 + hh % 2)] += 16
                        cnt["p%d" % (hh % 2)] += 16
                        cnt["p%d" % (2 + hh % 2)] += 16

                    @blk.tensor
                    def _(t):
                        for h in range(H):
                            base = 64 * (h % 2)
                            c = h // 2
                            cp = c % 2
                            hp = h % 2
                            ksem = "k%d" % cp
                            t.wait_ge(sem[ksem], kq0[cp] + 32 * (c // 2 + 1))
                            t.wait_ge(sem["r%d" % (2 + hp)], vh0[hp] + 16 * (h // 2 + 1))
                            if h >= 1:
                                t.wait_ge(sem["v"], v0 + 3 * (h - 1) + 3)
                            for qt in range(NT):
                                t.matmul(PS[:, qt * 128:(qt + 1) * 128],
                                         MRL[base:base + 4, :],
                                         MRR[base:base + 4, qt * 128:(qt + 1) * 128],
                                         start=True, stop=False)
                                mm = t.matmul(PS[:, qt * 128:(qt + 1) * 128],
                                              KC[base:base + 64, cp * T + qt * 128:cp * T + (qt + 1) * 128],
                                              QC[base:base + 64, cp * T + qt * 128:cp * T + (qt + 1) * 128],
                                              start=False, stop=True)
                            mm.then_inc(sem["t"], 1)
                            if h >= 1:
                                t.wait_ge(sem["v"], v0 + 3 * (h - 1) + 2)
                            for qt in range(NT):
                                t.matmul(PG[:, qt * 128:(qt + 1) * 128],
                                         MGL[base:base + 9, :],
                                         MGR[base:base + 9, qt * 128:(qt + 1) * 128],
                                         start=True, stop=False, skip_group_check=True)
                                mm = t.matmul(PG[0:8, qt * 128:(qt + 1) * 128],
                                              KGB[base:base + 64, c * 8:(c + 1) * 8],
                                              QC[base:base + 64, cp * T + qt * 128:cp * T + (qt + 1) * 128],
                                              start=False, stop=True, skip_group_check=True)
                            mm.then_inc(sem["t"], 1)
                            t.wait_ge(sem["s"], s0 + 3 * h + 2)
                            for kt in range(NT):
                                mm = t.matmul(PG[:, kt * 8:(kt + 1) * 8],
                                              KC[base:base + 64, cp * T + kt * 128:cp * T + (kt + 1) * 128],
                                              QGB[base:base + 64, c * 8:(c + 1) * 8],
                                              start=True, stop=True)
                            mm.then_inc(sem["t"], 1)
                            t.wait_ge(sem["s"], s0 + 3 * h + 3)
                            for qt in range(NT):
                                t.matmul(PS[0:65, qt * 128:(qt + 1) * 128],
                                         VH[:, hp * VW + qt * 65:hp * VW + (qt + 1) * 65],
                                         EXT[:, qt * 128:(qt + 1) * 128],
                                         start=True, stop=False)
                                t.matmul(PS[0:65, qt * 128:(qt + 1) * 128],
                                         VGB[0:8, h * 65:(h + 1) * 65],
                                         EXG[0:8, qt * 128:(qt + 1) * 128],
                                         start=False, stop=True)
                            for kt in range(NT):
                                mm = t.matmul(PG[0:65, 512:520],
                                              VH[:, hp * VW + kt * 65:hp * VW + (kt + 1) * 65],
                                              GEX[:, kt * 8:(kt + 1) * 8],
                                              start=(kt == 0), stop=(kt == NT - 1))
                            mm.then_inc(sem["t"], 1)
                            t.wait_ge(sem["v"], v0 + 3 * h + 1)
                            t.transpose(PG[0:8, 640:705], GRP[0:65, :],
                                        IDF[0:65, 0:65]).then_inc(sem["t"], 1)
                    cnt["t"] += 5 * H

                    @blk.scalar
                    def _(s):
                        for h in range(H):
                            if h >= 1:
                                s.wait_ge(sem["t"], t0 + 5 * (h - 1) + 4)
                            s.wait_ge(sem["t"], t0 + 5 * h + 1)
                            s.activation(EXT[:, :], PS[:, :], AF.Exp,
                                         bias=NMneg[:, h:h + 1],
                                         scale=0.125).then_inc(sem["s"], 1)
                            s.wait_ge(sem["t"], t0 + 5 * h + 2)
                            s.activation(EXG[:, :], PG[:, :], AF.Exp,
                                         bias=NMneg[:, h:h + 1],
                                         scale=0.125).then_inc(sem["s"], 1)
                            s.wait_ge(sem["t"], t0 + 5 * h + 3)
                            s.activation(GEX[:, :], PG[:, 0:128], AF.Exp,
                                         bias=NMneg[:, h:h + 1],
                                         scale=0.125).then_inc(sem["s"], 1)
                    cnt["s"] += 3 * H

                    @blk.vector
                    def _(v):
                        for h in range(H):
                            hp = h % 2
                            v.wait_ge(sem["t"], t0 + 5 * h + 4)
                            v.tensor_copy(GRP[0:65, :],
                                          PG[0:65, 512:520]).then_inc(sem["v"], 1)
                            v.wait_ge(sem["t"], t0 + 5 * h + 5)
                            v.tensor_copy(PAY[0:8, h * 66:h * 66 + 65], PG[0:8, 640:705])
                            v.tensor_copy(PAY[0:8, h * 66 + 65:h * 66 + 66],
                                          MHP[0:8, h:h + 1]).then_inc(sem["v"], 1)
                            if h >= 2:
                                v.wait_ge(sem["p%d" % hp], pc0[hp] + 16 * (h // 2))
                                v.wait_ge(sem["p%d" % (2 + hp)], sf0[hp] + 16 * (h // 2))
                            v.tensor_copy(CTG[0:65, hp * T:(hp + 1) * T], PS[0:65, :])
                            v.tensor_copy(SHb[0:1, hp * T:(hp + 1) * T],
                                          PS[64:65, :]).then_inc(sem["v"], 1)
                    cnt["v"] += 3 * H

                with nc.Block() as blk:
                    @blk.gpsimd
                    def _(g):
                        g.wait_ge(sem["lg"], cnt["lg"])
                        g.dma_start(out=grow_in.ap()[0:1, :],
                                    in_=PAY[0:8, :]).then_inc(sem["lg"], 16)
                        g.wait_ge(sem["lg"], cnt["lg"] + 16)
                        g.collective_compute(
                            "AllGather", ALU.bypass, replica_groups=RG2,
                            ins=[grow_in.ap().opt()], outs=[grow_out.ap().opt()],
                        ).then_inc(sem["cc"], 1)
                        g.wait_ge(sem["cc"], cnt["cc"] + 1)
                        g.dma_start(out=PAb[0:8, :], in_=grow_out.ap()[0:1, :]
                                    ).then_inc(sem["lg"], 16)
                        g.dma_start(out=PBb[0:8, :], in_=grow_out.ap()[1:2, :]
                                    ).then_inc(sem["lg"], 16)
                        g.wait_ge(sem["lg"], cnt["lg"] + 48)
                    cnt["lg"] += 48
                    cnt["cc"] += 1

        ctx45 = ExitStack()
        if nstage >= 5:
            E45 = ctx45.enter_context
            PAY = E45(nc.sbuf_tensor("PAY", [8, 16 * 66], F32))
            PAb = E45(nc.sbuf_tensor("PAb", [8, 16 * 66], F32))
            PBb = E45(nc.sbuf_tensor("PBb", [8, 16 * 66], F32))
            S128 = E45(nc.sbuf_tensor("S128", [128, 16 * 16], F32))
            l0_attn_passA("a0")

        def l0_attn_passB(ctxsrc_d, ctxtmp_d, ctf_dst, tag):
            with ExitStack() as c2:
                E2 = c2.enter_context
                SCT = E2(nc.sbuf_tensor(tag + "SCT", [8, T], BF16))
                CMB = E2(nc.sbuf_tensor(tag + "CMB", [8, 16 * 65], BF16))
                CM = E2(nc.sbuf_tensor(tag + "CM", [8, 6 * 16], F32))
                CTG2 = E2(nc.sbuf_tensor(tag + "CTG2", [128, 2 * T], BF16))
                SH2 = E2(nc.sbuf_tensor(tag + "SH2", [1, 2 * T], F32))
                SP128 = E2(nc.sbuf_tensor(tag + "SP128", [128, 256], F32))
                R128 = E2(nc.sbuf_tensor(tag + "R128", [128, 256], BF16))
                RBr = E2(nc.sbuf_tensor(tag + "RBr", [1, 2 * T], BF16))
                CTF1 = E2(nc.sbuf_tensor(tag + "CTF1", [128, 2 * T], BF16))
                PSC = E2(nc.psum_tensor(tag + "PSC", [128, T], F32))
                PRB = E2(nc.psum_tensor(tag + "PRB", [128, T], F32))

                with nc.Block() as blk:
                    @blk.sync
                    def _(sy):
                        sy.wait_ge(sem["ld"], cnt["ld"])
                        sy.dma_start(out=SCT[:, :], in_=DI["scat_s"].ap()).then_inc(sem["ld"], 16)
                    cnt["ld"] += 16
                    lds = cnt["ld"]

                    @blk.vector
                    def _(v):
                        PAv = PAb[0:8, :].rearrange("p (h w) -> p h w", h=16)[:, :, 65:66]
                        PBv = PBb[0:8, :].rearrange("p (h w) -> p h w", h=16)[:, :, 65:66]
                        v.tensor_tensor(CM[:, 0:16], PAv, PBv, ALU.max)
                        v.drain()
                        v.tensor_tensor(CM[:, 16:32], PAv, CM[:, 0:16], ALU.subtract)
                        v.drain()
                        v.tensor_tensor(CM[:, 32:48], PBv, CM[:, 0:16], ALU.subtract)
                        v.drain()
                        v.nop().then_inc(sem["v"], 1)
                    cnt["v"] += 1
                    vb1 = cnt["v"]

                    @blk.scalar
                    def _(s):
                        s.wait_ge(sem["v"], vb1)
                        s.activation(CM[:, 48:64], CM[:, 16:32], AF.Exp)
                        s.drain()
                        s.activation(CM[:, 64:80], CM[:, 32:48], AF.Exp)
                        s.drain()
                        s.nop().then_inc(sem["s"], 1)
                    cnt["s"] += 1
                    sb1 = cnt["s"]

                    @blk.vector
                    def _(v):
                        v.wait_ge(sem["s"], sb1)
                        for h in range(H):
                            v.tensor_scalar(CM[:, 80:96][:, 0:1].rearrange("p a -> p a")
                                            if False else CMB[0:8, h * 65:(h + 1) * 65],
                                            PAb[0:8, h * 66:h * 66 + 65],
                                            CM[:, 48 + h:49 + h], None, ALU.mult)
                            v.drain()
                            v.tensor_scalar(CTG2[0:8, 0:65], PBb[0:8, h * 66:h * 66 + 65],
                                            CM[:, 64 + h:65 + h], None, ALU.mult)
                            v.drain()
                            v.tensor_tensor(CMB[0:8, h * 65:(h + 1) * 65],
                                            CMB[0:8, h * 65:(h + 1) * 65],
                                            CTG2[0:8, 0:65], ALU.add)
                            v.drain()
                        v.nop().then_inc(sem["v"], 1)
                    cnt["v"] += 1

                t0 = cnt["t"]; v0 = cnt["v"]
                pc0 = {0: cnt["p0"], 1: cnt["p1"]}
                ps0 = {0: cnt["p2"], 1: cnt["p3"]}
                with nc.Block() as blk:
                    @blk.tensor
                    def _(t):
                        t.wait_ge(sem["ld"], lds)
                        for h in range(H):
                            if h >= 1:
                                t.wait_ge(sem["v"], v0 + 2 * h)
                            for g4 in range(NG):
                                mm = t.matmul(PSC[0:65, g4 * 512:(g4 + 1) * 512],
                                              CMB[0:8, h * 65:(h + 1) * 65],
                                              SCT[0:8, g4 * 512:(g4 + 1) * 512],
                                              start=True, stop=True)
                            mm.then_inc(sem["t"], 1)
                    cnt["t"] += H

                    @blk.sync
                    def _(sy):
                        for p_ in ["p0", "p1", "p2", "p3"]:
                            sy.wait_ge(sem[p_], cnt[p_])
                        for h in range(H):
                            hp = h % 2
                            sy.wait_ge(sem["p%d" % hp], pc0[hp] + 32 * (h // 2))
                            if h >= 2:
                                sy.wait_ge(sem["v"], v0 + 2 * (h - 2) + 1)
                            sy.dma_start(out=CTG2[0:65, hp * T:(hp + 1) * T],
                                         in_=ctxsrc_d[h * 65:(h + 1) * 65, :]
                                         ).then_inc(sem["p%d" % hp], 16)
                            sy.wait_ge(sem["v"], v0 + 2 * h + 1)
                            sy.dma_start(out=ctxtmp_d[h * 65:(h + 1) * 65, :],
                                         in_=CTG2[0:65, hp * T:(hp + 1) * T]
                                         ).then_inc(sem["p%d" % hp], 16)
                            sy.wait_ge(sem["p%d" % (2 + hp)], ps0[hp] + 16 * (h // 2))
                            sy.wait_ge(sem["v"], v0 + 2 * h + 2)
                            sy.dma_start(out=bass.AP(SP128, h * 16, [[256, 128], [1, 16]]),
                                         in_=SH2[0:1, hp * T:(hp + 1) * T]
                                         ).then_inc(sem["p%d" % (2 + hp)], 16)
                        sy.wait_ge(sem["p0"], pc0[0] + 32 * 8)
                        sy.wait_ge(sem["p1"], pc0[1] + 32 * 8)
                        sy.wait_ge(sem["p2"], ps0[0] + 16 * 8)
                        sy.wait_ge(sem["p3"], ps0[1] + 16 * 8)
                    for hh in range(H):
                        cnt["p%d" % (hh % 2)] += 32
                        cnt["p%d" % (2 + hh % 2)] += 16

                    @blk.vector
                    def _(v):
                        for h in range(H):
                            hp = h % 2
                            v.wait_ge(sem["t"], t0 + h + 1)
                            v.wait_ge(sem["p%d" % hp], pc0[hp] + 32 * (h // 2) + 16)
                            v.tensor_tensor(CTG2[0:65, hp * T:(hp + 1) * T],
                                            CTG2[0:65, hp * T:(hp + 1) * T],
                                            PSC[0:65, :], ALU.add).then_inc(sem["v"], 1)
                            if h >= 2:
                                v.wait_ge(sem["p%d" % (2 + hp)], ps0[hp] + 16 * (h // 2))
                            v.tensor_copy(SH2[0:1, hp * T:(hp + 1) * T],
                                          PSC[64:65, :]).then_inc(sem["v"], 1)
                    cnt["v"] += 2 * H

                with nc.Block() as blk:
                    @blk.vector
                    def _(v):
                        v.tensor_tensor(S128[:, :], S128[:, :], SP128[:, :], ALU.add)
                        v.drain()
                        v.tensor_scalar(S128[:, :], S128[:, :], 1e-30, None, ALU.max)
                        v.drain()
                        with nc.allow_low_precision(reason="bf16 normalizer broadcast"):
                            v.reciprocal(R128[:, :], S128[:, :])
                        v.drain()
                        v.nop().then_inc(sem["v"], 1)
                    cnt["v"] += 1

                t1 = cnt["t"]; v1 = cnt["v"]
                pr0 = {0: cnt["r0"], 1: cnt["r1"]}
                pf0 = {0: cnt["p0"], 1: cnt["p1"]}
                with nc.Block() as blk:
                    @blk.sync
                    def _(sy):
                        for p_ in ["r0", "r1", "p0", "p1"]:
                            sy.wait_ge(sem[p_], cnt[p_])
                        for h in range(H):
                            hp = h % 2
                            sy.wait_ge(sem["r%d" % hp], pr0[hp] + 16 * (h // 2))
                            if h >= 2:
                                sy.wait_ge(sem["t"], t1 + h - 1)
                            sy.dma_start(out=RBr[0:1, hp * T:(hp + 1) * T],
                                         in_=bass.AP(R128, h * 16, [[256, 128], [1, 16]])
                                         ).then_inc(sem["r%d" % hp], 16)
                            sy.wait_ge(sem["p%d" % hp], pf0[hp] + 32 * (h // 2))
                            if h >= 2:
                                sy.wait_ge(sem["v"], v1 + h - 1)
                            sy.dma_start(out=CTG2[0:65, hp * T:(hp + 1) * T],
                                         in_=ctxtmp_d[h * 65:(h + 1) * 65, :]
                                         ).then_inc(sem["p%d" % hp], 16)
                            sy.wait_ge(sem["v"], v1 + h + 1)
                            sy.dma_start(out=ctf_dst[h * 64:(h + 1) * 64, :],
                                         in_=CTF1[0:64, hp * T:(hp + 1) * T]
                                         ).then_inc(sem["p%d" % hp], 16)
                        sy.wait_ge(sem["r0"], pr0[0] + 16 * 8)
                        sy.wait_ge(sem["r1"], pr0[1] + 16 * 8)
                        sy.wait_ge(sem["p0"], pf0[0] + 32 * 8)
                        sy.wait_ge(sem["p1"], pf0[1] + 32 * 8)
                    for hh in range(H):
                        cnt["r%d" % (hh % 2)] += 16
                        cnt["p%d" % (hh % 2)] += 32

                    @blk.tensor
                    def _(t):
                        for h in range(H):
                            hp = h % 2
                            t.wait_ge(sem["r%d" % hp], pr0[hp] + 16 * (h // 2 + 1))
                            if h >= 1:
                                t.wait_ge(sem["v"], v1 + h)
                            for g4 in range(NG):
                                mm = t.matmul(PRB[:, g4 * 512:(g4 + 1) * 512], ONE1B[:, :],
                                              RBr[0:1, hp * T + g4 * 512:hp * T + (g4 + 1) * 512],
                                              start=True, stop=True)
                            mm.then_inc(sem["t"], 1)
                    cnt["t"] += H

                    @blk.vector
                    def _(v):
                        for h in range(H):
                            hp = h % 2
                            v.wait_ge(sem["t"], t1 + h + 1)
                            v.wait_ge(sem["p%d" % hp], pf0[hp] + 32 * (h // 2) + 16)
                            v.tensor_tensor(CTF1[0:64, hp * T:(hp + 1) * T],
                                            CTG2[0:64, hp * T:(hp + 1) * T],
                                            PRB[0:64, :], ALU.mult).then_inc(sem["v"], 1)
                    cnt["v"] += H

        if nstage >= 6:
            l0_attn_passB(ctx_d.ap(), ctxb_d.ap(), ctf_d.ap(), "b0")
        ctx45.close()

        if nstage >= 7:
            with ExitStack() as c5:
                CTFS = c5.enter_context(nc.sbuf_tensor("CTFS", [128, ND * T], BF16))
                with nc.Block() as blk:
                    @blk.sync
                    def _(sy):
                        sy.wait_ge(sem["ld"], cnt["ld"])
                        for c in range(ND):
                            sy.dma_start(out=CTFS[:, c * T:(c + 1) * T],
                                         in_=ctf_d.ap()[c * 128:(c + 1) * 128, :]
                                         ).then_inc(sem["ld"], 16)
                        cnt["ld"] += 16 * ND
                        sy.wait_ge(sem["ld"], cnt["ld"])
                proj_fm(DI["l0wo"].ap(), DI["l0bo"], x1_d.ap(), "p5o",
                        src_sb=CTFS, res_dram=DI["xf"].ap(), out_f32=True)


        woq_d = dint("woq_d", [FF, D], BF16)

        def quant_ffn(tag):
            with ExitStack() as c2:
                E2 = c2.enter_context
                HID = E2(nc.sbuf_tensor(tag + "HID", [128, NFF * T], BF16))
                TMP = E2(nc.sbuf_tensor(tag + "TMP", [128, T], F32))
                SCL = E2(nc.sbuf_tensor(tag + "SCL", [128, 4], F32))
                SC1 = E2(nc.sbuf_tensor(tag + "SC1", [1, 4], F32))
                WST = E2(nc.sbuf_tensor(tag + "WST", [128, 2 * D], F32))
                WQT = E2(nc.sbuf_tensor(tag + "WQT", [128, D], F32))
                WQB = E2(nc.sbuf_tensor(tag + "WQB", [128, D], BF16))
                WQ = E2(nc.sbuf_tensor(tag + "WQ", [128, 2 * D], BF16))
                QS = E2(nc.sbuf_tensor(tag + "QS", [128, 8], F32))
                QSA = E2(nc.sbuf_tensor(tag + "QSA", [128, 2 * NFF], F32))
                QSB = E2(nc.sbuf_tensor(tag + "QSB", [128, 2 * NFF], F32))
                BIN = E2(nc.sbuf_tensor(tag + "BIN", [128, NFF], F32))
                BOUT = E2(nc.sbuf_tensor(tag + "BOUT", [128, ND], F32))
                HAM = E2(nc.sbuf_tensor(tag + "HAM", [128, NFF + 2], F32))
                X1R = E2(nc.sbuf_tensor(tag + "X1R", [128, 2 * 512], F32))
                OT2 = E2(nc.sbuf_tensor(tag + "OT2", [128, 2 * 512], F32))
                PH = E2(nc.psum_tensor(tag + "PH", [128, T], F32))
                PT = E2(nc.psum_tensor(tag + "PT", [128, 2048], BF16))
                PO = E2(nc.psum_tensor(tag + "PO", [128, 512], F32))

                with nc.Block() as blk:
                    @blk.gpsimd
                    def _(g):
                        g.wait_ge(sem["cc"], cc_amax)
                        g.wait_ge(sem["lg"], cnt["lg"])
                        g.dma_start(out=SC1[0:1, 0:1],
                                    in_=amax_io[1].ap()[0:1, 0:1]).then_inc(sem["lg"], 16)
                        g.wait_ge(sem["lg"], cnt["lg"] + 16)
                    cnt["lg"] += 16
                    lgA = cnt["lg"]

                    @blk.vector
                    def _(v):
                        v.wait_ge(sem["lg"], lgA)
                        v.memset(SC1[0:1, 3:4], 0.0)
                        v.drain()
                        v.tensor_scalar(SC1[0:1, 0:1], SC1[0:1, 0:1], 1e-8, None, ALU.max)
                        v.drain()
                        v.reciprocal(SC1[0:1, 1:2], SC1[0:1, 0:1])
                        v.drain()
                        v.tensor_scalar(SC1[0:1, 1:2], SC1[0:1, 1:2], 32767.0, None, ALU.mult)
                        v.drain()
                        v.tensor_scalar(SC1[0:1, 2:3], SC1[0:1, 0:1], 1.0 / 32767.0, None, ALU.mult)
                        v.drain()
                        v.nop().then_inc(sem["v"], 1)
                    cnt["v"] += 1
                    vsc = cnt["v"]

                    @blk.tensor
                    def _(t):
                        t.wait_ge(sem["v"], vsc)
                        t.matmul(PO[:, 0:4], ONE1F[:, :], SC1[0:1, 0:4],
                                 start=True, stop=True).then_inc(sem["t"], 1)
                    cnt["t"] += 1
                    tsc = cnt["t"]

                    @blk.vector
                    def _(v):
                        v.wait_ge(sem["t"], tsc)
                        v.tensor_copy(SCL[:, :], PO[:, 0:4])
                        v.drain()
                        for c in range(ND):
                            v.tensor_scalar(TMP[:, :], XN[:, c * T:(c + 1) * T],
                                            SCL[:, 1:2], MAGIC, ALU.mult, ALU.add)
                            v.drain()
                            v.tensor_scalar(TMP[:, :], TMP[:, :], -MAGIC, SCL[:, 2:3],
                                            ALU.add, ALU.mult)
                            v.drain()
                            v.tensor_copy(XN[:, c * T:(c + 1) * T], TMP[:, :])
                            v.drain()
                        v.nop().then_inc(sem["v"], 1)
                    cnt["v"] += 1

                with nc.Block() as blk:
                    @blk.sync
                    def _(sy):
                        sy.wait_ge(sem["ld"], cnt["ld"])
                        sy.dma_start(out=BIN[:, :],
                                     in_=bass.AP(DI["l0bin"], 0, [[1, 128], [128, NFF]]),
                                     allow_slow_non_contiguous=True).then_inc(sem["ld"], 16)
                        sy.dma_start(out=BOUT[:, :],
                                     in_=bass.AP(DI["l0bout"], 0, [[1, 128], [128, ND]]),
                                     allow_slow_non_contiguous=True).then_inc(sem["ld"], 16)
                    cnt["ld"] += 32
                    ldb8 = cnt["ld"]
                    w0 = {0: cnt["r0"], 1: cnt["r1"]}
                    t80 = cnt["t"]; v80 = cnt["v"]; s80 = cnt["s"]

                    @blk.sync
                    def _(sy):
                        for p_ in ["r0", "r1"]:
                            sy.wait_ge(sem[p_], cnt[p_])
                        for fc in range(NFF):
                            fp = fc % 2
                            sy.wait_ge(sem["r%d" % fp], w0[fp] + 16 * (fc // 2))
                            if fc >= 2:
                                sy.wait_ge(sem["v"], v80 + 4 * (fc - 2) + 2)
                            sy.dma_start(out=WST[:, fp * D:(fp + 1) * D],
                                         in_=DI["l0winT"].ap()[fc * 128:(fc + 1) * 128, :]
                                         ).then_inc(sem["r%d" % fp], 16)
                        for p_ in [0, 1]:
                            sy.wait_ge(sem["r%d" % p_], w0[p_] + 16 * 16)
                    cnt["r0"] += 16 * 16
                    cnt["r1"] += 16 * 16

                    @blk.vector
                    def _(v):
                        for fc in range(NFF):
                            fp = fc % 2
                            v.wait_ge(sem["r%d" % fp], w0[fp] + 16 * (fc // 2 + 1))
                            v.tensor_reduce(QS[:, 0:1], WST[:, fp * D:(fp + 1) * D],
                                            AX.X, ALU.max, apply_absolute_value=True)
                            v.drain()
                            v.tensor_scalar(QS[:, 0:1], QS[:, 0:1], 1e-8, None, ALU.max)
                            v.drain()
                            v.reciprocal(QS[:, 1:2], QS[:, 0:1])
                            v.drain()
                            v.tensor_scalar(QS[:, 1:2], QS[:, 1:2], 127.0, None, ALU.mult)
                            v.drain()
                            v.tensor_scalar(QS[:, 2:3], QS[:, 0:1], 1.0 / 127.0,
                                            None, ALU.mult).then_inc(sem["v"], 1)
                            v.drain()
                            v.tensor_scalar(WQT[:, :], WST[:, fp * D:(fp + 1) * D],
                                            QS[:, 1:2], MAGIC, ALU.mult, ALU.add)
                            v.drain()
                            v.tensor_scalar(WQB[:, :], WQT[:, :], -MAGIC, QS[:, 2:3],
                                            ALU.add, ALU.mult).then_inc(sem["v"], 1)
                            tr_done = (t80 + 8 if fc == 0 else t80 + 16 if fc == 1
                                       else t80 + 10 * (fc - 2) + 26)
                            v.wait_ge(sem["t"], tr_done)
                            if fc >= 2:
                                v.wait_ge(sem["t"], t80 + 10 * (fc - 2) + 18)
                            v.tensor_copy(WQ[:, fp * D:(fp + 1) * D],
                                          PT[:, (fc % 2) * D:(fc % 2 + 1) * D])
                            v.drain()
                            v.nop().then_inc(sem["v"], 1)
                            v.nop().then_inc(sem["v"], 1)
                    cnt["v"] += 4 * NFF

                    @blk.tensor
                    def _(t):
                        def transp(fc):
                            t.wait_ge(sem["v"], v80 + 4 * fc + 2)
                            if fc >= 2:
                                t.wait_ge(sem["v"], v80 + 4 * (fc - 2) + 3)
                            for j in range(ND):
                                t.transpose(PT[:, (fc % 2) * D + j * 128:(fc % 2) * D + (j + 1) * 128],
                                            WQB[:, j * 128:(j + 1) * 128],
                                            IDB[:, :]).then_inc(sem["t"], 1)

                        transp(0)
                        transp(1)
                        for fc in range(NFF):
                            fp = fc % 2
                            t.wait_ge(sem["v"], v80 + 4 * fc + 4)
                            if fc >= 1:
                                t.wait_ge(sem["s"], s80 + fc)
                            for g4 in range(NG):
                                for j in range(ND):
                                    mm = t.matmul(PH[:, g4 * 512:(g4 + 1) * 512],
                                                  WQ[:, fp * D + j * 128:fp * D + (j + 1) * 128],
                                                  XN[:, j * T + g4 * 512:j * T + (g4 + 1) * 512],
                                                  start=(j == 0), stop=(j == ND - 1))
                            mm.then_inc(sem["t"], 2)
                            if fc + 2 < NFF:
                                transp(fc + 2)
                    cnt["t"] += 10 * NFF

                    @blk.scalar
                    def _(s):
                        s.wait_ge(sem["ld"], ldb8)
                        for fc in range(NFF):
                            s.wait_ge(sem["t"], min(t80 + 10 * fc + 18, t80 + 10 * NFF))
                            s.activation(HID[:, fc * T:(fc + 1) * T], PH[:, :], AF.Gelu,
                                         bias=BIN[:, fc:fc + 1],
                                         scale=1.0).then_inc(sem["s"], 1)
                    cnt["s"] += NFF


                with nc.Block() as blk:
                    @blk.vector
                    def _(v):
                        for fc in range(NFF):
                            v.tensor_reduce(HAM[:, fc:fc + 1], HID[:, fc * T:(fc + 1) * T],
                                            AX.X, ALU.max, apply_absolute_value=True)
                        v.drain()
                        v.tensor_reduce(HAM[:, NFF:NFF + 1], HAM[:, 0:NFF], AX.X, ALU.max)
                        v.drain()
                        v.nop().then_inc(sem["v"], 1)
                    cnt["v"] += 1
                    vh1 = cnt["v"]

                    @blk.tensor
                    def _(t):
                        t.wait_ge(sem["v"], vh1)
                        t.transpose(PO[0:1, 0:128], HAM[:, NFF:NFF + 1],
                                    IDF[:, :]).then_inc(sem["t"], 1)
                    cnt["t"] += 1
                    th1 = cnt["t"]

                    @blk.vector
                    def _(v):
                        v.wait_ge(sem["t"], th1)
                        v.tensor_reduce(SC1[0:1, 0:1], PO[0:1, 0:128], AX.X, ALU.max)
                        v.drain()
                        v.nop().then_inc(sem["v"], 1)
                    cnt["v"] += 1
                    vh2 = cnt["v"]

                    @blk.gpsimd
                    def _(g):
                        g.wait_ge(sem["v"], vh2)
                        g.wait_ge(sem["lg"], cnt["lg"])
                        g.dma_start(out=hamax_io[0].ap()[0:1, 0:1],
                                    in_=SC1[0:1, 0:1]).then_inc(sem["lg"], 16)
                        g.wait_ge(sem["lg"], cnt["lg"] + 16)
                        g.collective_compute(
                            "AllReduce", ALU.max, replica_groups=RG8,
                            ins=[hamax_io[0].ap()[0:1, 0:1].opt()],
                            outs=[hamax_io[1].ap()[0:1, 0:1].opt()],
                        ).then_inc(sem["cc"], 1)
                        g.wait_ge(sem["cc"], cnt["cc"] + 1)
                        g.dma_start(out=SC1[0:1, 0:1],
                                    in_=hamax_io[1].ap()[0:1, 0:1]).then_inc(sem["lg"], 32 - 16)
                        g.wait_ge(sem["lg"], cnt["lg"] + 32)
                    cnt["lg"] += 32
                    cnt["cc"] += 1

                    @blk.vector
                    def _(v):
                        v.wait_ge(sem["lg"], cnt["lg"])
                        v.tensor_scalar(SC1[0:1, 0:1], SC1[0:1, 0:1], 1e-8, None, ALU.max)
                        v.drain()
                        v.reciprocal(SC1[0:1, 1:2], SC1[0:1, 0:1])
                        v.drain()
                        v.tensor_scalar(SC1[0:1, 1:2], SC1[0:1, 1:2], 32767.0, None, ALU.mult)
                        v.drain()
                        v.tensor_scalar(SC1[0:1, 2:3], SC1[0:1, 0:1], 1.0 / 32767.0, None, ALU.mult)
                        v.drain()
                        v.nop().then_inc(sem["v"], 1)
                    cnt["v"] += 1
                    vh3 = cnt["v"]

                    @blk.tensor
                    def _(t):
                        t.wait_ge(sem["v"], vh3)
                        t.matmul(PO[:, 0:4], ONE1F[:, :], SC1[0:1, 0:4],
                                 start=True, stop=True).then_inc(sem["t"], 1)
                    cnt["t"] += 1
                    th2 = cnt["t"]

                    @blk.vector
                    def _(v):
                        v.wait_ge(sem["t"], th2)
                        v.tensor_copy(SCL[:, :], PO[:, 0:4])
                        v.drain()
                        for fc in range(NFF):
                            v.tensor_scalar(TMP[:, :], HID[:, fc * T:(fc + 1) * T],
                                            SCL[:, 1:2], MAGIC, ALU.mult, ALU.add)
                            v.drain()
                            v.tensor_scalar(HID[:, fc * T:(fc + 1) * T], TMP[:, :],
                                            -MAGIC, SCL[:, 2:3], ALU.add, ALU.mult)
                            v.drain()
                        v.nop().then_inc(sem["v"], 1)
                    cnt["v"] += 1

                with nc.Block() as blk:
                    wo0 = {0: cnt["r0"], 1: cnt["r1"]}
                    wq0 = {0: cnt["p0"], 1: cnt["p1"]}
                    v81 = cnt["v"]

                    @blk.scalar
                    def _(s):
                        for p_ in ["r0", "r1"]:
                            s.wait_ge(sem[p_], cnt[p_])
                        for fc in range(NFF):
                            fp = fc % 2
                            s.wait_ge(sem["r%d" % fp], wo0[fp] + 16 * (fc // 2))
                            if fc >= 2:
                                s.wait_ge(sem["v"], v81 + (fc - 2) + 1)
                            s.dma_start(out=WST[:, fp * D:(fp + 1) * D],
                                        in_=DI["l0wout"].ap()[fc * 128:(fc + 1) * 128, :]
                                        ).then_inc(sem["r%d" % fp], 16)
                        for p_ in [0, 1]:
                            s.wait_ge(sem["r%d" % p_], wo0[p_] + 16 * 16)
                    cnt["r0"] += 16 * 16
                    cnt["r1"] += 16 * 16

                    @blk.vector
                    def _(v):
                        for fc in range(NFF):
                            fp = fc % 2
                            v.wait_ge(sem["r%d" % fp], wo0[fp] + 16 * (fc // 2 + 1))
                            if fc >= 2:
                                v.wait_ge(sem["p%d" % fp], wq0[fp] + 16 * (fc // 2))
                            v.tensor_reduce(QS[:, 0:1], WST[:, fp * D:(fp + 1) * D],
                                            AX.X, ALU.max, apply_absolute_value=True)
                            v.drain()
                            v.tensor_scalar(QS[:, 0:1], QS[:, 0:1], 1e-8, None, ALU.max)
                            v.drain()
                            v.reciprocal(QS[:, 1:2], QS[:, 0:1])
                            v.drain()
                            v.tensor_scalar(QS[:, 1:2], QS[:, 1:2], 127.0, None, ALU.mult)
                            v.drain()
                            v.tensor_scalar(QS[:, 2:3], QS[:, 0:1], 1.0 / 127.0, None, ALU.mult)
                            v.drain()
                            v.tensor_scalar(WQT[:, :], WST[:, fp * D:(fp + 1) * D],
                                            QS[:, 1:2], MAGIC, ALU.mult, ALU.add)
                            v.drain()
                            v.tensor_scalar(WQ[:, fp * D:(fp + 1) * D], WQT[:, :],
                                            -MAGIC, QS[:, 2:3], ALU.add,
                                            ALU.mult).then_inc(sem["v"], 1)
                    cnt["v"] += NFF

                    @blk.sync
                    def _(sy):
                        for p_ in ["p0", "p1"]:
                            sy.wait_ge(sem[p_], cnt[p_])
                        for fc in range(NFF):
                            fp = fc % 2
                            sy.wait_ge(sem["v"], v81 + fc + 1)
                            sy.dma_start(out=woq_d.ap()[fc * 128:(fc + 1) * 128, :],
                                         in_=WQ[:, fp * D:(fp + 1) * D]
                                         ).then_inc(sem["p%d" % fp], 16)
                        for p_ in [0, 1]:
                            sy.wait_ge(sem["p%d" % p_], wq0[p_] + 16 * 16)
                    cnt["p0"] += 16 * 16
                    cnt["p1"] += 16 * 16

                with nc.Block() as blk:
                    wo1 = {0: cnt["r0"], 1: cnt["r1"]}
                    x10 = {0: cnt["k0"], 1: cnt["k1"]}
                    o80 = {0: cnt["p2"], 1: cnt["p3"]}
                    t82 = cnt["t"]; v82 = cnt["v"]

                    @blk.scalar
                    def _(s):
                        for p_ in ["r0", "r1"]:
                            s.wait_ge(sem[p_], cnt[p_])
                        for i in range(NG * 2 * NFF):
                            fc = i % NFF
                            fp = i % 2
                            s.wait_ge(sem["r%d" % fp], wo1[fp] + 16 * (i // 2))
                            if i >= 2:
                                s.wait_ge(sem["t"], t82 + (i - 2) + 1)
                            s.dma_start(out=WQ[:, fp * D:(fp + 1) * D],
                                        in_=woq_d.ap()[fc * 128:(fc + 1) * 128, :]
                                        ).then_inc(sem["r%d" % fp], 16)
                        for p_ in [0, 1]:
                            s.wait_ge(sem["r%d" % p_], wo1[p_] + 16 * NG * NFF)
                    cnt["r0"] += 16 * NG * NFF
                    cnt["r1"] += 16 * NG * NFF

                    @blk.sync
                    def _(sy):
                        for p_ in ["k0", "k1", "p2", "p3"]:
                            sy.wait_ge(sem[p_], cnt[p_])
                        for i in range(NG * ND):
                            g4, o = divmod(i, ND)
                            fp = i % 2
                            if i >= 2:
                                j = i - 2
                                jg4, jo = divmod(j, ND)
                                jfp = j % 2
                                sy.wait_ge(sem["v"], v82 + j + 1)
                                sy.wait_ge(sem["p%d" % (2 + jfp)], o80[jfp] + 16 * (j // 2))
                                sy.dma_start(out=x2_d.ap()[jo * 128:(jo + 1) * 128,
                                                           jg4 * 512:(jg4 + 1) * 512],
                                             in_=OT2[:, jfp * 512:(jfp + 1) * 512]
                                             ).then_inc(sem["p%d" % (2 + jfp)], 16)
                            sy.wait_ge(sem["k%d" % fp], x10[fp] + 16 * (i // 2))
                            sy.dma_start(out=X1R[:, fp * 512:(fp + 1) * 512],
                                         in_=x1_d.ap()[o * 128:(o + 1) * 128,
                                                       g4 * 512:(g4 + 1) * 512]
                                         ).then_inc(sem["k%d" % fp], 16)
                        for j in range(NG * ND - 2, NG * ND):
                            jg4, jo = divmod(j, ND)
                            jfp = j % 2
                            sy.wait_ge(sem["v"], v82 + j + 1)
                            sy.wait_ge(sem["p%d" % (2 + jfp)], o80[jfp] + 16 * (j // 2))
                            sy.dma_start(out=x2_d.ap()[jo * 128:(jo + 1) * 128,
                                                       jg4 * 512:(jg4 + 1) * 512],
                                         in_=OT2[:, jfp * 512:(jfp + 1) * 512]
                                         ).then_inc(sem["p%d" % (2 + jfp)], 16)
                        for p_ in [0, 1]:
                            sy.wait_ge(sem["k%d" % p_], x10[p_] + 16 * NG * ND // 2)
                            sy.wait_ge(sem["p%d" % (2 + p_)], o80[p_] + 16 * NG * ND // 2)
                    cnt["k0"] += 16 * NG * ND // 2
                    cnt["k1"] += 16 * NG * ND // 2
                    cnt["p2"] += 16 * NG * ND // 2
                    cnt["p3"] += 16 * NG * ND // 2

                    @blk.tensor
                    def _(t):
                        for gg in range(NG * 2):
                            g4, oh = divmod(gg, 2)
                            if gg >= 1:
                                t.wait_ge(sem["v"], v82 + 4 * gg)
                            for fc in range(NFF):
                                i = gg * NFF + fc
                                fp = i % 2
                                t.wait_ge(sem["r%d" % fp], wo1[fp] + 16 * (i // 2 + 1))
                                for oo in range(4):
                                    o = oh * 4 + oo
                                    mm = t.matmul(PH[:, oo * 512:(oo + 1) * 512],
                                                  WQ[:, fp * D + o * 128:fp * D + (o + 1) * 128],
                                                  HID[:, fc * T + g4 * 512:fc * T + (g4 + 1) * 512],
                                                  start=(fc == 0), stop=(fc == NFF - 1))
                                mm.then_inc(sem["t"], 1)
                    cnt["t"] += NG * 2 * NFF

                    @blk.vector
                    def _(v):
                        for gg in range(NG * 2):
                            g4, oh = divmod(gg, 2)
                            v.wait_ge(sem["t"], t82 + (gg + 1) * NFF)
                            for oo in range(4):
                                o = oh * 4 + oo
                                i = gg * 4 + oo
                                fp = i % 2
                                v.wait_ge(sem["k%d" % fp], x10[fp] + 16 * (i // 2 + 1))
                                if i >= 2:
                                    v.wait_ge(sem["p%d" % (2 + fp)], o80[fp] + 16 * (i // 2))
                                v.scalar_tensor_tensor(OT2[:, fp * 512:(fp + 1) * 512],
                                                        PH[:, oo * 512:(oo + 1) * 512],
                                                        BOUT[:, o:o + 1],
                                                        X1R[:, fp * 512:(fp + 1) * 512],
                                                        ALU.add, ALU.add).then_inc(sem["v"], 1)
                    cnt["v"] += NG * 2 * 4



 
        if nstage >= 8:
            quant_ffn("f0")

        if nstage >= 9:
            layernorm(x2_d.ap(), "ln2", None)

        k1h_d = dint("k1h_d", [D, T], BF16)
        v1h_d = dint("v1h_d", [NT * 128, VW], BF16)
        k1f_d = dint("k1f_d", [2, D * T], BF16)
        v1f_d = dint("v1f_d", [2, NT * 128 * VW], BF16)
        ctf2_d = dint("ctf2_d", [D, T], BF16)

        if nstage >= 10:
            proj_fm(DI["l1wq"].ap(), DI["l1bq"], q1_d.ap(), "p9q")
            proj_fm(DI["l1wk"].ap(), DI["l1bk"], k1h_d.ap(), "p9k")
            proj_tm(DI["l1wv"].ap(), DI["l1bv"], v1h_d.ap(), "p9v")
            with nc.Block() as blk:
                @blk.gpsimd
                def _(g):
                    g.wait_ge(sem["lg"], cnt["lg"])
                    g.collective_compute(
                        "AllGather", ALU.bypass, replica_groups=RG2,
                        ins=[bass.AP(k1h_d, 0, [[D * T, 1], [1, D * T]]).opt()],
                        outs=[k1f_d.ap().opt()],
                    ).then_inc(sem["cc"], 1)
                    g.collective_compute(
                        "AllGather", ALU.bypass, replica_groups=RG2,
                        ins=[bass.AP(v1h_d, 0, [[NT * 128 * VW, 1], [1, NT * 128 * VW]]).opt()],
                        outs=[v1f_d.ap().opt()],
                    ).then_inc(sem["cc"], 1)
                cnt["cc"] += 2
            cc_kv = cnt["cc"]

        GT = E_(nc.sbuf_tensor("GT", [8, T], BF16)) if nstage >= 10 else None
        NM2 = E_(nc.sbuf_tensor("NM2", [128, 16], F32)) if nstage >= 10 else None

        def moe_route(tag):
            with ExitStack() as c2:
                E2 = c2.enter_context
                WRT = E2(nc.sbuf_tensor(tag + "WRT", [128, 64], BF16))
                IOE = E2(nc.sbuf_tensor(tag + "IOE", [128, NE], F32))
                RVT = E2(nc.sbuf_tensor(tag + "RVT", [128, NE], F32))
                LGT = E2(nc.sbuf_tensor(tag + "LGT", [128, 128], F32))
                OH1 = E2(nc.sbuf_tensor(tag + "OH1", [128, 128], F32))
                OH2 = E2(nc.sbuf_tensor(tag + "OH2", [128, 128], F32))
                MV = E2(nc.sbuf_tensor(tag + "MV", [128, 16], F32))
                E16 = E2(nc.sbuf_tensor(tag + "E16", [128, 48], F32))
                SC2 = E2(nc.sbuf_tensor(tag + "SC2", [128, 8], F32))
                SCR = E2(nc.sbuf_tensor(tag + "SCR", [128, 8], F32))
                OH1T = E2(nc.sbuf_tensor(tag + "OH1T", [8, T], F32))
                OH2T = E2(nc.sbuf_tensor(tag + "OH2T", [8, T], F32))
                SINC = E2(nc.sbuf_tensor(tag + "SINC", [8, T], F32))
                KF8 = E2(nc.sbuf_tensor(tag + "KF8", [8, T], F32))
                KFTS = E2(nc.sbuf_tensor(tag + "KFTS", [128, 8], F32))
                GTT = E2(nc.sbuf_tensor(tag + "GTT", [128, 8], F32))
                TRS = E2(nc.sbuf_tensor(tag + "TRS", [8, 2], F32))
                CP = E2(nc.sbuf_tensor(tag + "CP", [1, 24], F32))
                QN16 = E2(nc.sbuf_tensor(tag + "QN16", [1, 16], F32))
                M16 = E2(nc.sbuf_tensor(tag + "M16", [1, 16], F32))
                CB0 = E2(nc.sbuf_tensor(tag + "CB0", [1, 24], F32))
                CB1 = E2(nc.sbuf_tensor(tag + "CB1", [1, 24], F32))
                BASE = E2(nc.sbuf_tensor(tag + "BASE", [8, 2], F32))
                ODD = E2(nc.sbuf_tensor(tag + "ODD", [1, 1], F32))
                KCH = E2(nc.sbuf_tensor(tag + "KCH", [128, 2 * T], BF16))
                KSQ = E2(nc.sbuf_tensor(tag + "KSQ", [128, T], BF16))
                N2 = E2(nc.sbuf_tensor(tag + "N2", [2, 64], F32))
                PL = E2(nc.psum_tensor(tag + "PL", [128, 512], F32))
                PT8 = E2(nc.psum_tensor(tag + "PT8", [128, 1024], F32))
                PN = E2(nc.psum_tensor(tag + "PN", [2, T], F32))

                with nc.Block() as blk:
                    @blk.gpsimd
                    def _(g):
                        g.wait_ge(sem["lg"], cnt["lg"])
                        for c in range(ND):
                            g.dma_start(out=WRT[:, c * 8:(c + 1) * 8],
                                        in_=DI["l1wrT"].ap()[c * 128:(c + 1) * 128, :]
                                        ).then_inc(sem["lg"], 16)
                    cnt["lg"] += 16 * ND
                    lgr = cnt["lg"]

                    @blk.sync
                    def _(sy):
                        sy.wait_ge(sem["ld"], cnt["ld"])
                        sy.dma_start(out=IOE[:, :], in_=DI["iota_e"].ap()).then_inc(sem["ld"], 16)
                        sy.dma_start(out=RVT[:, :], in_=DI["revta"].ap()).then_inc(sem["ld"], 16)
                        sy.dma_start(out=ODD[:, :], in_=DI["odd1"].ap()).then_inc(sem["ld"], 16)
                    cnt["ld"] += 48
                    ldio = cnt["ld"]

                    tl0 = cnt["t"]

                    @blk.tensor
                    def _(t):
                        t.wait_ge(sem["lg"], lgr)
                        for tt in range(NT):
                            for c in range(ND):
                                mm = t.matmul(PL[:, (tt % 16) * 8:(tt % 16) * 8 + 8],
                                              XN[:, c * T + tt * 128:c * T + (tt + 1) * 128],
                                              WRT[:, c * 8:(c + 1) * 8],
                                              start=(c == 0), stop=(c == ND - 1))
                            mm.then_inc(sem["t"], 1)
                    cnt["t"] += NT

                    @blk.vector
                    def _(v):
                        v.wait_ge(sem["ld"], ldio)
                        v.wait_ge(sem["t"], tl0 + NT)
                        v.tensor_copy(LGT[:, :], PL[:, 0:128])
                        v.drain()
                        v.nop().then_inc(sem["v"], 1)
                    cnt["v"] += 1

                with nc.Block() as blk:
                    @blk.vector
                    def _(v):
                        for tt in range(NT):
                            L = LGT[:, tt * 8:(tt + 1) * 8]
                            v.tensor_reduce(MV[:, 0:1], L, AX.X, ALU.max)
                            v.drain()
                            v.tensor_scalar(SC2[:, :], L, MV[:, 0:1], None, ALU.is_ge)
                            v.drain()
                            v.tensor_tensor(SC2[:, :], SC2[:, :], RVT[:, :], ALU.mult)
                            v.drain()
                            v.tensor_reduce(MV[:, 1:2], SC2[:, :], AX.X, ALU.max)
                            v.drain()
                            v.tensor_scalar(OH1[:, tt * 8:(tt + 1) * 8], SC2[:, :],
                                            MV[:, 1:2], None, ALU.is_ge)
                            v.drain()
                            v.scalar_tensor_tensor(SCR[:, :], OH1[:, tt * 8:(tt + 1) * 8],
                                                   -BIG, L, ALU.mult, ALU.add)
                            v.drain()
                            v.tensor_reduce(MV[:, 2:3], SCR[:, :], AX.X, ALU.max)
                            v.drain()
                            v.tensor_scalar(SC2[:, :], SCR[:, :], MV[:, 2:3], None, ALU.is_ge)
                            v.drain()
                            v.tensor_tensor(SC2[:, :], SC2[:, :], RVT[:, :], ALU.mult)
                            v.drain()
                            v.tensor_reduce(MV[:, 3:4], SC2[:, :], AX.X, ALU.max)
                            v.drain()
                            v.tensor_scalar(OH2[:, tt * 8:(tt + 1) * 8], SC2[:, :],
                                            MV[:, 3:4], None, ALU.is_ge)
                            v.drain()
                            v.tensor_tensor(E16[:, tt:tt + 1], MV[:, 2:3], MV[:, 0:1],
                                            ALU.subtract)
                            v.drain()
                        v.nop().then_inc(sem["v"], 1)
                    cnt["v"] += 1
                    vt1 = cnt["v"]

                    @blk.scalar
                    def _(s):
                        s.wait_ge(sem["v"], vt1)
                        s.activation(E16[:, 16:32], E16[:, 0:16], AF.Exp)
                        s.drain()
                        s.nop().then_inc(sem["s"], 1)
                    cnt["s"] += 1
                    st1 = cnt["s"]

                    @blk.vector
                    def _(v):
                        v.wait_ge(sem["s"], st1)
                        v.tensor_scalar(E16[:, 32:48], E16[:, 16:32], 1.0, None, ALU.add)
                        v.drain()
                        v.reciprocal(E16[:, 0:16], E16[:, 32:48])
                        v.drain()
                        v.tensor_tensor(E16[:, 16:32], E16[:, 16:32], E16[:, 0:16], ALU.mult)
                        v.drain()
                        v.nop().then_inc(sem["v"], 1)
                    cnt["v"] += 1
                    vt2 = cnt["v"]

                with nc.Block() as blk:
                    to0 = cnt["t"]

                    @blk.tensor
                    def _(t):
                        t.wait_ge(sem["v"], vt2)
                        for tt in range(NT):
                            if tt >= 2:
                                t.wait_ge(sem["v"], vt2 + 2 * (tt - 1))
                            t.transpose(PT8[0:8, (tt % 2) * 512:(tt % 2) * 512 + 128],
                                        OH1[:, tt * 8:(tt + 1) * 8],
                                        IDF[:, :]).then_inc(sem["t"], 1)
                            t.transpose(PT8[0:8, (tt % 2) * 512 + 128:(tt % 2) * 512 + 256],
                                        OH2[:, tt * 8:(tt + 1) * 8],
                                        IDF[:, :]).then_inc(sem["t"], 1)
                    cnt["t"] += 2 * NT

                    @blk.vector
                    def _(v):
                        for tt in range(NT):
                            v.wait_ge(sem["t"], to0 + 2 * tt + 2)
                            v.tensor_copy(OH1T[0:8, tt * 128:(tt + 1) * 128],
                                          PT8[0:8, (tt % 2) * 512:(tt % 2) * 512 + 128]
                                          ).then_inc(sem["v"], 1)
                            v.tensor_copy(OH2T[0:8, tt * 128:(tt + 1) * 128],
                                          PT8[0:8, (tt % 2) * 512 + 128:(tt % 2) * 512 + 256]
                                          ).then_inc(sem["v"], 1)
                        v.drain()
                        v.tensor_tensor_scan(SINC[:, :], OH1T[:, :], OH2T[:, :], 0.0,
                                             ALU.add, ALU.add)
                        v.drain()
                        v.nop().then_inc(sem["v"], 1)
                    cnt["v"] += 2 * NT + 1
                    vs1 = cnt["v"]

                    @blk.tensor
                    def _(t):
                        t.wait_ge(sem["v"], vs1)
                        t.transpose(PT8[0:1, 0:8], SINC[0:8, T - 1:T],
                                    IDF[0:8, 0:8]).then_inc(sem["t"], 1)
                    cnt["t"] += 1
                    tc1 = cnt["t"]

                    @blk.vector
                    def _(v):
                        v.wait_ge(sem["t"], tc1)
                        v.tensor_copy(CP[0:1, 0:8], PT8[0:1, 0:8])
                        v.drain()
                        v.nop().then_inc(sem["v"], 1)
                    cnt["v"] += 1

                def norms2(src_dram, col0):
                    with nc.Block() as blk:
                        ks0 = cnt["ld"]
                        v0_ = cnt["v"]; t0_ = cnt["t"]

                        @blk.sync
                        def _(sy):
                            sy.wait_ge(sem["ld"], cnt["ld"])
                            for c in range(ND):
                                if c >= 2:
                                    sy.wait_ge(sem["v"], v0_ + 2 * (c - 2) + 1)
                                sy.dma_start(out=KCH[:, (c % 2) * T:(c % 2 + 1) * T],
                                             in_=src_dram[c * 128:(c + 1) * 128, :]
                                             ).then_inc(sem["ld"], 16)
                                sy.wait_ge(sem["ld"], ks0 + 16 * (c + 1))
                        cnt["ld"] += 16 * ND

                        @blk.vector
                        def _(v):
                            for c in range(ND):
                                p = c % 2
                                v.wait_ge(sem["ld"], ks0 + 16 * (c + 1))
                                if c >= 1:
                                    v.wait_ge(sem["t"], t0_ + c)
                                v.tensor_tensor(KSQ[:, :], KCH[:, p * T:(p + 1) * T],
                                                KCH[:, p * T:(p + 1) * T], ALU.mult)
                                v.drain()
                                v.nop().then_inc(sem["v"], 1)
                                v.wait_ge(sem["t"], t0_ + c + 1)
                                v.tensor_reduce(N2[0:2, col0 + c:col0 + c + 1], PN[0:2, :],
                                                AX.X, ALU.max)
                                v.drain()
                                v.nop().then_inc(sem["v"], 1)
                        cnt["v"] += 2 * ND

                        @blk.tensor
                        def _(t):
                            for c in range(ND):
                                t.wait_ge(sem["v"], v0_ + 2 * c + 1)
                                for g4 in range(NG):
                                    mm = t.matmul(PN[0:2, g4 * 512:(g4 + 1) * 512],
                                                  ONEPAIR[:, :],
                                                  KSQ[:, g4 * 512:(g4 + 1) * 512],
                                                  start=True, stop=True)
                                mm.then_inc(sem["t"], 1)
                        cnt["t"] += ND

                norms2(k1h_d.ap(), 0)
                norms2(q1_d.ap(), 8)

                with nc.Block() as blk:
                    @blk.tensor
                    def _(t):
                        t.transpose(PT8[0:8, 0:2], N2[0:2, 0:8], IDF[0:2, 0:2]).then_inc(sem["t"], 1)
                        t.transpose(PT8[0:8, 512:514], N2[0:2, 8:16],
                                    IDF[0:2, 0:2]).then_inc(sem["t"], 1)
                    cnt["t"] += 2
                    tf1 = cnt["t"]

                    @blk.vector
                    def _(v):
                        v.wait_ge(sem["t"], tf1)
                        v.tensor_copy(TRS[0:8, 0:2], PT8[0:8, 0:2])
                        v.drain()
                        v.nop().then_inc(sem["v"], 1)
                    cnt["v"] += 1
                    vf1 = cnt["v"]

                    @blk.sync
                    def _(sy):
                        sy.wait_ge(sem["v"], vf1)
                        sy.wait_ge(sem["ld"], cnt["ld"])
                        sy.dma_start(out=CP[0:1, 8:24], in_=TRS[0:8, 0:2]).then_inc(sem["ld"], 16)
                    cnt["ld"] += 16
                    ldf1 = cnt["ld"]

                    @blk.vector
                    def _(v):
                        v.wait_ge(sem["ld"], ldf1)
                        v.tensor_copy(TRS[0:8, 0:2], PT8[0:8, 512:514])
                        v.drain()
                        v.nop().then_inc(sem["v"], 1)
                    cnt["v"] += 1
                    vf2 = cnt["v"]

                    @blk.sync
                    def _(sy):
                        sy.wait_ge(sem["v"], vf2)
                        sy.dma_start(out=QN16[0:1, 0:16], in_=TRS[0:8, 0:2]).then_inc(sem["ld"], 16)
                        sy.wait_ge(sem["ld"], ldf1 + 16)
                    cnt["ld"] += 16

                    @blk.gpsimd
                    def _(g):
                        g.wait_ge(sem["ld"], cnt["ld"])
                        g.wait_ge(sem["lg"], cnt["lg"])
                        g.dma_start(out=cnt_in.ap()[0:1, :], in_=CP[0:1, :]).then_inc(sem["lg"], 16)
                        g.wait_ge(sem["lg"], cnt["lg"] + 16)
                        g.collective_compute(
                            "AllGather", ALU.bypass, replica_groups=RG2,
                            ins=[cnt_in.ap().opt()], outs=[cnt_out.ap().opt()],
                        ).then_inc(sem["cc"], 1)
                        g.wait_ge(sem["cc"], cnt["cc"] + 1)
                        g.dma_start(out=CB0[0:1, :], in_=cnt_out.ap()[0:1, :]).then_inc(sem["lg"], 16)
                        g.dma_start(out=CB1[0:1, :], in_=cnt_out.ap()[1:2, :]).then_inc(sem["lg"], 16)
                        g.wait_ge(sem["lg"], cnt["lg"] + 48)
                    cnt["lg"] += 48
                    cnt["cc"] += 1
                    lgc1 = cnt["lg"]

                    @blk.vector
                    def _(v):
                        v.wait_ge(sem["lg"], lgc1)
                        v.tensor_tensor(M16[0:1, :], CB0[0:1, 8:24], CB1[0:1, 8:24], ALU.max)
                        v.drain()
                        v.tensor_tensor(M16[0:1, :], M16[0:1, :], QN16[0:1, :], ALU.mult)
                        v.drain()
                        v.nop().then_inc(sem["v"], 1)
                    cnt["v"] += 1
                    vm1 = cnt["v"]

                    @blk.scalar
                    def _(s):
                        s.wait_ge(sem["v"], vm1)
                        s.activation(M16[0:1, :], M16[0:1, :], AF.Sqrt)
                        s.drain()
                        s.mul(M16[0:1, :], M16[0:1, :], -0.125)
                        s.drain()
                        s.nop().then_inc(sem["s"], 1)
                    cnt["s"] += 1
                    sm1 = cnt["s"]

                    @blk.tensor
                    def _(t):
                        t.wait_ge(sem["s"], sm1)
                        t.matmul(PT8[:, 0:16], ONE1F[:, :], M16[0:1, :],
                                 start=True, stop=True).then_inc(sem["t"], 1)
                        t.wait_ge(sem["v"], cnt["v"] + 1)
                        t.transpose(PT8[0:8, 512:513], CP[0:1, 0:8],
                                    IDF[0:1, 0:1]).then_inc(sem["t"], 1)
                    cnt["t"] += 2
                    tm1 = cnt["t"]

                    @blk.vector
                    def _(v):
                        v.wait_ge(sem["lg"], lgc1)
                        v.tensor_scalar(CP[0:1, 0:8], CB0[0:1, 0:8], ODD[0:1, 0:1],
                                        None, ALU.mult)
                        v.drain()
                        v.nop().then_inc(sem["v"], 1)
                        v.wait_ge(sem["t"], tm1)
                        v.tensor_copy(NM2[:, :], PT8[:, 0:16])
                        v.drain()
                        v.tensor_scalar(BASE[0:8, 0:1], PT8[0:8, 512:513], -1538.0,
                                        None, ALU.add)
                        v.drain()
                        v.nop().then_inc(sem["v"], 1)
                    cnt["v"] += 2
                    vb2 = cnt["v"]

                with nc.Block() as blk:
                    @blk.vector
                    def _(v):
                        v.tensor_scalar(KF8[:, :], SINC[:, :], BASE[0:8, 0:1], None, ALU.add)
                        v.drain()
                        v.tensor_scalar(KF8[:, :], KF8[:, :], 0.0, None, ALU.is_lt)
                        v.drain()
                        v.nop().then_inc(sem["v"], 1)
                    cnt["v"] += 1
                    vk1 = cnt["v"]
                    tg0 = cnt["t"]

                    @blk.tensor
                    def _(t):
                        t.wait_ge(sem["v"], vk1)
                        for tt in range(NT):
                            if tt >= 1:
                                t.wait_ge(sem["v"], vk1 + 2 * tt)
                            t.transpose(PT8[:, (tt % 2) * 512:(tt % 2) * 512 + 8],
                                        KF8[0:8, tt * 128:(tt + 1) * 128],
                                        IDF[0:8, 0:8]).then_inc(sem["t"], 1)
                            t.wait_ge(sem["v"], vk1 + 2 * tt + 1)
                            t.transpose(PT8[0:8, (tt % 2) * 512 + 128:(tt % 2) * 512 + 256],
                                        GTT[:, :], IDF[:, :]).then_inc(sem["t"], 1)
                    cnt["t"] += 2 * NT

                    @blk.vector
                    def _(v):
                        for tt in range(NT):
                            v.wait_ge(sem["t"], tg0 + 2 * tt + 1)
                            v.tensor_copy(KFTS[:, :], PT8[:, (tt % 2) * 512:(tt % 2) * 512 + 8])
                            v.drain()
                            v.tensor_tensor(SC2[:, :], OH1[:, tt * 8:(tt + 1) * 8],
                                            KFTS[:, :], ALU.mult)
                            v.drain()
                            v.tensor_scalar(SC2[:, :], SC2[:, :], E16[:, tt:tt + 1],
                                            None, ALU.mult)
                            v.drain()
                            v.tensor_tensor(SCR[:, :], OH2[:, tt * 8:(tt + 1) * 8],
                                            KFTS[:, :], ALU.mult)
                            v.drain()
                            v.tensor_scalar(SCR[:, :], SCR[:, :], E16[:, 16 + tt:17 + tt],
                                            None, ALU.mult)
                            v.drain()
                            v.tensor_tensor(GTT[:, :], SC2[:, :], SCR[:, :], ALU.add)
                            v.drain()
                            v.nop().then_inc(sem["v"], 1)
                            v.wait_ge(sem["t"], tg0 + 2 * tt + 2)
                            with nc.allow_low_precision(reason="bf16 gate broadcast"):
                                v.tensor_copy(GT[0:8, tt * 128:(tt + 1) * 128],
                                              PT8[0:8, (tt % 2) * 512 + 128:(tt % 2) * 512 + 256])
                            v.drain()
                            v.nop().then_inc(sem["v"], 1)
                    cnt["v"] += 2 * NT

        if nstage >= 10:
            moe_route("m0")

        we_bf = dint("we_bf", [NE * D, D], BF16)

        def moe_experts(tag):
            with ExitStack() as c2:
                E2 = c2.enter_context
                XEG = E2(nc.sbuf_tensor(tag + "XEG", [128, NE * ND * 512], BF16))
                WED = E2(nc.sbuf_tensor(tag + "WED", [128, 2 * 1024], BF16))
                GROW = E2(nc.sbuf_tensor(tag + "GROW", [1, 2 * T], BF16))
                GBS = E2(nc.sbuf_tensor(tag + "GBS", [128, 2 * 512], BF16))
                OTM = E2(nc.sbuf_tensor(tag + "OTM", [128, ND * 512], F32))
                PM = E2(nc.psum_tensor(tag + "PM", [128, ND * 512], F32))

                with nc.Block() as blk:
                    @blk.gpsimd
                    def _(g):
                        g.wait_ge(sem["lg"], cnt["lg"])
                        for e in range(NE):
                            g.dma_start(out=we_bf.ap()[e * D:(e + 1) * D, :],
                                        in_=DI["l1we"].ap()[e, :, :]).then_inc(sem["lg"], 16)
                        g.wait_ge(sem["lg"], cnt["lg"] + 16 * NE)
                    cnt["lg"] += 16 * NE

                for g4 in range(NG):
                    with nc.Block() as blk:
                        pg0 = {0: cnt["r0"], 1: cnt["r1"]}
                        tm0 = cnt["t"]
                        vm0 = cnt["v"]

                        @blk.sync
                        def _(sy):
                            for p_ in ["r0", "r1"]:
                                sy.wait_ge(sem[p_], cnt[p_])
                            for e in range(NE):
                                fp = e % 2
                                sy.wait_ge(sem["r%d" % fp], pg0[fp] + 16 * (e // 2))
                                if e >= 2:
                                    sy.wait_ge(sem["t"], tm0 + (e - 2) + 1)
                                sy.dma_start(out=GROW[0:1, fp * T + g4 * 512:fp * T + (g4 + 1) * 512],
                                             in_=bass.AP(GT, e * T + g4 * 512,
                                                         [[T, 1], [1, 512]])
                                             ).then_inc(sem["r%d" % fp], 16)
                            for p_ in [0, 1]:
                                sy.wait_ge(sem["r%d" % p_], pg0[p_] + 16 * NE // 2)
                        cnt["r0"] += 16 * NE // 2
                        cnt["r1"] += 16 * NE // 2

                        @blk.tensor
                        def _(t):
                            for e in range(NE):
                                fp = e % 2
                                t.wait_ge(sem["r%d" % fp], pg0[fp] + 16 * (e // 2 + 1))
                                if e >= 2:
                                    t.wait_ge(sem["v"], vm0 + (e - 2) + 1)
                                t.matmul(PM[:, (e % 2) * 512:(e % 2 + 1) * 512], ONE1B[:, :],
                                         GROW[0:1, fp * T + g4 * 512:fp * T + (g4 + 1) * 512],
                                         start=True, stop=True).then_inc(sem["t"], 1)
                        cnt["t"] += NE

                        @blk.vector
                        def _(v):
                            for e in range(NE):
                                fp = e % 2
                                v.wait_ge(sem["t"], tm0 + e + 1)
                                v.tensor_copy(GBS[:, fp * 512:(fp + 1) * 512],
                                              PM[:, (e % 2) * 512:(e % 2 + 1) * 512])
                                v.drain()
                                for di in range(ND):
                                    mmv = v.tensor_tensor(
                                        XEG[:, (e * ND + di) * 512:(e * ND + di + 1) * 512],
                                        XN[:, di * T + g4 * 512:di * T + (g4 + 1) * 512],
                                        GBS[:, fp * 512:(fp + 1) * 512], ALU.mult)
                                mmv.then_inc(sem["v"], 1)
                        cnt["v"] += NE

                    with nc.Block() as blk:
                        we0 = {0: cnt["r0"], 1: cnt["r1"]}
                        tx0 = cnt["t"]
                        vx0 = cnt["v"]

                        @blk.sync
                        def _(sy):
                            for p_ in ["r0", "r1"]:
                                sy.wait_ge(sem[p_], cnt[p_])
                            for i in range(NE * ND):
                                e, di = divmod(i, ND)
                                fp = i % 2
                                sy.wait_ge(sem["r%d" % fp], we0[fp] + 16 * (i // 2))
                                if i >= 2:
                                    sy.wait_ge(sem["t"], tx0 + (i - 2) + 1)
                                sy.dma_start(out=WED[:, fp * 1024:(fp + 1) * 1024],
                                             in_=we_bf.ap()[e * D + di * 128:e * D + (di + 1) * 128, :]
                                             ).then_inc(sem["r%d" % fp], 16)
                            for p_ in [0, 1]:
                                sy.wait_ge(sem["r%d" % p_], we0[p_] + 16 * NE * ND // 2)
                        cnt["r0"] += 16 * NE * ND // 2
                        cnt["r1"] += 16 * NE * ND // 2

                        @blk.tensor
                        def _(t):
                            for i in range(NE * ND):
                                e, di = divmod(i, ND)
                                fp = i % 2
                                t.wait_ge(sem["r%d" % fp], we0[fp] + 16 * (i // 2 + 1))
                                for o in range(ND):
                                    mm = t.matmul(PM[:, o * 512:(o + 1) * 512],
                                                  WED[:, fp * 1024 + o * 128:fp * 1024 + (o + 1) * 128],
                                                  XEG[:, i * 512:(i + 1) * 512],
                                                  start=(i == 0), stop=(i == NE * ND - 1))
                                mm.then_inc(sem["t"], 1)
                        cnt["t"] += NE * ND
                        tx1 = cnt["t"]

                        @blk.vector
                        def _(v):
                            v.wait_ge(sem["t"], tx1)
                            for o in range(ND):
                                mmv = v.tensor_copy(OTM[:, o * 512:(o + 1) * 512],
                                                    PM[:, o * 512:(o + 1) * 512])
                            mmv.then_inc(sem["v"], 1)
                        cnt["v"] += 1
                        vx1 = cnt["v"]

                        @blk.sync
                        def _(sy):
                            sy.wait_ge(sem["v"], vx1)
                            sy.wait_ge(sem["p0"], cnt["p0"])
                            for o in range(ND):
                                sy.dma_start(out=moe_d.ap()[o * 128:(o + 1) * 128,
                                                            g4 * 512:(g4 + 1) * 512],
                                             in_=OTM[:, o * 512:(o + 1) * 512]
                                             ).then_inc(sem["p0"], 16)
                            sy.wait_ge(sem["p0"], cnt["p0"] + 16 * ND)
                        cnt["p0"] += 16 * ND

        if nstage >= 11:
            moe_experts("e0")

        def l1_attn(tag):
            with ExitStack() as c2:
                E2 = c2.enter_context
                KC2 = E2(nc.sbuf_tensor(tag + "KC2", [128, 2 * 2 * T], BF16))
                QC2 = E2(nc.sbuf_tensor(tag + "QC2", [128, 2 * T], BF16))
                VH2 = E2(nc.sbuf_tensor(tag + "VH2", [128, 2 * 2 * VW], BF16))
                EXT2 = E2(nc.sbuf_tensor(tag + "EXT2", [128, 3 * 1024], BF16))
                CTG3 = E2(nc.sbuf_tensor(tag + "CTG3", [128, 2 * T], BF16))
                SH3 = E2(nc.sbuf_tensor(tag + "SH3", [1, 2 * T], F32))
                S128b = E2(nc.sbuf_tensor(tag + "S128b", [128, 256], F32))
                R128b = E2(nc.sbuf_tensor(tag + "R128b", [128, 256], BF16))
                RBr2 = E2(nc.sbuf_tensor(tag + "RBr2", [1, 2 * T], BF16))
                CTF2 = E2(nc.sbuf_tensor(tag + "CTF2", [128, 2 * T], BF16))
                PS2 = E2(nc.psum_tensor(tag + "PS2", [128, 3 * 1024], F32))
                PC2 = E2(nc.psum_tensor(tag + "PC2", [128, 2 * 512], F32))

                NT2 = 2 * NT
                t0 = cnt["t"]; v0 = cnt["v"]; s0 = cnt["s"]
                kq0 = {0: cnt["k0"], 1: cnt["k1"]}
                vh0 = {0: cnt["r2"], 1: cnt["r3"]}
                pc0 = {0: cnt["p0"], 1: cnt["p1"]}
                sf0 = {0: cnt["p2"], 1: cnt["p3"]}
                TI_Q = 32
                SI_Q = 16

                with nc.Block() as blk:
                    @blk.gpsimd
                    def _(g):
                        g.wait_ge(sem["cc"], cc_kv)
                        g.nop().then_inc(sem["cc"], 1)
                    cnt["cc"] += 1
                    cc_r = cnt["cc"]

                    @blk.sync
                    def _(sy):
                        for p_ in ["k0", "k1", "r2", "r3", "p0", "p1", "p2", "p3"]:
                            sy.wait_ge(sem[p_], cnt[p_])
                        sy.wait_ge(sem["cc"], cc_r)
                        for h in range(H):
                            c = h // 2
                            cp = c % 2
                            hp = h % 2
                            if h % 2 == 0:
                                ksem = "k%d" % cp
                                sy.wait_ge(sem[ksem], kq0[cp] + 48 * (c // 2))
                                if c >= 2:
                                    sy.wait_ge(sem["t"], t0 + TI_Q * NG * (2 * (c - 2) + 2))
                                for r in range(2):
                                    sy.dma_start(
                                        out=KC2[:, cp * 2 * T + r * T:cp * 2 * T + (r + 1) * T],
                                        in_=bass.AP(k1f_d, r * D * T + c * 128 * T,
                                                    [[T, 128], [1, T]])
                                        ).then_inc(sem[ksem], 16)
                                sy.dma_start(out=QC2[:, cp * T:(cp + 1) * T],
                                             in_=q1_d.ap()[c * 128:(c + 1) * 128, :]
                                             ).then_inc(sem[ksem], 16)
                            vsem = "r%d" % (2 + hp)
                            sy.wait_ge(sem[vsem], vh0[hp] + 32 * (h // 2))
                            if h >= 2:
                                sy.wait_ge(sem["t"], t0 + TI_Q * NG * (h - 1))
                            for r in range(2):
                                sy.dma_start(
                                    out=VH2[:, hp * 2 * VW + r * VW:hp * 2 * VW + (r + 1) * VW],
                                    in_=bass.AP(v1f_d, r * NT * 128 * VW + h * 65,
                                                [[VW, 128], [128 * VW, NT], [1, 65]])
                                    ).then_inc(sem[vsem], 16)
                            psem = "p%d" % hp
                            sy.wait_ge(sem[psem], pc0[hp] + 16 * (h // 2))
                            sy.wait_ge(sem["v"], v0 + 5 * (h + 1))
                            sy.dma_start(out=ctxb_d.ap()[h * 65:(h + 1) * 65, :],
                                         in_=CTG3[0:65, hp * T:(hp + 1) * T]
                                         ).then_inc(sem[psem], 16)
                            fsem = "p%d" % (2 + hp)
                            sy.wait_ge(sem[fsem], sf0[hp] + 16 * (h // 2))
                            sy.wait_ge(sem["v"], v0 + 5 * (h + 1))
                            sy.dma_start(out=bass.AP(S128b, h * 16, [[256, 128], [1, 16]]),
                                         in_=SH3[0:1, hp * T:(hp + 1) * T]
                                         ).then_inc(sem[fsem], 16)
                        for hh2, sname, base_, amt in [(0, "k0", kq0[0], 48 * 4),
                                                       (0, "k1", kq0[1], 48 * 4),
                                                       (0, "r2", vh0[0], 32 * 8),
                                                       (0, "r3", vh0[1], 32 * 8),
                                                       (0, "p0", pc0[0], 16 * 8),
                                                       (0, "p1", pc0[1], 16 * 8),
                                                       (0, "p2", sf0[0], 16 * 8),
                                                       (0, "p3", sf0[1], 16 * 8)]:
                            sy.wait_ge(sem[sname], base_ + amt)
                    cnt["k0"] += 48 * 4
                    cnt["k1"] += 48 * 4
                    cnt["r2"] += 32 * 8
                    cnt["r3"] += 32 * 8
                    for p_ in range(4):
                        cnt["p%d" % p_] += 16 * 8

                    @blk.tensor
                    def _(t):
                        for h in range(H):
                            c = h // 2
                            cp = c % 2
                            hp = h % 2
                            base = 64 * hp
                            t.wait_ge(sem["k%d" % cp], kq0[cp] + 48 * (c // 2 + 1))
                            t.wait_ge(sem["r%d" % (2 + hp)], vh0[hp] + 32 * (h // 2 + 1))
                            if h >= 1:
                                t.wait_ge(sem["v"], v0 + 5 * h)
                            for qg in range(NG):
                                if qg >= 2:
                                    t.wait_ge(sem["v"], v0 + 5 * h + qg - 1)

                                def scores_u(u):
                                    iq = SI_Q * (h * NG + qg) + u
                                    if iq >= 3:
                                        t.wait_ge(sem["s"], s0 + iq - 2)
                                    for k4 in range(2):
                                        kt = u * 2 + k4
                                        mm = t.matmul(
                                            PS2[:, (iq % 3) * 1024 + k4 * 512:
                                                (iq % 3) * 1024 + (k4 + 1) * 512],
                                            KC2[base:base + 64,
                                                cp * 2 * T + kt * 128:cp * 2 * T + (kt + 1) * 128],
                                            QC2[base:base + 64,
                                                cp * T + qg * 512:cp * T + (qg + 1) * 512],
                                            start=True, stop=True)
                                    mm.then_inc(sem["t"], 1)

                                scores_u(0)
                                scores_u(1)
                                for u in range(16):
                                    iq = SI_Q * (h * NG + qg) + u
                                    if u + 2 < 16:
                                        scores_u(u + 2)
                                    t.wait_ge(sem["s"], s0 + iq + 1)
                                    for k4 in range(2):
                                        kt = u * 2 + k4
                                        mm = t.matmul(
                                            PC2[0:65, (qg % 2) * 512:(qg % 2 + 1) * 512],
                                            VH2[:, hp * 2 * VW + kt * 65:hp * 2 * VW + (kt + 1) * 65],
                                            EXT2[:, (iq % 3) * 1024 + k4 * 512:
                                                 (iq % 3) * 1024 + (k4 + 1) * 512],
                                            start=(u == 0 and k4 == 0),
                                            stop=(u == 15 and k4 == 1))
                                    mm.then_inc(sem["t"], 1)
                    cnt["t"] += H * NG * TI_Q

                    @blk.scalar
                    def _(s):
                        for h in range(H):
                            for qg in range(NG):
                                for u in range(16):
                                    iq = SI_Q * (h * NG + qg) + u
                                    jsc = u if u < 2 else 2 * u - 2
                                    s.wait_ge(sem["t"], t0 + TI_Q * (h * NG + qg) + jsc + 1)
                                    s.activation(EXT2[:, (iq % 3) * 1024:(iq % 3 + 1) * 1024],
                                                 PS2[:, (iq % 3) * 1024:(iq % 3 + 1) * 1024],
                                                 AF.Exp, bias=NM2[:, h:h + 1],
                                                 scale=0.125).then_inc(sem["s"], 1)
                    cnt["s"] += H * NG * SI_Q

                    @blk.vector
                    def _(v):
                        for h in range(H):
                            hp = h % 2
                            if h >= 2:
                                v.wait_ge(sem["p%d" % hp], pc0[hp] + 16 * (h // 2))
                                v.wait_ge(sem["p%d" % (2 + hp)], sf0[hp] + 16 * (h // 2))
                            for qg in range(NG):
                                v.wait_ge(sem["t"], t0 + TI_Q * (h * NG + qg + 1))
                                v.tensor_copy(
                                    CTG3[0:65, hp * T + qg * 512:hp * T + (qg + 1) * 512],
                                    PC2[0:65, (qg % 2) * 512:(qg % 2 + 1) * 512])
                                v.tensor_copy(SH3[0:1, hp * T + qg * 512:hp * T + (qg + 1) * 512],
                                              PC2[64:65, (qg % 2) * 512:(qg % 2 + 1) * 512]
                                              ).then_inc(sem["v"], 1)
                            v.nop().then_inc(sem["v"], 1)
                    cnt["v"] += H * (NG + 1)

                with nc.Block() as blk:
                    @blk.vector
                    def _(v):
                        v.tensor_scalar(S128b[:, :], S128b[:, :], 1e-30, None, ALU.max)
                        v.drain()
                        with nc.allow_low_precision(reason="bf16 normalizer broadcast"):
                            v.reciprocal(R128b[:, :], S128b[:, :])
                        v.drain()
                        v.nop().then_inc(sem["v"], 1)
                    cnt["v"] += 1

                t1 = cnt["t"]; v1 = cnt["v"]
                pr0 = {0: cnt["r0"], 1: cnt["r1"]}
                pf0 = {0: cnt["p0"], 1: cnt["p1"]}
                with nc.Block() as blk:
                    @blk.sync
                    def _(sy):
                        for p_ in ["r0", "r1", "p0", "p1"]:
                            sy.wait_ge(sem[p_], cnt[p_])
                        for h in range(H):
                            hp = h % 2
                            sy.wait_ge(sem["r%d" % hp], pr0[hp] + 16 * (h // 2))
                            if h >= 2:
                                sy.wait_ge(sem["t"], t1 + h - 1)
                            sy.dma_start(out=RBr2[0:1, hp * T:(hp + 1) * T],
                                         in_=bass.AP(R128b, h * 16, [[256, 128], [1, 16]])
                                         ).then_inc(sem["r%d" % hp], 16)
                            sy.wait_ge(sem["p%d" % hp], pf0[hp] + 32 * (h // 2))
                            if h >= 2:
                                sy.wait_ge(sem["v"], v1 + h - 1)
                            sy.dma_start(out=CTG3[0:65, hp * T:(hp + 1) * T],
                                         in_=ctxb_d.ap()[h * 65:(h + 1) * 65, :]
                                         ).then_inc(sem["p%d" % hp], 16)
                            sy.wait_ge(sem["v"], v1 + h + 1)
                            sy.dma_start(out=ctf2_d.ap()[h * 64:(h + 1) * 64, :],
                                         in_=CTF2[0:64, hp * T:(hp + 1) * T]
                                         ).then_inc(sem["p%d" % hp], 16)
                        sy.wait_ge(sem["r0"], pr0[0] + 16 * 8)
                        sy.wait_ge(sem["r1"], pr0[1] + 16 * 8)
                        sy.wait_ge(sem["p0"], pf0[0] + 32 * 8)
                        sy.wait_ge(sem["p1"], pf0[1] + 32 * 8)
                    for hh in range(H):
                        cnt["r%d" % (hh % 2)] += 16
                        cnt["p%d" % (hh % 2)] += 32

                    @blk.tensor
                    def _(t):
                        for h in range(H):
                            hp = h % 2
                            t.wait_ge(sem["r%d" % hp], pr0[hp] + 16 * (h // 2 + 1))
                            if h >= 1:
                                t.wait_ge(sem["v"], v1 + h)
                            for g4 in range(NG):
                                mm = t.matmul(PS2[:, g4 * 512:(g4 + 1) * 512], ONE1B[:, :],
                                              RBr2[0:1, hp * T + g4 * 512:hp * T + (g4 + 1) * 512],
                                              start=True, stop=True)
                            mm.then_inc(sem["t"], 1)
                    cnt["t"] += H

                    @blk.vector
                    def _(v):
                        for h in range(H):
                            hp = h % 2
                            v.wait_ge(sem["t"], t1 + h + 1)
                            v.wait_ge(sem["p%d" % hp], pf0[hp] + 32 * (h // 2) + 16)
                            v.tensor_tensor(CTF2[0:64, hp * T:(hp + 1) * T],
                                            CTG3[0:64, hp * T:(hp + 1) * T],
                                            PS2[0:64, 0:T], ALU.mult).then_inc(sem["v"], 1)
                    cnt["v"] += H

        if nstage >= 12:
            l1_attn("a1")

        if nstage >= 13:
            with ExitStack() as c5:
                CTFS2 = c5.enter_context(nc.sbuf_tensor("CTFS2", [128, ND * T], BF16))
                with nc.Block() as blk:
                    @blk.sync
                    def _(sy):
                        sy.wait_ge(sem["ld"], cnt["ld"])
                        for c in range(ND):
                            sy.dma_start(out=CTFS2[:, c * T:(c + 1) * T],
                                         in_=ctf2_d.ap()[c * 128:(c + 1) * 128, :]
                                         ).then_inc(sem["ld"], 16)
                        cnt["ld"] += 16 * ND
                        sy.wait_ge(sem["ld"], cnt["ld"])
                proj_fm(DI["l1wo"].ap(), DI["l1bo"], out_d.ap(), "pfo",
                        src_sb=CTFS2, res_dram=x2_d.ap(), res2_dram=moe_d.ap(),
                        out_f32=True)


        if nstage < 13:
            with nc.Block() as blk:
                @blk.gpsimd
                def _(g):
                    g.wait_ge(sem["sg"], cnt["sg"])
                    for c in range(ND):
                        g.dma_start(out=out_d.ap()[c * 128:(c + 1) * 128, :],
                                    in_=XN[:, c * T:(c + 1) * T]).then_inc(sem["sg"], 16)
                    cnt["sg"] += 16 * ND
                    g.wait_ge(sem["sg"], cnt["sg"])

    return nc


def _masks(half):
    t0 = half * T
    L = np.zeros((128, 128), np.float32)
    for b0 in (0, 64):
        L[b0 + 0, :] = 1.0
        L[b0 + 1, 0:64] = BIG
        L[b0 + 2, 64:128] = BIG
        L[b0 + 3, :] = -BIG
    gq = np.zeros(T, np.float32)
    for pos in GLOBALS:
        if t0 <= pos < t0 + T:
            gq[pos - t0] = 1.0
    qmod = np.arange(T) % 128
    R = np.zeros((128, T), np.float32)
    for b0 in (0, 64):
        R[b0 + 0, :] = -BIG
        R[b0 + 1, :] = (qmod < 64)
        R[b0 + 2, :] = (qmod >= 64)
        R[b0 + 3, :] = gq
    GL = np.zeros((128, 128), np.float32)
    for b0 in (0, 64):
        for g in range(8):
            GL[b0 + g, g] = -BIG
        GL[b0 + 8, 0:8] = -BIG
    GR = np.zeros((128, T), np.float32)
    for b0 in (0, 64):
        for g, pos in enumerate(GLOBALS):
            blk0 = (pos // 64) * 64
            if t0 <= blk0 < t0 + T:
                GR[b0 + g, blk0 - t0:blk0 - t0 + 64] = 1.0
        GR[b0 + 8, :] = gq
    SC = np.zeros((8, T), np.float32)
    for g, pos in enumerate(GLOBALS):
        if t0 <= pos < t0 + T:
            SC[g, pos - t0] = 1.0
    return {"maskl_lhs": _bf(L), "maskl_rhs": _bf(R), "maskg_lhs": _bf(GL),
            "maskg_rhs": _bf(GR), "scat_s": _bf(SC),
            "odd1": np.full((1, 1), float(half), np.float32)}


def _host_prep(inputs):
    x = np.asarray(inputs["L0_attn_x"], np.float32)
    shared = {
        "identf": np.eye(128, dtype=np.float32),
        "identb": _bf(np.eye(128, dtype=np.float32)),
        "ones1b": _bf(np.ones((1, 128))),
        "ones1f": np.ones((1, 128), np.float32),
        "onescol": _bf(np.ones((128, 1))),
        "onespair": _bf(np.stack([np.r_[np.ones(64), np.zeros(64)],
                                  np.r_[np.zeros(64), np.ones(64)]], 1)),
        "iota_e": np.tile(np.arange(NE, dtype=np.float32), (128, 1)),
        "revta": np.tile(8.0 - np.arange(NE, dtype=np.float32), (128, 1)),
        "l0wq": np.asarray(inputs["L0_attn_Wq"], np.float32),
        "l0wk": np.asarray(inputs["L0_attn_Wk"], np.float32),
        "l0wv": np.asarray(inputs["L0_attn_Wv"], np.float32),
        "l0wo": np.asarray(inputs["L0_attn_Wo"], np.float32),
        "l0bq": np.asarray(inputs["L0_attn_bq"], np.float32),
        "l0bk": np.asarray(inputs["L0_attn_bk"], np.float32),
        "l0bv": np.asarray(inputs["L0_attn_bv"], np.float32),
        "l0bo": np.asarray(inputs["L0_attn_bo"], np.float32),
        "l0winT": np.ascontiguousarray(np.asarray(inputs["L0_ffn_W_in"], np.float32).T),
        "l0bin": np.asarray(inputs["L0_ffn_b_in"], np.float32),
        "l0wout": np.asarray(inputs["L0_ffn_W_out"], np.float32),
        "l0bout": np.asarray(inputs["L0_ffn_b_out"], np.float32),
        "l1wq": np.ascontiguousarray(np.asarray(inputs["L1_attn_q_w"], np.float32).T),
        "l1wk": np.ascontiguousarray(np.asarray(inputs["L1_attn_k_w"], np.float32).T),
        "l1wv": np.ascontiguousarray(np.asarray(inputs["L1_attn_v_w"], np.float32).T),
        "l1wo": np.ascontiguousarray(np.asarray(inputs["L1_attn_out_w"], np.float32).T),
        "l1bq": np.asarray(inputs["L1_attn_q_b"], np.float32),
        "l1bk": np.asarray(inputs["L1_attn_k_b"], np.float32),
        "l1bv": np.asarray(inputs["L1_attn_v_b"], np.float32),
        "l1bo": np.asarray(inputs["L1_attn_out_b"], np.float32),
        "l1wrT": np.ascontiguousarray(np.asarray(inputs["L1_ffn_Wr"], np.float32).T),
        "l1we": np.asarray(inputs["L1_ffn_We"], np.float32),
    }
    maps = []
    for c in range(8):
        b, half = c // 2, c % 2
        m = dict(shared)
        m["xf"] = np.ascontiguousarray(x[b, half * T:(half + 1) * T, :].T)
        m["xg"] = np.ascontiguousarray(x[b, GLOBALS, :])
        m.update(_masks(half))
        maps.append(m)
    return maps


def kernel(**inputs):
    maps = _host_prep(inputs)
    nc = build_nc(taps=_TAPS)
    res = run_bass_kernel_spmd(nc, maps, core_ids=list(range(8)))
    full = np.zeros((B, S, D), np.float32)
    for c in range(8):
        b, half = c // 2, c % 2
        full[b, half * T:(half + 1) * T, :] = np.asarray(
            res.results[c]["out"], np.float32).T
    return full


# version 11
# speedup vs baseline: 1.1220x; 1.1220x over previous
import numpy as np
import ml_dtypes
from contextlib import ExitStack

import concourse.bass as bass
import concourse.mybir as mybir
from concourse.bass_utils import run_bass_kernel_spmd

F32 = mybir.dt.float32
BF16 = mybir.dt.bfloat16
AF = mybir.ActivationFunctionType
ALU = mybir.AluOpType
AX = mybir.AxisListType

B, S, D, H, DH = 4, 4096, 1024, 16, 64
T = 2048
ND = D // 128
NT = T // 128
NG = T // 512
FF = 4096
NFF = FF // 128
NE = 8
CCAP = 1537
GLOBALS = [0, 585, 1170, 1755, 2340, 2925, 3510, 4095]
VW = H * 65
MAGIC = 12582912.0
BIG = 1e30
EPS = 1e-5

_TAPS = ()


def _bf(a):
    return np.ascontiguousarray(np.asarray(a, np.float32).astype(ml_dtypes.bfloat16))


def build_nc(taps=(), nstage=99):
    nc = bass.Bass("TRN2", target_bir_lowering=False)

    def din(nm, shp, dt=F32):
        return nc.dram_tensor(nm, shp, dt, kind="ExternalInput")

    DI = {}
    for nm, shp, dt in [
        ("xf", [D, T], F32), ("xg", [8, D], F32),
        ("l0wq", [D, D], F32), ("l0wk", [D, D], F32), ("l0wv", [D, D], F32),
        ("l0wo", [D, D], F32), ("l0bq", [D], F32), ("l0bk", [D], F32),
        ("l0bv", [D], F32), ("l0bo", [D], F32),
        ("l0winT", [FF, D], F32), ("l0bin", [FF], F32),
        ("l0wout", [FF, D], F32), ("l0bout", [D], F32),
        ("l1wq", [D, D], F32), ("l1wk", [D, D], F32), ("l1wv", [D, D], F32),
        ("l1wo", [D, D], F32), ("l1bq", [D], F32), ("l1bk", [D], F32),
        ("l1bv", [D], F32), ("l1bo", [D], F32),
        ("l1wrT", [D, NE], F32), ("l1we", [NE, D, D], F32),
        ("identf", [128, 128], F32), ("identb", [128, 128], BF16),
        ("ones1b", [1, 128], BF16), ("ones1f", [1, 128], F32),
        ("onescol", [128, 1], BF16), ("onespair", [128, 2], BF16),
        ("maskl_lhs", [128, 128], BF16), ("maskl_rhs", [128, T], BF16),
        ("maskg_lhs", [128, 128], BF16), ("maskg_rhs", [128, T], BF16),
        ("scat_s", [8, T], BF16),
        ("iota_e", [128, NE], F32), ("revta", [128, NE], F32),
        ("odd1", [1, 1], F32),
    ]:
        DI[nm] = din(nm, shp, dt)

    out_d = nc.dram_tensor("out", [D, T], F32, kind="ExternalOutput")
    tapspec = {
        "xn": ([D, T], BF16), "x1": ([D, T], F32), "hidden": ([FF, T], BF16),
        "x2": ([D, T], F32), "xn2": ([D, T], BF16), "moe": ([D, T], F32),
        "amax": ([1, 2], F32), "hamax": ([1, 2], F32),
        "gc": ([NE, T], F32), "ctx0": ([D, T], BF16), "ctx1": ([D, T], BF16),
        "cnt": ([2, 40], F32),
    }
    tap_d = {tn: nc.dram_tensor("tap_" + tn, tapspec[tn][0], tapspec[tn][1],
                                kind="ExternalOutput") for tn in taps if tn in tapspec}

    def dint(nm, shp, dt=F32):
        if nm in taps:
            return nc.dram_tensor(nm, shp, dt, kind="ExternalOutput")
        return nc.dram_tensor(nm, shp, dt)

    amax_io = (dint("amax_in", [1, 2]), dint("amax_out", [1, 2]))
    hamax_io = (dint("hamax_in", [1, 2]), dint("hamax_out", [1, 2]))
    grow_in = dint("grow_in", [1, 8 * 16 * 66])
    grow_out = dint("grow_out", [2, 8 * 16 * 66])
    cnt_in = dint("cnt_in", [1, 24])
    cnt_out = dint("cnt_out", [2, 24])
    KVH = ND * 128 * T + NT * 128 * VW
    kv_in = dint("kv_in", [1, KVH], BF16)
    kv_out = dint("kv_out", [2, KVH], BF16)
    k0_d = dint("k0_d", [D, T], BF16)
    q0_d = dint("q0_d", [D, T], BF16)
    v0_d = dint("v0_d", [NT * 128, VW], BF16)
    ctx_d = dint("ctx_d", [16 * 65, T], BF16)
    ctf_d = dint("ctf_d", [D, T], BF16)
    x1_d = dint("x1_d", [D, T])
    x2_d = dint("x2_d", [D, T])
    q1_d = dint("q1_d", [D, T], BF16)
    moe_d = dint("moe_d", [D, T])
    ctxb_d = dint("ctxb_d", [16 * 65, T], BF16)
    RG8 = [list(range(8))]
    RG2 = [[0, 1], [2, 3], [4, 5], [6, 7]]

    with ExitStack() as ctx:
        E_ = ctx.enter_context
        sem = {}
        cnt = {}
        for s_ in ["ld", "lg", "lsc", "st", "sg", "t", "v", "s", "cc",
                   "p0", "p1", "p2", "p3", "r0", "r1", "r2", "r3", "k0", "k1",
                   "g0", "g1"]:
            sem[s_] = E_(nc.semaphore("sem_" + s_))
            cnt[s_] = 0

        IDF = E_(nc.sbuf_tensor("IDF", [128, 128], F32))
        IDB = E_(nc.sbuf_tensor("IDB", [128, 128], BF16))
        ONE1B = E_(nc.sbuf_tensor("ONE1B", [1, 128], BF16))
        ONE1F = E_(nc.sbuf_tensor("ONE1F", [1, 128], F32))
        ONECOL = E_(nc.sbuf_tensor("ONECOL", [128, 1], BF16))
        ONEPAIR = E_(nc.sbuf_tensor("ONEPAIR", [128, 2], BF16))
        XN = E_(nc.sbuf_tensor("XN", [128, ND * T], BF16))

        with nc.Block() as blk:
            @blk.sync
            def _(sy):
                for tdst, tsrc in [(IDF, "identf"), (IDB, "identb"), (ONE1B, "ones1b"),
                                   (ONE1F, "ones1f"), (ONECOL, "onescol"),
                                   (ONEPAIR, "onespair")]:
                    sy.dma_start(out=tdst[:, :], in_=DI[tsrc].ap()).then_inc(sem["ld"], 16)
                    cnt["ld"] += 16
                sy.wait_ge(sem["ld"], cnt["ld"])

        def layernorm(x_dram, tag, amax_io_=None):
            TW = NT
            with ExitStack() as c2:
                E2 = c2.enter_context
                Xl = E2(nc.sbuf_tensor(tag + "X", [128, ND * T], F32))
                Xb = E2(nc.sbuf_tensor(tag + "Xb", [128, 2 * T], BF16))
                XSQ = E2(nc.sbuf_tensor(tag + "XSQ", [128, 2 * T], BF16))
                SR = E2(nc.sbuf_tensor(tag + "SR", [1, 2 * T], F32))
                STT_ = E2(nc.sbuf_tensor(tag + "ST", [128, 4 * TW], F32))
                RR = E2(nc.sbuf_tensor(tag + "RR", [1, 2 * T], F32))
                AB = E2(nc.sbuf_tensor(tag + "AB", [128, T], F32))
                BB = E2(nc.sbuf_tensor(tag + "BB", [128, T], F32))
                AM = E2(nc.sbuf_tensor(tag + "AM", [128, 2], F32))
                AM1 = E2(nc.sbuf_tensor(tag + "AM1", [1, 1], F32))
                TMP = E2(nc.sbuf_tensor(tag + "TMP", [128, T], F32))
                PZ = E2(nc.psum_tensor(tag + "PZ", [128, T], F32))
                PZS = E2(nc.psum_tensor(tag + "PZS", [1, T], F32))

                with nc.Block() as blk:
                    @blk.sync
                    def _(sy):
                        sy.wait_ge(sem["ld"], cnt["ld"])
                        for c in range(ND):
                            sy.dma_start(out=Xl[:, c * T:(c + 1) * T],
                                         in_=x_dram[c * 128:(c + 1) * 128, :]).then_inc(sem["ld"], 16)
                    cnt["ld"] += 16 * ND
                    ld1 = cnt["ld"]
                    v0 = cnt["v"]; t0 = cnt["t"]
                    lgx = {0: cnt["g0"], 1: cnt["g1"]}

                    @blk.gpsimd
                    def _(g):
                        for p_ in ["g0", "g1"]:
                            g.wait_ge(sem[p_], cnt[p_])
                        for c in range(ND):
                            p = c % 2
                            g.wait_ge(sem["g%d" % p], lgx[p] + 16 * (c // 2))
                            if c >= 2:
                                g.wait_ge(sem["t"], t0 + c - 1)
                            g.dma_start(out=Xb[:, p * T:(p + 1) * T],
                                        in_=x_dram[c * 128:(c + 1) * 128, :]
                                        ).then_inc(sem["g%d" % p], 16)
                        for p_ in [0, 1]:
                            g.wait_ge(sem["g%d" % p_], lgx[p_] + 16 * 4)
                    cnt["g0"] += 16 * 4
                    cnt["g1"] += 16 * 4

                    @blk.vector
                    def _(v):
                        for c in range(ND):
                            p = c % 2
                            v.wait_ge(sem["g%d" % p], lgx[p] + 16 * (c // 2 + 1))
                            if c >= 2:
                                v.wait_ge(sem["t"], t0 + c - 1)
                            v.tensor_tensor(XSQ[:, p * T:(p + 1) * T], Xb[:, p * T:(p + 1) * T],
                                            Xb[:, p * T:(p + 1) * T],
                                            ALU.mult).then_inc(sem["v"], 1)
                    cnt["v"] += ND

                    @blk.tensor
                    def _(t):
                        for c in range(ND):
                            p = c % 2
                            t.wait_ge(sem["v"], v0 + c + 1)
                            for g4 in range(NG):
                                t.matmul(PZS[0:1, g4 * 512:(g4 + 1) * 512], ONECOL[:, :],
                                         Xb[:, p * T + g4 * 512:p * T + (g4 + 1) * 512],
                                         start=(c == 0), stop=(c == ND - 1))
                                mm = t.matmul(PZ[0:1, g4 * 512:(g4 + 1) * 512], ONECOL[:, :],
                                              XSQ[:, p * T + g4 * 512:p * T + (g4 + 1) * 512],
                                              start=(c == 0), stop=(c == ND - 1))
                                if g4 == NG - 1:
                                    mm.then_inc(sem["t"], 1)
                    cnt["t"] += ND
                    t2 = cnt["t"]

                    @blk.vector
                    def _(v):
                        v.wait_ge(sem["t"], t2)
                        v.tensor_copy(SR[:, 0:T], PZS[0:1, :])
                        v.drain()
                        v.tensor_copy(SR[:, T:2 * T], PZ[0:1, :])
                        v.drain()
                        v.nop().then_inc(sem["v"], 1)
                    cnt["v"] += 1
                    v3 = cnt["v"]

                    @blk.sync
                    def _(sy):
                        sy.wait_ge(sem["v"], v3)
                        sy.wait_ge(sem["ld"], cnt["ld"])
                        sy.dma_start(out=STT_[:, 0:TW], in_=SR[:, 0:T]).then_inc(sem["ld"], 16)
                        sy.dma_start(out=STT_[:, TW:2 * TW], in_=SR[:, T:2 * T]).then_inc(sem["ld"], 16)
                    cnt["ld"] += 32
                    ld2 = cnt["ld"]

                    @blk.vector
                    def _(v):
                        v.wait_ge(sem["ld"], ld2)
                        v.tensor_scalar(STT_[:, 0:TW], STT_[:, 0:TW], 1.0 / D, None, ALU.mult)
                        v.tensor_scalar(STT_[:, TW:2 * TW], STT_[:, TW:2 * TW], 1.0 / D, None, ALU.mult)
                        v.drain()
                        v.tensor_tensor(STT_[:, 2 * TW:3 * TW], STT_[:, 0:TW], STT_[:, 0:TW], ALU.mult)
                        v.drain()
                        v.tensor_tensor(STT_[:, 2 * TW:3 * TW], STT_[:, TW:2 * TW],
                                        STT_[:, 2 * TW:3 * TW], ALU.subtract)
                        v.drain()
                        v.tensor_scalar(STT_[:, 2 * TW:3 * TW], STT_[:, 2 * TW:3 * TW], EPS, None, ALU.add)
                        v.drain()
                        v.nop().then_inc(sem["v"], 1)
                    cnt["v"] += 1
                    v4 = cnt["v"]

                    @blk.scalar
                    def _(s):
                        s.wait_ge(sem["v"], v4)
                        s.activation(STT_[:, 3 * TW:4 * TW], STT_[:, 2 * TW:3 * TW], AF.Sqrt)
                        s.drain()
                        s.nop().then_inc(sem["s"], 1)
                    cnt["s"] += 1
                    s1 = cnt["s"]

                    @blk.vector
                    def _(v):
                        v.wait_ge(sem["s"], s1)
                        v.reciprocal(STT_[:, 2 * TW:3 * TW], STT_[:, 3 * TW:4 * TW])
                        v.drain()
                        v.tensor_tensor(STT_[:, 3 * TW:4 * TW], STT_[:, 0:TW],
                                        STT_[:, 2 * TW:3 * TW], ALU.mult)
                        v.drain()
                        v.tensor_scalar(STT_[:, 3 * TW:4 * TW], STT_[:, 3 * TW:4 * TW],
                                        -1.0, None, ALU.mult)
                        v.drain()
                        v.nop().then_inc(sem["v"], 1)
                    cnt["v"] += 1
                    v5 = cnt["v"]

                    @blk.sync
                    def _(sy):
                        sy.wait_ge(sem["v"], v5)
                        sy.wait_ge(sem["ld"], cnt["ld"])
                        sy.dma_start(out=RR[:, 0:T], in_=STT_[:, 2 * TW:3 * TW]).then_inc(sem["ld"], 16)
                        sy.dma_start(out=RR[:, T:2 * T], in_=STT_[:, 3 * TW:4 * TW]).then_inc(sem["ld"], 16)
                    cnt["ld"] += 32
                    ld3 = cnt["ld"]

                    @blk.tensor
                    def _(t):
                        t.wait_ge(sem["ld"], ld3)
                        for g4 in range(NG):
                            mm = t.matmul(PZ[:, g4 * 512:(g4 + 1) * 512], ONE1F[:, :],
                                          RR[:, g4 * 512:(g4 + 1) * 512], start=True, stop=True)
                        mm.then_inc(sem["t"], 1)
                    cnt["t"] += 1
                    t3 = cnt["t"]

                    @blk.vector
                    def _(v):
                        v.wait_ge(sem["t"], t3)
                        v.tensor_copy(AB[:, :], PZ[:, :])
                        v.drain()
                        v.nop().then_inc(sem["v"], 1)
                    cnt["v"] += 1
                    v6 = cnt["v"]

                    @blk.tensor
                    def _(t):
                        t.wait_ge(sem["v"], v6)
                        for g4 in range(NG):
                            mm = t.matmul(PZ[:, g4 * 512:(g4 + 1) * 512], ONE1F[:, :],
                                          RR[:, T + g4 * 512:T + (g4 + 1) * 512], start=True, stop=True)
                        mm.then_inc(sem["t"], 1)
                    cnt["t"] += 1
                    t4 = cnt["t"]

                    @blk.vector
                    def _(v):
                        v.wait_ge(sem["t"], t4)
                        v.tensor_copy(BB[:, :], PZ[:, :])
                        v.drain()
                        for c in range(ND):
                            v.tensor_tensor(TMP[:, :], Xl[:, c * T:(c + 1) * T], AB[:, :], ALU.mult)
                            v.drain()
                            v.tensor_tensor(XN[:, c * T:(c + 1) * T], TMP[:, :], BB[:, :], ALU.add)
                            v.drain()
                        if amax_io_ is not None:
                            for c in range(ND):
                                v.tensor_reduce(AM[:, 1:2], XN[:, c * T:(c + 1) * T], AX.X,
                                                ALU.max, apply_absolute_value=True)
                                v.drain()
                                if c == 0:
                                    v.tensor_copy(AM[:, 0:1], AM[:, 1:2])
                                else:
                                    v.tensor_tensor(AM[:, 0:1], AM[:, 0:1], AM[:, 1:2], ALU.max)
                                v.drain()
                        v.nop().then_inc(sem["v"], 1)
                    cnt["v"] += 1
                    v7 = cnt["v"]

                    if amax_io_ is not None:
                        @blk.tensor
                        def _(t):
                            t.wait_ge(sem["v"], v7)
                            t.transpose(PZ[0:1, 0:128], AM[:, 0:1], IDF[:, :]).then_inc(sem["t"], 1)
                        cnt["t"] += 1
                        t5 = cnt["t"]

                        @blk.vector
                        def _(v):
                            v.wait_ge(sem["t"], t5)
                            v.tensor_reduce(AM1[:, :], PZ[0:1, 0:128], AX.X, ALU.max)
                            v.drain()
                            v.nop().then_inc(sem["v"], 1)
                        cnt["v"] += 1
                        v8 = cnt["v"]

                        @blk.gpsimd
                        def _(g):
                            g.wait_ge(sem["v"], v8)
                            g.dma_start(out=amax_io_[0].ap()[0:1, 0:1],
                                        in_=AM1[:, :]).then_inc(sem["lg"], 16)
                            g.wait_ge(sem["lg"], cnt["lg"] + 16)
                            g.collective_compute(
                                "AllReduce", ALU.max, replica_groups=RG8,
                                ins=[amax_io_[0].ap()[0:1, 0:1].opt()],
                                outs=[amax_io_[1].ap()[0:1, 0:1].opt()],
                            ).then_inc(sem["cc"], 1)
                        cnt["lg"] += 16
                        cnt["cc"] += 1

        def proj_fm(w_dram, b_dram, dst_dram, tag, dst_sb=None, src_sb=None,
                    res_dram=None, res2_dram=None, out_f32=False):
            src = src_sb if src_sb is not None else XN
            with ExitStack() as c2:
                E2 = c2.enter_context
                WB = E2(nc.sbuf_tensor(tag + "WB", [128, ND * D], BF16))
                BC = E2(nc.sbuf_tensor(tag + "BC", [128, ND], F32))
                OT = E2(nc.sbuf_tensor(tag + "OT", [128, 4 * 512], F32 if out_f32 else BF16))
                RES = E2(nc.sbuf_tensor(tag + "RES", [128, 4 * 512], F32)) \
                    if res_dram is not None else None
                RES2 = E2(nc.sbuf_tensor(tag + "RES2", [128, 4 * 512], F32)) \
                    if res2_dram is not None else None
                PP = E2(nc.psum_tensor(tag + "PP", [128, T], F32))
                with nc.Block() as blk:
                    @blk.sync
                    def _(sy):
                        sy.wait_ge(sem["ld"], cnt["ld"])
                        sy.dma_start(out=BC[:, :],
                                     in_=bass.AP(b_dram, 0, [[1, 128], [128, ND]]),
                                     allow_slow_non_contiguous=True).then_inc(sem["ld"], 16)
                    cnt["ld"] += 16
                    ldb = cnt["ld"]

                    @blk.gpsimd
                    def _(g):
                        g.wait_ge(sem["lg"], cnt["lg"])
                        for c in range(ND):
                            g.dma_start(out=WB[:, c * D:(c + 1) * D],
                                        in_=w_dram[c * 128:(c + 1) * 128, :]).then_inc(sem["lg"], 16)
                    cnt["lg"] += 16 * ND
                    lgw = cnt["lg"]

                    t0 = cnt["t"]; s0 = cnt["s"]
                    pp0 = {p: cnt["p%d" % p] for p in range(4)}
                    rr0 = {p: cnt["r%d" % p] for p in range(4)}

                    nres = (1 if res_dram is not None else 0) + (1 if res2_dram is not None else 0)
                    if res_dram is not None:
                        @blk.scalar
                        def _(s):
                            for p_ in ["r0", "r1", "r2", "r3"]:
                                s.wait_ge(sem[p_], cnt[p_])
                            for i in range(ND * NG):
                                do, g4 = divmod(i, NG)
                                p = i % 4
                                s.wait_ge(sem["r%d" % p], rr0[p] + 16 * nres * (i // 4))
                                if i >= 4:
                                    s.wait_ge(sem["s"], s0 + i - 3)
                                s.dma_start(out=RES[:, p * 512:(p + 1) * 512],
                                            in_=res_dram[do * 128:(do + 1) * 128,
                                                         g4 * 512:(g4 + 1) * 512]
                                            ).then_inc(sem["r%d" % p], 16)
                                if res2_dram is not None:
                                    s.dma_start(out=RES2[:, p * 512:(p + 1) * 512],
                                                in_=res2_dram[do * 128:(do + 1) * 128,
                                                              g4 * 512:(g4 + 1) * 512]
                                                ).then_inc(sem["r%d" % p], 16)
                            for p_ in range(4):
                                s.wait_ge(sem["r%d" % p_], rr0[p_] + 16 * nres * len(
                                    [i for i in range(ND * NG) if i % 4 == p_]))
                        for p_ in range(4):
                            cnt["r%d" % p_] += 16 * nres * len(
                                [i for i in range(ND * NG) if i % 4 == p_])

                    @blk.tensor
                    def _(t):
                        t.wait_ge(sem["lg"], lgw)
                        for i in range(ND * NG):
                            do, g4 = divmod(i, NG)
                            if i >= 4:
                                t.wait_ge(sem["s"], s0 + i - 3)
                            for c in range(ND):
                                mm = t.matmul(PP[:, (i % 4) * 512:((i % 4) + 1) * 512],
                                              WB[:, c * D + do * 128:c * D + (do + 1) * 128],
                                              src[:, c * T + g4 * 512:c * T + (g4 + 1) * 512],
                                              start=(c == 0), stop=(c == ND - 1))
                            mm.then_inc(sem["t"], 1)
                    cnt["t"] += ND * NG

                    @blk.vector
                    def _(v):
                        v.wait_ge(sem["ld"], ldb)
                        for i in range(ND * NG):
                            do, g4 = divmod(i, NG)
                            p = i % 4
                            v.wait_ge(sem["t"], t0 + i + 1)
                            if i >= 4:
                                v.wait_ge(sem["p%d" % p], pp0[p] + 16 * (i // 4))
                            if res_dram is not None:
                                v.wait_ge(sem["r%d" % p], rr0[p] + 16 * nres * (i // 4 + 1))
                                mmv = v.scalar_tensor_tensor(OT[:, p * 512:(p + 1) * 512],
                                                             PP[:, p * 512:(p + 1) * 512],
                                                             BC[:, do:do + 1],
                                                             RES[:, p * 512:(p + 1) * 512],
                                                             ALU.add, ALU.add)
                                if res2_dram is not None:
                                    v.drain()
                                    mmv = v.tensor_tensor(OT[:, p * 512:(p + 1) * 512],
                                                          OT[:, p * 512:(p + 1) * 512],
                                                          RES2[:, p * 512:(p + 1) * 512], ALU.add)
                            else:
                                mmv = v.tensor_scalar(OT[:, (i % 4) * 512:((i % 4) + 1) * 512],
                                                      PP[:, (i % 4) * 512:((i % 4) + 1) * 512],
                                                      BC[:, do:do + 1], None, ALU.add)
                            if dst_sb is not None:
                                mmv = v.tensor_scalar(dst_sb[:, do * T + g4 * 512:do * T + (g4 + 1) * 512],
                                                      PP[:, (i % 4) * 512:((i % 4) + 1) * 512],
                                                      BC[:, do:do + 1], None, ALU.add)
                            mmv.then_inc(sem["s"], 1)
                    cnt["s"] += ND * NG

                    @blk.sync
                    def _(sy):
                        for p in range(4):
                            sy.wait_ge(sem["p%d" % p], pp0[p])
                        for i in range(ND * NG):
                            do, g4 = divmod(i, NG)
                            p = i % 4
                            sy.wait_ge(sem["s"], s0 + i + 1)
                            sy.dma_start(out=dst_dram[do * 128:(do + 1) * 128, g4 * 512:(g4 + 1) * 512],
                                         in_=OT[:, p * 512:(p + 1) * 512]
                                         ).then_inc(sem["p%d" % p], 16)
                        for p_ in range(4):
                            sy.wait_ge(sem["p%d" % p_], pp0[p_] + 16 * len(
                                [i for i in range(ND * NG) if i % 4 == p_]))
                    for p_ in range(4):
                        cnt["p%d" % p_] += 16 * len([i for i in range(ND * NG) if i % 4 == p_])

        def tap_copy(name, src_dram, rows, cols, dt_rows=128):
            if name in tap_d:
                with nc.Block() as blk:
                    @blk.sync
                    def _(sy):
                        n = 0
                        for r0 in range(0, rows, dt_rows):
                            sy.dma_start(out=tap_d[name].ap()[r0:r0 + dt_rows, :],
                                         in_=src_dram[r0:r0 + dt_rows, :]).then_inc(sem["st"], 16)
                            n += 16
                        cnt["st"] += n
                        sy.wait_ge(sem["st"], cnt["st"])

        layernorm(DI["xf"].ap(), "ln1", amax_io if nstage >= 2 else None)
        cc_amax = cnt["cc"]

        if "xn" in tap_d:
            with nc.Block() as blk:
                @blk.sync
                def _(sy):
                    for c in range(ND):
                        sy.dma_start(out=tap_d["xn"].ap()[c * 128:(c + 1) * 128, :],
                                     in_=XN[:, c * T:(c + 1) * T]).then_inc(sem["st"], 16)
                    cnt["st"] += 16 * ND
                    sy.wait_ge(sem["st"], cnt["st"])

        if nstage >= 2:
            proj_fm(DI["l0wk"].ap(), DI["l0bk"], k0_d.ap(), "p2k")
            proj_fm(DI["l0wq"].ap(), DI["l0bq"], q0_d.ap(), "p2q")

        def proj_tm(w_dram, bv_dram, dst_dram, tag, src_sb=None):
            src = src_sb if src_sb is not None else XN
            with ExitStack() as c2:
                E2 = c2.enter_context
                WB = E2(nc.sbuf_tensor(tag + "WB", [128, ND * D], BF16))
                BR = E2(nc.sbuf_tensor(tag + "BR", [1, D], BF16))
                OT = E2(nc.sbuf_tensor(tag + "OT", [128, 4 * VW // 2], BF16))
                ONES = E2(nc.sbuf_tensor(tag + "ONES", [128, 16], BF16))
                PP = E2(nc.psum_tensor(tag + "PP", [128, T], F32))
                with nc.Block() as blk:
                    @blk.gpsimd
                    def _(g):
                        g.wait_ge(sem["lg"], cnt["lg"])
                        g.dma_start(out=BR[:, :],
                                    in_=bass.AP(bv_dram, 0, [[D, 1], [1, D]])).then_inc(sem["lg"], 16)
                        for c in range(ND):
                            g.dma_start(out=WB[:, c * D:(c + 1) * D],
                                        in_=w_dram[c * 128:(c + 1) * 128, :]).then_inc(sem["lg"], 16)
                    cnt["lg"] += 16 * (ND + 1)
                    lgw = cnt["lg"]

                    @blk.vector
                    def _(v):
                        v.memset(ONES[:, :], 1.0)
                        v.drain()
                        v.nop().then_inc(sem["v"], 1)
                    cnt["v"] += 1
                    vON = cnt["v"]

                    t0 = cnt["t"]; v0 = cnt["v"]
                    pp0 = {p: cnt["p%d" % p] for p in range(4)}

                    @blk.tensor
                    def _(t):
                        t.wait_ge(sem["lg"], lgw)
                        for i in range(NT * 2):
                            tt, g2 = divmod(i, 2)
                            if i >= 4:
                                t.wait_ge(sem["v"], v0 + i - 3)
                            for c in range(ND):
                                t.matmul(PP[:, (i % 4) * 512:((i % 4) + 1) * 512],
                                         src[:, c * T + tt * 128:c * T + (tt + 1) * 128],
                                         WB[:, c * D + g2 * 512:c * D + (g2 + 1) * 512],
                                         start=(c == 0), stop=False)
                            t.matmul(PP[:, (i % 4) * 512:((i % 4) + 1) * 512],
                                     ONE1B[0:1, :],
                                     BR[:, g2 * 512:(g2 + 1) * 512],
                                     start=False, stop=True).then_inc(sem["t"], 1)
                    cnt["t"] += NT * 2

                    @blk.vector
                    def _(v):
                        v.wait_ge(sem["v"], vON)
                        for i in range(NT * 2):
                            tt, g2 = divmod(i, 2)
                            p = i % 4
                            v.wait_ge(sem["t"], t0 + i + 1)
                            if i >= 4:
                                v.wait_ge(sem["p%d" % p], pp0[p] + 16 * (i // 4))
                            v.tensor_copy(
                                OT[:, (i % 4) * 520:(i % 4) * 520 + 520].rearrange(
                                    "p (h w) -> p h w", h=8)[:, :, 0:64],
                                PP[:, (i % 4) * 512:((i % 4) + 1) * 512].rearrange(
                                    "p (h w) -> p h w", h=8))
                            v.tensor_copy(OT[:, (i % 4) * 520:(i % 4) * 520 + 520].rearrange(
                                "p (h w) -> p h w", h=8)[:, :, 64:65], ONES[:, 0:8].rearrange(
                                "p (h w) -> p h w", h=8)).then_inc(sem["v"], 1)
                    cnt["v"] += NT * 2

                    @blk.sync
                    def _(sy):
                        for p in range(4):
                            sy.wait_ge(sem["p%d" % p], pp0[p])
                        for i in range(NT * 2):
                            tt, g2 = divmod(i, 2)
                            p = i % 4
                            sy.wait_ge(sem["v"], v0 + i + 1)
                            sy.dma_start(
                                out=dst_dram[tt * 128:(tt + 1) * 128, g2 * 520:(g2 + 1) * 520],
                                in_=OT[:, p * 520:p * 520 + 520]).then_inc(sem["p%d" % p], 16)
                        for p_ in range(4):
                            sy.wait_ge(sem["p%d" % p_], pp0[p_] + 16 * len(
                                [i for i in range(NT * 2) if i % 4 == p_]))
                    for p_ in range(4):
                        cnt["p%d" % p_] += 16 * len([i for i in range(NT * 2) if i % 4 == p_])

        if nstage >= 3:
            proj_tm(DI["l0wv"].ap(), DI["l0bv"], v0_d.ap(), "p2v")

        QGB = E_(nc.sbuf_tensor("QGB", [128, 64], BF16))
        KGB = E_(nc.sbuf_tensor("KGB", [128, 64], BF16))
        VGB = E_(nc.sbuf_tensor("VGB", [8, VW], BF16))
        NMneg = E_(nc.sbuf_tensor("NMneg", [128, 16], F32))
        MHP = E_(nc.sbuf_tensor("MHP", [128, 16], F32))

        def globals_prep(wq_nm, wk_nm, wv_nm, bq_nm, bk_nm, bv_nm, k_dram, q_dram,
                         xg_src, scale_inv, tag):
            with ExitStack() as c2:
                E2 = c2.enter_context
                XGl = E2(nc.sbuf_tensor(tag + "XG", [8, D], F32))
                XNG = E2(nc.sbuf_tensor(tag + "XNG", [8, D], F32))
                GS = E2(nc.sbuf_tensor(tag + "GS", [8, 8], F32))
                XGF = E2(nc.sbuf_tensor(tag + "XGF", [128, 64], BF16))
                WG = E2(nc.sbuf_tensor(tag + "WG", [128, ND * D], BF16))
                BCg = E2(nc.sbuf_tensor(tag + "BCg", [128, ND], F32))
                BRg = E2(nc.sbuf_tensor(tag + "BRg", [1, D], BF16))
                KCH = E2(nc.sbuf_tensor(tag + "KCH", [128, 2 * T], BF16))
                KSQ = E2(nc.sbuf_tensor(tag + "KSQ", [128, T], BF16))
                N2 = E2(nc.sbuf_tensor(tag + "N2", [2, 64], F32))
                NR = E2(nc.sbuf_tensor(tag + "NR", [1, 16], F32))
                PX = E2(nc.psum_tensor(tag + "PX", [128, 512], F32))
                PX2 = E2(nc.psum_tensor(tag + "PX2", [128, 64], F32))
                PV = E2(nc.psum_tensor(tag + "PV", [8, 1024], F32))
                PN = E2(nc.psum_tensor(tag + "PN", [2, T], F32))

                with nc.Block() as blk:
                    @blk.sync
                    def _(sy):
                        sy.wait_ge(sem["ld"], cnt["ld"])
                        sy.dma_start(out=XGl[:, :], in_=xg_src).then_inc(sem["ld"], 16)
                    cnt["ld"] += 16
                    ldg = cnt["ld"]

                    @blk.vector
                    def _(v):
                        v.wait_ge(sem["ld"], ldg)
                        v.tensor_reduce(GS[:, 0:1], XGl[:, :], AX.X, ALU.add)
                        v.drain()
                        v.tensor_scalar(GS[:, 0:1], GS[:, 0:1], 1.0 / D, None, ALU.mult)
                        v.drain()
                        v.tensor_tensor(XNG[:, :], XGl[:, :], XGl[:, :], ALU.mult)
                        v.drain()
                        v.tensor_reduce(GS[:, 1:2], XNG[:, :], AX.X, ALU.add)
                        v.drain()
                        v.tensor_scalar(GS[:, 1:2], GS[:, 1:2], 1.0 / D, None, ALU.mult)
                        v.drain()
                        v.tensor_tensor(GS[:, 2:3], GS[:, 0:1], GS[:, 0:1], ALU.mult)
                        v.drain()
                        v.tensor_tensor(GS[:, 2:3], GS[:, 1:2], GS[:, 2:3], ALU.subtract)
                        v.drain()
                        v.tensor_scalar(GS[:, 2:3], GS[:, 2:3], EPS, None, ALU.add)
                        v.drain()
                        v.nop().then_inc(sem["v"], 1)
                    cnt["v"] += 1
                    vg1 = cnt["v"]

                    @blk.scalar
                    def _(s):
                        s.wait_ge(sem["v"], vg1)
                        s.activation(GS[:, 3:4], GS[:, 2:3], AF.Sqrt)
                        s.drain()
                        s.nop().then_inc(sem["s"], 1)
                    cnt["s"] += 1
                    sg1 = cnt["s"]

                    @blk.vector
                    def _(v):
                        v.wait_ge(sem["s"], sg1)
                        v.reciprocal(GS[:, 4:5], GS[:, 3:4])
                        v.drain()
                        v.tensor_tensor(GS[:, 5:6], GS[:, 0:1], GS[:, 4:5], ALU.mult)
                        v.drain()
                        v.tensor_scalar(GS[:, 5:6], GS[:, 5:6], -1.0, None, ALU.mult)
                        v.drain()
                        v.tensor_scalar(XNG[:, :], XGl[:, :], GS[:, 4:5], GS[:, 5:6],
                                        ALU.mult, ALU.add)
                        v.drain()
                        v.nop().then_inc(sem["v"], 1)
                    cnt["v"] += 1
                    vg2 = cnt["v"]

                    @blk.tensor
                    def _(t):
                        t.wait_ge(sem["v"], vg2)
                        for c in range(ND):
                            t.transpose(PX2[:, c * 8:(c + 1) * 8],
                                        XNG[0:8, c * 128:(c + 1) * 128],
                                        IDF[0:8, 0:8]).then_inc(sem["t"], 1)
                    cnt["t"] += ND
                    tg1 = cnt["t"]

                    @blk.vector
                    def _(v):
                        v.wait_ge(sem["t"], tg1)
                        v.tensor_copy(XGF[:, :], PX2[:, :])
                        v.drain()
                        v.nop().then_inc(sem["v"], 1)
                    cnt["v"] += 1
                    vg3 = cnt["v"]

                def mini_fm(w_nm, b_nm, dstb):
                    with nc.Block() as blk:
                        @blk.gpsimd
                        def _(g):
                            g.wait_ge(sem["lg"], cnt["lg"])
                            for c in range(ND):
                                g.dma_start(out=WG[:, c * D:(c + 1) * D],
                                            in_=DI[w_nm].ap()[c * 128:(c + 1) * 128, :]
                                            ).then_inc(sem["lg"], 16)
                        cnt["lg"] += 16 * ND
                        lgw = cnt["lg"]

                        @blk.sync
                        def _(sy):
                            sy.wait_ge(sem["ld"], cnt["ld"])
                            sy.dma_start(out=BCg[:, :],
                                         in_=bass.AP(DI[b_nm], 0, [[1, 128], [128, ND]]),
                                         allow_slow_non_contiguous=True).then_inc(sem["ld"], 16)
                        cnt["ld"] += 16
                        ldb = cnt["ld"]

                        @blk.tensor
                        def _(t):
                            t.wait_ge(sem["lg"], lgw)
                            for do in range(ND):
                                for c in range(ND):
                                    mm = t.matmul(PX2[:, do * 8:(do + 1) * 8],
                                                  WG[:, c * D + do * 128:c * D + (do + 1) * 128],
                                                  XGF[:, c * 8:(c + 1) * 8],
                                                  start=(c == 0), stop=(c == ND - 1))
                            mm.then_inc(sem["t"], 1)
                        cnt["t"] += 1
                        tq = cnt["t"]

                        @blk.vector
                        def _(v):
                            v.wait_ge(sem["ld"], ldb)
                            v.wait_ge(sem["t"], tq)
                            for do in range(ND):
                                v.tensor_scalar(dstb[:, do * 8:(do + 1) * 8],
                                                PX2[:, do * 8:(do + 1) * 8],
                                                BCg[:, do:do + 1], None, ALU.add)
                            v.drain()
                            v.nop().then_inc(sem["v"], 1)
                        cnt["v"] += 1

                mini_fm(wq_nm, bq_nm, QGB)
                mini_fm(wk_nm, bk_nm, KGB)

                with nc.Block() as blk:
                    @blk.gpsimd
                    def _(g):
                        g.wait_ge(sem["lg"], cnt["lg"])
                        g.dma_start(out=BRg[:, :],
                                    in_=bass.AP(DI[bv_nm], 0, [[D, 1], [1, D]])
                                    ).then_inc(sem["lg"], 16)
                        for c in range(ND):
                            g.dma_start(out=WG[:, c * D:(c + 1) * D],
                                        in_=DI[wv_nm].ap()[c * 128:(c + 1) * 128, :]
                                        ).then_inc(sem["lg"], 16)
                    cnt["lg"] += 16 * (ND + 1)
                    lgv = cnt["lg"]

                    @blk.tensor
                    def _(t):
                        t.wait_ge(sem["lg"], lgv)
                        for g2 in range(2):
                            for c in range(ND):
                                t.matmul(PV[0:8, g2 * 512:(g2 + 1) * 512],
                                         XGF[:, c * 8:(c + 1) * 8],
                                         WG[:, c * D + g2 * 512:c * D + (g2 + 1) * 512],
                                         start=(c == 0), stop=False)
                            t.matmul(PV[0:8, g2 * 512:(g2 + 1) * 512],
                                     ONE1B[0:1, 0:8],
                                     BRg[:, g2 * 512:(g2 + 1) * 512],
                                     start=False, stop=True).then_inc(sem["t"], 1)
                    cnt["t"] += 2
                    tv = cnt["t"]

                    @blk.vector
                    def _(v):
                        v.wait_ge(sem["t"], tv)
                        for g2 in range(2):
                            v.tensor_copy(
                                VGB[:, g2 * 520:(g2 + 1) * 520].rearrange(
                                    "p (h w) -> p h w", h=8)[:, :, 0:64],
                                PV[0:8, g2 * 512:(g2 + 1) * 512].rearrange(
                                    "p (h w) -> p h w", h=8))
                            v.drain()
                        v.memset(VGB[:, :].rearrange("p (h w) -> p h w", h=16)[:, :, 64:65], 1.0)
                        v.drain()
                        v.nop().then_inc(sem["v"], 1)
                    cnt["v"] += 1

                def norms(src_dram, col0):
                    with nc.Block() as blk:
                        ks0 = cnt["ld"]
                        v0_ = cnt["v"]; t0_ = cnt["t"]

                        @blk.sync
                        def _(sy):
                            sy.wait_ge(sem["ld"], cnt["ld"])
                            for c in range(ND):
                                if c >= 2:
                                    sy.wait_ge(sem["v"], v0_ + 2 * (c - 2) + 1)
                                sy.dma_start(out=KCH[:, (c % 2) * T:(c % 2 + 1) * T],
                                             in_=src_dram[c * 128:(c + 1) * 128, :]
                                             ).then_inc(sem["ld"], 16)
                                sy.wait_ge(sem["ld"], ks0 + 16 * (c + 1))
                        cnt["ld"] += 16 * ND

                        @blk.vector
                        def _(v):
                            for c in range(ND):
                                p = c % 2
                                v.wait_ge(sem["ld"], ks0 + 16 * (c + 1))
                                if c >= 1:
                                    v.wait_ge(sem["t"], t0_ + c)
                                v.tensor_tensor(KSQ[:, :], KCH[:, p * T:(p + 1) * T],
                                                KCH[:, p * T:(p + 1) * T], ALU.mult)
                                v.drain()
                                v.nop().then_inc(sem["v"], 1)
                                v.wait_ge(sem["t"], t0_ + c + 1)
                                v.tensor_reduce(N2[0:2, col0 + c:col0 + c + 1], PN[0:2, :],
                                                AX.X, ALU.max)
                                v.drain()
                                v.nop().then_inc(sem["v"], 1)
                        cnt["v"] += 2 * ND

                        @blk.tensor
                        def _(t):
                            for c in range(ND):
                                t.wait_ge(sem["v"], v0_ + 2 * c + 1)
                                for g4 in range(NG):
                                    mm = t.matmul(PN[0:2, g4 * 512:(g4 + 1) * 512],
                                                  ONEPAIR[:, :],
                                                  KSQ[:, g4 * 512:(g4 + 1) * 512],
                                                  start=True, stop=True)
                                mm.then_inc(sem["t"], 1)
                        cnt["t"] += ND

                norms(k_dram, 0)
                norms(q_dram, 8)

                with nc.Block() as blk:
                    @blk.vector
                    def _(v):
                        v.tensor_tensor(KSQ[:, 0:64], QGB[:, :], QGB[:, :], ALU.mult)
                        v.drain()
                        v.tensor_tensor(KSQ[:, 64:128], KGB[:, :], KGB[:, :], ALU.mult)
                        v.drain()
                        v.nop().then_inc(sem["v"], 1)
                    cnt["v"] += 1
                    vq = cnt["v"]

                    @blk.tensor
                    def _(t):
                        t.wait_ge(sem["v"], vq)
                        t.matmul(PN[0:2, 0:64], ONEPAIR[:, :], KSQ[:, 0:64],
                                 start=True, stop=True)
                        t.matmul(PN[0:2, 64:128], ONEPAIR[:, :], KSQ[:, 64:128],
                                 start=True, stop=True).then_inc(sem["t"], 1)
                    cnt["t"] += 1
                    tq2 = cnt["t"]

                    @blk.vector
                    def _(v):
                        v.wait_ge(sem["t"], tq2)
                        for do in range(ND):
                            v.tensor_reduce(N2[0:2, 16 + do:17 + do],
                                            PN[0:2, do * 8:(do + 1) * 8], AX.X, ALU.max)
                            v.drain()
                            v.tensor_reduce(N2[0:2, 24 + do:25 + do],
                                            PN[0:2, 64 + do * 8:64 + (do + 1) * 8], AX.X, ALU.max)
                            v.drain()
                        v.tensor_tensor(N2[0:2, 32:40], N2[0:2, 8:16], N2[0:2, 16:24], ALU.max)
                        v.drain()
                        v.tensor_tensor(N2[0:2, 40:48], N2[0:2, 0:8], N2[0:2, 24:32], ALU.max)
                        v.drain()
                        v.tensor_tensor(N2[0:2, 48:56], N2[0:2, 32:40], N2[0:2, 40:48], ALU.mult)
                        v.drain()
                        v.nop().then_inc(sem["v"], 1)
                    cnt["v"] += 1
                    vq3 = cnt["v"]

                    @blk.scalar
                    def _(s):
                        s.wait_ge(sem["v"], vq3)
                        s.activation(N2[0:2, 56:64], N2[0:2, 48:56], AF.Sqrt)
                        s.drain()
                        s.mul(N2[0:2, 56:64], N2[0:2, 56:64], scale_inv)
                        s.drain()
                        s.nop().then_inc(sem["s"], 1)
                    cnt["s"] += 1
                    sq2 = cnt["s"]

                    @blk.tensor
                    def _(t):
                        t.wait_ge(sem["s"], sq2)
                        t.transpose(PX2[0:8, 0:2], N2[0:2, 56:64],
                                    IDF[0:2, 0:2]).then_inc(sem["t"], 1)
                    cnt["t"] += 1
                    tq25 = cnt["t"]

                    @blk.vector
                    def _(v):
                        v.wait_ge(sem["t"], tq25)
                        v.tensor_copy(GS[:, 6:8], PX2[0:8, 0:2])
                        v.drain()
                        v.nop().then_inc(sem["v"], 1)
                    cnt["v"] += 1
                    vq25 = cnt["v"]

                    @blk.sync
                    def _(sy):
                        sy.wait_ge(sem["v"], vq25)
                        sy.wait_ge(sem["ld"], cnt["ld"])
                        sy.dma_start(out=NR[0:1, 0:16],
                                     in_=GS[0:8, 6:8]).then_inc(sem["ld"], 16)
                    cnt["ld"] += 16
                    ldn = cnt["ld"]

                    @blk.tensor
                    def _(t):
                        t.wait_ge(sem["ld"], ldn)
                        t.matmul(PX[:, 0:16], ONE1F[:, :], NR[:, :],
                                 start=True, stop=True).then_inc(sem["t"], 1)
                    cnt["t"] += 1
                    tq3 = cnt["t"]

                    @blk.vector
                    def _(v):
                        v.wait_ge(sem["t"], tq3)
                        v.tensor_copy(MHP[:, :], PX[:, 0:16])
                        v.drain()
                        v.tensor_scalar(NMneg[:, :], PX[:, 0:16], -1.0, None, ALU.mult)
                        v.drain()
                        v.nop().then_inc(sem["v"], 1)
                    cnt["v"] += 1

        if nstage >= 4:
            globals_prep("l0wq", "l0wk", "l0wv", "l0bq", "l0bk", "l0bv",
                         k0_d.ap(), q0_d.ap(), DI["xg"].ap(), 0.125, "g0")

        def l0_attn_passA(tag):
            with ExitStack() as c2:
                E2 = c2.enter_context
                MRL = E2(nc.sbuf_tensor(tag + "MRL", [128, 128], BF16))
                MRR = E2(nc.sbuf_tensor(tag + "MRR", [128, T], BF16))
                MGL = E2(nc.sbuf_tensor(tag + "MGL", [128, 128], BF16))
                MGR = E2(nc.sbuf_tensor(tag + "MGR", [128, T], BF16))
                KC = E2(nc.sbuf_tensor(tag + "KC", [128, 2 * T], BF16))
                QC = E2(nc.sbuf_tensor(tag + "QC", [128, 2 * T], BF16))
                VH = E2(nc.sbuf_tensor(tag + "VH", [128, 2 * VW], BF16))
                EXT = E2(nc.sbuf_tensor(tag + "EXT", [128, T], BF16))
                EXG = E2(nc.sbuf_tensor(tag + "EXG", [128, T], BF16))
                GEX = E2(nc.sbuf_tensor(tag + "GEX", [128, 128], BF16))
                GRP = E2(nc.sbuf_tensor(tag + "GRP", [128, 8], F32))
                CTG = E2(nc.sbuf_tensor(tag + "CTG", [128, 2 * T], BF16))
                SHb = E2(nc.sbuf_tensor(tag + "SH", [1, 2 * T], F32))
                PS = E2(nc.psum_tensor(tag + "PS", [128, T], F32))
                PG = E2(nc.psum_tensor(tag + "PG", [128, T], F32))

                with nc.Block() as blk:
                    @blk.sync
                    def _(sy):
                        sy.wait_ge(sem["ld"], cnt["ld"])
                        for tdst, tsrc in [(MRL, "maskl_lhs"), (MRR, "maskl_rhs"),
                                           (MGL, "maskg_lhs"), (MGR, "maskg_rhs")]:
                            sy.dma_start(out=tdst[:, :], in_=DI[tsrc].ap()).then_inc(sem["ld"], 16)
                        cnt["ld"] += 16 * 4
                        sy.wait_ge(sem["ld"], cnt["ld"])
                    ldm = cnt["ld"]

                t0 = cnt["t"]; v0 = cnt["v"]; s0 = cnt["s"]
                kq0 = {0: cnt["k0"], 1: cnt["k1"]}
                vh0 = {0: cnt["r2"], 1: cnt["r3"]}
                pc0 = {0: cnt["p0"], 1: cnt["p1"]}
                sf0 = {0: cnt["p2"], 1: cnt["p3"]}

                with nc.Block() as blk:
                    @blk.sync
                    def _(sy):
                        for p_ in ["k0", "k1", "r2", "r3", "p0", "p1", "p2", "p3"]:
                            sy.wait_ge(sem[p_], cnt[p_])
                        for h in range(H):
                            base = 64 * (h % 2)
                            c = h // 2
                            cp = c % 2
                            hp = h % 2
                            if h % 2 == 0:
                                ksem = "k%d" % cp
                                sy.wait_ge(sem[ksem], kq0[cp] + 32 * (c // 2))
                                if c >= 2:
                                    sy.wait_ge(sem["t"], t0 + 5 * (2 * (c - 2) + 1) + 5)
                                sy.dma_start(out=KC[:, cp * T:(cp + 1) * T],
                                             in_=k0_d.ap()[c * 128:(c + 1) * 128, :]
                                             ).then_inc(sem[ksem], 16)
                                sy.dma_start(out=QC[:, cp * T:(cp + 1) * T],
                                             in_=q0_d.ap()[c * 128:(c + 1) * 128, :]
                                             ).then_inc(sem[ksem], 16)
                            vsem = "r%d" % (2 + hp)
                            sy.wait_ge(sem[vsem], vh0[hp] + 16 * (h // 2))
                            if h >= 2:
                                sy.wait_ge(sem["t"], t0 + 5 * (h - 2) + 5)
                            sy.dma_start(out=VH[:, hp * VW:(hp + 1) * VW],
                                         in_=bass.AP(v0_d, h * 65,
                                                     [[VW, 128], [128 * VW, NT], [1, 65]])
                                         ).then_inc(sem[vsem], 16)
                            psem = "p%d" % hp
                            sy.wait_ge(sem[psem], pc0[hp] + 16 * (h // 2))
                            sy.wait_ge(sem["v"], v0 + 3 * h + 3)
                            sy.dma_start(out=ctx_d.ap()[h * 65:(h + 1) * 65, :],
                                         in_=CTG[0:65, hp * T:(hp + 1) * T]
                                         ).then_inc(sem[psem], 16)
                            fsem = "p%d" % (2 + hp)
                            sy.wait_ge(sem[fsem], sf0[hp] + 16 * (h // 2))
                            sy.dma_start(out=bass.AP(S128, h * 16, [[256, 128], [1, 16]]),
                                         in_=SHb[0:1, hp * T:(hp + 1) * T]
                                         ).then_inc(sem[fsem], 16)
                        sy.wait_ge(sem["p0"], pc0[0] + 16 * 8)
                        sy.wait_ge(sem["p1"], pc0[1] + 16 * 8)
                        sy.wait_ge(sem["p2"], sf0[0] + 16 * 8)
                        sy.wait_ge(sem["p3"], sf0[1] + 16 * 8)
                        sy.wait_ge(sem["k0"], kq0[0] + 32 * 4)
                        sy.wait_ge(sem["k1"], kq0[1] + 32 * 4)
                        sy.wait_ge(sem["r2"], vh0[0] + 16 * 8)
                        sy.wait_ge(sem["r3"], vh0[1] + 16 * 8)
                    for hh in range(H):
                        cnt["k%d" % ((hh // 2) % 2)] += 32 if hh % 2 == 0 else 0
                        cnt["r%d" % (2 + hh % 2)] += 16
                        cnt["p%d" % (hh % 2)] += 16
                        cnt["p%d" % (2 + hh % 2)] += 16

                    @blk.tensor
                    def _(t):
                        for h in range(H):
                            base = 64 * (h % 2)
                            c = h // 2
                            cp = c % 2
                            hp = h % 2
                            ksem = "k%d" % cp
                            t.wait_ge(sem[ksem], kq0[cp] + 32 * (c // 2 + 1))
                            t.wait_ge(sem["r%d" % (2 + hp)], vh0[hp] + 16 * (h // 2 + 1))
                            if h >= 1:
                                t.wait_ge(sem["v"], v0 + 3 * (h - 1) + 3)
                            for qt in range(NT):
                                t.matmul(PS[:, qt * 128:(qt + 1) * 128],
                                         MRL[base:base + 4, :],
                                         MRR[base:base + 4, qt * 128:(qt + 1) * 128],
                                         start=True, stop=False)
                                mm = t.matmul(PS[:, qt * 128:(qt + 1) * 128],
                                              KC[base:base + 64, cp * T + qt * 128:cp * T + (qt + 1) * 128],
                                              QC[base:base + 64, cp * T + qt * 128:cp * T + (qt + 1) * 128],
                                              start=False, stop=True)
                            mm.then_inc(sem["t"], 1)
                            if h >= 1:
                                t.wait_ge(sem["v"], v0 + 3 * (h - 1) + 2)
                            for qt in range(NT):
                                t.matmul(PG[:, qt * 128:(qt + 1) * 128],
                                         MGL[base:base + 9, :],
                                         MGR[base:base + 9, qt * 128:(qt + 1) * 128],
                                         start=True, stop=False, skip_group_check=True)
                                mm = t.matmul(PG[0:8, qt * 128:(qt + 1) * 128],
                                              KGB[base:base + 64, c * 8:(c + 1) * 8],
                                              QC[base:base + 64, cp * T + qt * 128:cp * T + (qt + 1) * 128],
                                              start=False, stop=True, skip_group_check=True)
                            mm.then_inc(sem["t"], 1)
                            t.wait_ge(sem["s"], s0 + 3 * h + 2)
                            for kt in range(NT):
                                mm = t.matmul(PG[:, kt * 8:(kt + 1) * 8],
                                              KC[base:base + 64, cp * T + kt * 128:cp * T + (kt + 1) * 128],
                                              QGB[base:base + 64, c * 8:(c + 1) * 8],
                                              start=True, stop=True)
                            mm.then_inc(sem["t"], 1)
                            t.wait_ge(sem["s"], s0 + 3 * h + 3)
                            for qt in range(NT):
                                t.matmul(PS[0:65, qt * 128:(qt + 1) * 128],
                                         VH[:, hp * VW + qt * 65:hp * VW + (qt + 1) * 65],
                                         EXT[:, qt * 128:(qt + 1) * 128],
                                         start=True, stop=False)
                                t.matmul(PS[0:65, qt * 128:(qt + 1) * 128],
                                         VGB[0:8, h * 65:(h + 1) * 65],
                                         EXG[0:8, qt * 128:(qt + 1) * 128],
                                         start=False, stop=True)
                            for kt in range(NT):
                                mm = t.matmul(PG[0:65, 512:520],
                                              VH[:, hp * VW + kt * 65:hp * VW + (kt + 1) * 65],
                                              GEX[:, kt * 8:(kt + 1) * 8],
                                              start=(kt == 0), stop=(kt == NT - 1))
                            mm.then_inc(sem["t"], 1)
                            t.wait_ge(sem["v"], v0 + 3 * h + 1)
                            t.transpose(PG[0:8, 640:705], GRP[0:65, :],
                                        IDF[0:65, 0:65]).then_inc(sem["t"], 1)
                    cnt["t"] += 5 * H

                    @blk.scalar
                    def _(s):
                        for h in range(H):
                            if h >= 1:
                                s.wait_ge(sem["t"], t0 + 5 * (h - 1) + 4)
                            s.wait_ge(sem["t"], t0 + 5 * h + 1)
                            s.activation(EXT[:, :], PS[:, :], AF.Exp,
                                         bias=NMneg[:, h:h + 1],
                                         scale=0.125).then_inc(sem["s"], 1)
                            s.wait_ge(sem["t"], t0 + 5 * h + 2)
                            s.activation(EXG[:, :], PG[:, :], AF.Exp,
                                         bias=NMneg[:, h:h + 1],
                                         scale=0.125).then_inc(sem["s"], 1)
                            s.wait_ge(sem["t"], t0 + 5 * h + 3)
                            s.activation(GEX[:, :], PG[:, 0:128], AF.Exp,
                                         bias=NMneg[:, h:h + 1],
                                         scale=0.125).then_inc(sem["s"], 1)
                    cnt["s"] += 3 * H

                    @blk.vector
                    def _(v):
                        for h in range(H):
                            hp = h % 2
                            v.wait_ge(sem["t"], t0 + 5 * h + 4)
                            v.tensor_copy(GRP[0:65, :],
                                          PG[0:65, 512:520]).then_inc(sem["v"], 1)
                            v.wait_ge(sem["t"], t0 + 5 * h + 5)
                            v.tensor_copy(PAY[0:8, h * 66:h * 66 + 65], PG[0:8, 640:705])
                            v.tensor_copy(PAY[0:8, h * 66 + 65:h * 66 + 66],
                                          MHP[0:8, h:h + 1]).then_inc(sem["v"], 1)
                            if h >= 2:
                                v.wait_ge(sem["p%d" % hp], pc0[hp] + 16 * (h // 2))
                                v.wait_ge(sem["p%d" % (2 + hp)], sf0[hp] + 16 * (h // 2))
                            v.tensor_copy(CTG[0:65, hp * T:(hp + 1) * T], PS[0:65, :])
                            v.tensor_copy(SHb[0:1, hp * T:(hp + 1) * T],
                                          PS[64:65, :]).then_inc(sem["v"], 1)
                    cnt["v"] += 3 * H

                with nc.Block() as blk:
                    @blk.gpsimd
                    def _(g):
                        g.wait_ge(sem["lg"], cnt["lg"])
                        g.dma_start(out=grow_in.ap()[0:1, :],
                                    in_=PAY[0:8, :]).then_inc(sem["lg"], 16)
                        g.wait_ge(sem["lg"], cnt["lg"] + 16)
                        g.collective_compute(
                            "AllGather", ALU.bypass, replica_groups=RG2,
                            ins=[grow_in.ap().opt()], outs=[grow_out.ap().opt()],
                        ).then_inc(sem["cc"], 1)
                        g.wait_ge(sem["cc"], cnt["cc"] + 1)
                        g.dma_start(out=PAb[0:8, :], in_=grow_out.ap()[0:1, :]
                                    ).then_inc(sem["lg"], 16)
                        g.dma_start(out=PBb[0:8, :], in_=grow_out.ap()[1:2, :]
                                    ).then_inc(sem["lg"], 16)
                        g.wait_ge(sem["lg"], cnt["lg"] + 48)
                    cnt["lg"] += 48
                    cnt["cc"] += 1

        ctx45 = ExitStack()
        if nstage >= 5:
            E45 = ctx45.enter_context
            PAY = E45(nc.sbuf_tensor("PAY", [8, 16 * 66], F32))
            PAb = E45(nc.sbuf_tensor("PAb", [8, 16 * 66], F32))
            PBb = E45(nc.sbuf_tensor("PBb", [8, 16 * 66], F32))
            S128 = E45(nc.sbuf_tensor("S128", [128, 16 * 16], F32))
            l0_attn_passA("a0")

        def l0_attn_passB(ctxsrc_d, ctxtmp_d, ctf_dst, tag):
            with ExitStack() as c2:
                E2 = c2.enter_context
                SCT = E2(nc.sbuf_tensor(tag + "SCT", [8, T], BF16))
                CMB = E2(nc.sbuf_tensor(tag + "CMB", [8, 16 * 65], BF16))
                CM = E2(nc.sbuf_tensor(tag + "CM", [8, 6 * 16], F32))
                CTG2 = E2(nc.sbuf_tensor(tag + "CTG2", [128, 2 * T], BF16))
                SH2 = E2(nc.sbuf_tensor(tag + "SH2", [1, 2 * T], F32))
                SP128 = E2(nc.sbuf_tensor(tag + "SP128", [128, 256], F32))
                R128 = E2(nc.sbuf_tensor(tag + "R128", [128, 256], BF16))
                RBr = E2(nc.sbuf_tensor(tag + "RBr", [1, 2 * T], BF16))
                CTF1 = E2(nc.sbuf_tensor(tag + "CTF1", [128, 2 * T], BF16))
                PSC = E2(nc.psum_tensor(tag + "PSC", [128, T], F32))
                PRB = E2(nc.psum_tensor(tag + "PRB", [128, T], F32))

                with nc.Block() as blk:
                    @blk.sync
                    def _(sy):
                        sy.wait_ge(sem["ld"], cnt["ld"])
                        sy.dma_start(out=SCT[:, :], in_=DI["scat_s"].ap()).then_inc(sem["ld"], 16)
                    cnt["ld"] += 16
                    lds = cnt["ld"]

                    @blk.vector
                    def _(v):
                        PAv = PAb[0:8, :].rearrange("p (h w) -> p h w", h=16)[:, :, 65:66]
                        PBv = PBb[0:8, :].rearrange("p (h w) -> p h w", h=16)[:, :, 65:66]
                        v.tensor_tensor(CM[:, 0:16], PAv, PBv, ALU.max)
                        v.drain()
                        v.tensor_tensor(CM[:, 16:32], PAv, CM[:, 0:16], ALU.subtract)
                        v.drain()
                        v.tensor_tensor(CM[:, 32:48], PBv, CM[:, 0:16], ALU.subtract)
                        v.drain()
                        v.nop().then_inc(sem["v"], 1)
                    cnt["v"] += 1
                    vb1 = cnt["v"]

                    @blk.scalar
                    def _(s):
                        s.wait_ge(sem["v"], vb1)
                        s.activation(CM[:, 48:64], CM[:, 16:32], AF.Exp)
                        s.drain()
                        s.activation(CM[:, 64:80], CM[:, 32:48], AF.Exp)
                        s.drain()
                        s.nop().then_inc(sem["s"], 1)
                    cnt["s"] += 1
                    sb1 = cnt["s"]

                    @blk.vector
                    def _(v):
                        v.wait_ge(sem["s"], sb1)
                        for h in range(H):
                            v.tensor_scalar(CM[:, 80:96][:, 0:1].rearrange("p a -> p a")
                                            if False else CMB[0:8, h * 65:(h + 1) * 65],
                                            PAb[0:8, h * 66:h * 66 + 65],
                                            CM[:, 48 + h:49 + h], None, ALU.mult)
                            v.drain()
                            v.tensor_scalar(CTG2[0:8, 0:65], PBb[0:8, h * 66:h * 66 + 65],
                                            CM[:, 64 + h:65 + h], None, ALU.mult)
                            v.drain()
                            v.tensor_tensor(CMB[0:8, h * 65:(h + 1) * 65],
                                            CMB[0:8, h * 65:(h + 1) * 65],
                                            CTG2[0:8, 0:65], ALU.add)
                            v.drain()
                        v.nop().then_inc(sem["v"], 1)
                    cnt["v"] += 1

                t0 = cnt["t"]; v0 = cnt["v"]
                pc0 = {0: cnt["p0"], 1: cnt["p1"]}
                ps0 = {0: cnt["p2"], 1: cnt["p3"]}
                with nc.Block() as blk:
                    @blk.tensor
                    def _(t):
                        t.wait_ge(sem["ld"], lds)
                        for h in range(H):
                            if h >= 1:
                                t.wait_ge(sem["v"], v0 + 2 * h)
                            for g4 in range(NG):
                                mm = t.matmul(PSC[0:65, g4 * 512:(g4 + 1) * 512],
                                              CMB[0:8, h * 65:(h + 1) * 65],
                                              SCT[0:8, g4 * 512:(g4 + 1) * 512],
                                              start=True, stop=True)
                            mm.then_inc(sem["t"], 1)
                    cnt["t"] += H

                    @blk.sync
                    def _(sy):
                        for p_ in ["p0", "p1", "p2", "p3"]:
                            sy.wait_ge(sem[p_], cnt[p_])
                        for h in range(H):
                            hp = h % 2
                            sy.wait_ge(sem["p%d" % hp], pc0[hp] + 32 * (h // 2))
                            if h >= 2:
                                sy.wait_ge(sem["v"], v0 + 2 * (h - 2) + 1)
                            sy.dma_start(out=CTG2[0:65, hp * T:(hp + 1) * T],
                                         in_=ctxsrc_d[h * 65:(h + 1) * 65, :]
                                         ).then_inc(sem["p%d" % hp], 16)
                            sy.wait_ge(sem["v"], v0 + 2 * h + 1)
                            sy.dma_start(out=ctxtmp_d[h * 65:(h + 1) * 65, :],
                                         in_=CTG2[0:65, hp * T:(hp + 1) * T]
                                         ).then_inc(sem["p%d" % hp], 16)
                            sy.wait_ge(sem["p%d" % (2 + hp)], ps0[hp] + 16 * (h // 2))
                            sy.wait_ge(sem["v"], v0 + 2 * h + 2)
                            sy.dma_start(out=bass.AP(SP128, h * 16, [[256, 128], [1, 16]]),
                                         in_=SH2[0:1, hp * T:(hp + 1) * T]
                                         ).then_inc(sem["p%d" % (2 + hp)], 16)
                        sy.wait_ge(sem["p0"], pc0[0] + 32 * 8)
                        sy.wait_ge(sem["p1"], pc0[1] + 32 * 8)
                        sy.wait_ge(sem["p2"], ps0[0] + 16 * 8)
                        sy.wait_ge(sem["p3"], ps0[1] + 16 * 8)
                    for hh in range(H):
                        cnt["p%d" % (hh % 2)] += 32
                        cnt["p%d" % (2 + hh % 2)] += 16

                    @blk.vector
                    def _(v):
                        for h in range(H):
                            hp = h % 2
                            v.wait_ge(sem["t"], t0 + h + 1)
                            v.wait_ge(sem["p%d" % hp], pc0[hp] + 32 * (h // 2) + 16)
                            v.tensor_tensor(CTG2[0:65, hp * T:(hp + 1) * T],
                                            CTG2[0:65, hp * T:(hp + 1) * T],
                                            PSC[0:65, :], ALU.add).then_inc(sem["v"], 1)
                            if h >= 2:
                                v.wait_ge(sem["p%d" % (2 + hp)], ps0[hp] + 16 * (h // 2))
                            v.tensor_copy(SH2[0:1, hp * T:(hp + 1) * T],
                                          PSC[64:65, :]).then_inc(sem["v"], 1)
                    cnt["v"] += 2 * H

                with nc.Block() as blk:
                    @blk.vector
                    def _(v):
                        v.tensor_tensor(S128[:, :], S128[:, :], SP128[:, :], ALU.add)
                        v.drain()
                        v.tensor_scalar(S128[:, :], S128[:, :], 1e-30, None, ALU.max)
                        v.drain()
                        with nc.allow_low_precision(reason="bf16 normalizer broadcast"):
                            v.reciprocal(R128[:, :], S128[:, :])
                        v.drain()
                        v.nop().then_inc(sem["v"], 1)
                    cnt["v"] += 1

                t1 = cnt["t"]; v1 = cnt["v"]
                pr0 = {0: cnt["r0"], 1: cnt["r1"]}
                pf0 = {0: cnt["p0"], 1: cnt["p1"]}
                with nc.Block() as blk:
                    @blk.sync
                    def _(sy):
                        for p_ in ["r0", "r1", "p0", "p1"]:
                            sy.wait_ge(sem[p_], cnt[p_])
                        for h in range(H):
                            hp = h % 2
                            sy.wait_ge(sem["r%d" % hp], pr0[hp] + 16 * (h // 2))
                            if h >= 2:
                                sy.wait_ge(sem["t"], t1 + h - 1)
                            sy.dma_start(out=RBr[0:1, hp * T:(hp + 1) * T],
                                         in_=bass.AP(R128, h * 16, [[256, 128], [1, 16]])
                                         ).then_inc(sem["r%d" % hp], 16)
                            sy.wait_ge(sem["p%d" % hp], pf0[hp] + 32 * (h // 2))
                            if h >= 2:
                                sy.wait_ge(sem["v"], v1 + h - 1)
                            sy.dma_start(out=CTG2[0:65, hp * T:(hp + 1) * T],
                                         in_=ctxtmp_d[h * 65:(h + 1) * 65, :]
                                         ).then_inc(sem["p%d" % hp], 16)
                            sy.wait_ge(sem["v"], v1 + h + 1)
                            sy.dma_start(out=ctf_dst[h * 64:(h + 1) * 64, :],
                                         in_=CTF1[0:64, hp * T:(hp + 1) * T]
                                         ).then_inc(sem["p%d" % hp], 16)
                        sy.wait_ge(sem["r0"], pr0[0] + 16 * 8)
                        sy.wait_ge(sem["r1"], pr0[1] + 16 * 8)
                        sy.wait_ge(sem["p0"], pf0[0] + 32 * 8)
                        sy.wait_ge(sem["p1"], pf0[1] + 32 * 8)
                    for hh in range(H):
                        cnt["r%d" % (hh % 2)] += 16
                        cnt["p%d" % (hh % 2)] += 32

                    @blk.tensor
                    def _(t):
                        for h in range(H):
                            hp = h % 2
                            t.wait_ge(sem["r%d" % hp], pr0[hp] + 16 * (h // 2 + 1))
                            if h >= 1:
                                t.wait_ge(sem["v"], v1 + h)
                            for g4 in range(NG):
                                mm = t.matmul(PRB[:, g4 * 512:(g4 + 1) * 512], ONE1B[:, :],
                                              RBr[0:1, hp * T + g4 * 512:hp * T + (g4 + 1) * 512],
                                              start=True, stop=True)
                            mm.then_inc(sem["t"], 1)
                    cnt["t"] += H

                    @blk.vector
                    def _(v):
                        for h in range(H):
                            hp = h % 2
                            v.wait_ge(sem["t"], t1 + h + 1)
                            v.wait_ge(sem["p%d" % hp], pf0[hp] + 32 * (h // 2) + 16)
                            v.tensor_tensor(CTF1[0:64, hp * T:(hp + 1) * T],
                                            CTG2[0:64, hp * T:(hp + 1) * T],
                                            PRB[0:64, :], ALU.mult).then_inc(sem["v"], 1)
                    cnt["v"] += H

        if nstage >= 6:
            l0_attn_passB(ctx_d.ap(), ctxb_d.ap(), ctf_d.ap(), "b0")
        ctx45.close()

        if nstage >= 7:
            with ExitStack() as c5:
                CTFS = c5.enter_context(nc.sbuf_tensor("CTFS", [128, ND * T], BF16))
                with nc.Block() as blk:
                    @blk.sync
                    def _(sy):
                        sy.wait_ge(sem["ld"], cnt["ld"])
                        for c in range(ND):
                            sy.dma_start(out=CTFS[:, c * T:(c + 1) * T],
                                         in_=ctf_d.ap()[c * 128:(c + 1) * 128, :]
                                         ).then_inc(sem["ld"], 16)
                        cnt["ld"] += 16 * ND
                        sy.wait_ge(sem["ld"], cnt["ld"])
                proj_fm(DI["l0wo"].ap(), DI["l0bo"], x1_d.ap(), "p5o",
                        src_sb=CTFS, res_dram=DI["xf"].ap(), out_f32=True)


        woq_d = dint("woq_d", [FF, D], BF16)

        def quant_ffn(tag):
            with ExitStack() as c2:
                E2 = c2.enter_context
                HID = E2(nc.sbuf_tensor(tag + "HID", [128, NFF * T], BF16))
                TMP = E2(nc.sbuf_tensor(tag + "TMP", [128, T], F32))
                SCL = E2(nc.sbuf_tensor(tag + "SCL", [128, 4], F32))
                SC1 = E2(nc.sbuf_tensor(tag + "SC1", [1, 4], F32))
                WST = E2(nc.sbuf_tensor(tag + "WST", [128, 2 * D], F32))
                WQT = E2(nc.sbuf_tensor(tag + "WQT", [128, D], F32))
                WQB = E2(nc.sbuf_tensor(tag + "WQB", [128, D], BF16))
                WQ = E2(nc.sbuf_tensor(tag + "WQ", [128, 2 * D], BF16))
                QS = E2(nc.sbuf_tensor(tag + "QS", [128, 8], F32))
                QSA = E2(nc.sbuf_tensor(tag + "QSA", [128, 2 * NFF], F32))
                QSB = E2(nc.sbuf_tensor(tag + "QSB", [128, 2 * NFF], F32))
                BIN = E2(nc.sbuf_tensor(tag + "BIN", [128, NFF], F32))
                BOUT = E2(nc.sbuf_tensor(tag + "BOUT", [128, ND], F32))
                HAM = E2(nc.sbuf_tensor(tag + "HAM", [128, NFF + 2], F32))
                X1R = E2(nc.sbuf_tensor(tag + "X1R", [128, 2 * 512], F32))
                OT2 = E2(nc.sbuf_tensor(tag + "OT2", [128, 2 * 512], F32))
                PH = E2(nc.psum_tensor(tag + "PH", [128, T], F32))
                PT = E2(nc.psum_tensor(tag + "PT", [128, 2048], BF16))
                PO = E2(nc.psum_tensor(tag + "PO", [128, 512], F32))

                with nc.Block() as blk:
                    @blk.gpsimd
                    def _(g):
                        g.wait_ge(sem["cc"], cc_amax)
                        g.wait_ge(sem["lg"], cnt["lg"])
                        g.dma_start(out=SC1[0:1, 0:1],
                                    in_=amax_io[1].ap()[0:1, 0:1]).then_inc(sem["lg"], 16)
                        g.wait_ge(sem["lg"], cnt["lg"] + 16)
                    cnt["lg"] += 16
                    lgA = cnt["lg"]

                    @blk.vector
                    def _(v):
                        v.wait_ge(sem["lg"], lgA)
                        v.memset(SC1[0:1, 3:4], 0.0)
                        v.drain()
                        v.tensor_scalar(SC1[0:1, 0:1], SC1[0:1, 0:1], 1e-8, None, ALU.max)
                        v.drain()
                        v.reciprocal(SC1[0:1, 1:2], SC1[0:1, 0:1])
                        v.drain()
                        v.tensor_scalar(SC1[0:1, 1:2], SC1[0:1, 1:2], 32767.0, None, ALU.mult)
                        v.drain()
                        v.tensor_scalar(SC1[0:1, 2:3], SC1[0:1, 0:1], 1.0 / 32767.0, None, ALU.mult)
                        v.drain()
                        v.nop().then_inc(sem["v"], 1)
                    cnt["v"] += 1
                    vsc = cnt["v"]

                    @blk.tensor
                    def _(t):
                        t.wait_ge(sem["v"], vsc)
                        t.matmul(PO[:, 0:4], ONE1F[:, :], SC1[0:1, 0:4],
                                 start=True, stop=True).then_inc(sem["t"], 1)
                    cnt["t"] += 1
                    tsc = cnt["t"]

                    @blk.vector
                    def _(v):
                        v.wait_ge(sem["t"], tsc)
                        v.tensor_copy(SCL[:, :], PO[:, 0:4])
                        v.drain()
                        for c in range(ND):
                            v.tensor_scalar(TMP[:, :], XN[:, c * T:(c + 1) * T],
                                            SCL[:, 1:2], MAGIC, ALU.mult, ALU.add)
                            v.drain()
                            v.tensor_scalar(TMP[:, :], TMP[:, :], -MAGIC, SCL[:, 2:3],
                                            ALU.add, ALU.mult)
                            v.drain()
                            v.tensor_copy(XN[:, c * T:(c + 1) * T], TMP[:, :])
                            v.drain()
                        v.nop().then_inc(sem["v"], 1)
                    cnt["v"] += 1

                with nc.Block() as blk:
                    @blk.sync
                    def _(sy):
                        sy.wait_ge(sem["ld"], cnt["ld"])
                        sy.dma_start(out=BIN[:, :],
                                     in_=bass.AP(DI["l0bin"], 0, [[1, 128], [128, NFF]]),
                                     allow_slow_non_contiguous=True).then_inc(sem["ld"], 16)
                        sy.dma_start(out=BOUT[:, :],
                                     in_=bass.AP(DI["l0bout"], 0, [[1, 128], [128, ND]]),
                                     allow_slow_non_contiguous=True).then_inc(sem["ld"], 16)
                    cnt["ld"] += 32
                    ldb8 = cnt["ld"]
                    w0 = {0: cnt["r0"], 1: cnt["r1"]}
                    t80 = cnt["t"]; v80 = cnt["v"]; s80 = cnt["s"]

                    @blk.sync
                    def _(sy):
                        for p_ in ["r0", "r1"]:
                            sy.wait_ge(sem[p_], cnt[p_])
                        for fc in range(NFF):
                            fp = fc % 2
                            sy.wait_ge(sem["r%d" % fp], w0[fp] + 16 * (fc // 2))
                            if fc >= 2:
                                sy.wait_ge(sem["v"], v80 + 4 * (fc - 2) + 2)
                            sy.dma_start(out=WST[:, fp * D:(fp + 1) * D],
                                         in_=DI["l0winT"].ap()[fc * 128:(fc + 1) * 128, :]
                                         ).then_inc(sem["r%d" % fp], 16)
                        for p_ in [0, 1]:
                            sy.wait_ge(sem["r%d" % p_], w0[p_] + 16 * 16)
                    cnt["r0"] += 16 * 16
                    cnt["r1"] += 16 * 16

                    @blk.vector
                    def _(v):
                        for fc in range(NFF):
                            fp = fc % 2
                            v.wait_ge(sem["r%d" % fp], w0[fp] + 16 * (fc // 2 + 1))
                            v.tensor_reduce(QS[:, 0:1], WST[:, fp * D:(fp + 1) * D],
                                            AX.X, ALU.max, apply_absolute_value=True)
                            v.drain()
                            v.tensor_scalar(QS[:, 0:1], QS[:, 0:1], 1e-8, None, ALU.max)
                            v.drain()
                            v.reciprocal(QS[:, 1:2], QS[:, 0:1])
                            v.drain()
                            v.tensor_scalar(QS[:, 1:2], QS[:, 1:2], 127.0, None, ALU.mult)
                            v.drain()
                            v.tensor_scalar(QS[:, 2:3], QS[:, 0:1], 1.0 / 127.0,
                                            None, ALU.mult).then_inc(sem["v"], 1)
                            v.drain()
                            v.tensor_scalar(WQT[:, :], WST[:, fp * D:(fp + 1) * D],
                                            QS[:, 1:2], MAGIC, ALU.mult, ALU.add)
                            v.drain()
                            v.tensor_scalar(WQB[:, :], WQT[:, :], -MAGIC, QS[:, 2:3],
                                            ALU.add, ALU.mult).then_inc(sem["v"], 1)
                            tr_done = (t80 + 8 if fc == 0 else t80 + 16 if fc == 1
                                       else t80 + 10 * (fc - 2) + 26)
                            v.wait_ge(sem["t"], tr_done)
                            if fc >= 2:
                                v.wait_ge(sem["t"], t80 + 10 * (fc - 2) + 18)
                            v.tensor_copy(WQ[:, fp * D:(fp + 1) * D],
                                          PT[:, (fc % 2) * D:(fc % 2 + 1) * D])
                            v.drain()
                            v.nop().then_inc(sem["v"], 1)
                            v.nop().then_inc(sem["v"], 1)
                    cnt["v"] += 4 * NFF

                    @blk.tensor
                    def _(t):
                        def transp(fc):
                            t.wait_ge(sem["v"], v80 + 4 * fc + 2)
                            if fc >= 2:
                                t.wait_ge(sem["v"], v80 + 4 * (fc - 2) + 3)
                            for j in range(ND):
                                t.transpose(PT[:, (fc % 2) * D + j * 128:(fc % 2) * D + (j + 1) * 128],
                                            WQB[:, j * 128:(j + 1) * 128],
                                            IDB[:, :]).then_inc(sem["t"], 1)

                        transp(0)
                        transp(1)
                        for fc in range(NFF):
                            fp = fc % 2
                            t.wait_ge(sem["v"], v80 + 4 * fc + 4)
                            if fc >= 1:
                                t.wait_ge(sem["s"], s80 + fc)
                            for g4 in range(NG):
                                for j in range(ND):
                                    mm = t.matmul(PH[:, g4 * 512:(g4 + 1) * 512],
                                                  WQ[:, fp * D + j * 128:fp * D + (j + 1) * 128],
                                                  XN[:, j * T + g4 * 512:j * T + (g4 + 1) * 512],
                                                  start=(j == 0), stop=(j == ND - 1))
                            mm.then_inc(sem["t"], 2)
                            if fc + 2 < NFF:
                                transp(fc + 2)
                    cnt["t"] += 10 * NFF

                    @blk.scalar
                    def _(s):
                        s.wait_ge(sem["ld"], ldb8)
                        for fc in range(NFF):
                            s.wait_ge(sem["t"], min(t80 + 10 * fc + 18, t80 + 10 * NFF))
                            s.activation(HID[:, fc * T:(fc + 1) * T], PH[:, :], AF.Gelu,
                                         bias=BIN[:, fc:fc + 1],
                                         scale=1.0).then_inc(sem["s"], 1)
                    cnt["s"] += NFF


                with nc.Block() as blk:
                    @blk.vector
                    def _(v):
                        for fc in range(NFF):
                            v.tensor_reduce(HAM[:, fc:fc + 1], HID[:, fc * T:(fc + 1) * T],
                                            AX.X, ALU.max, apply_absolute_value=True)
                        v.drain()
                        v.tensor_reduce(HAM[:, NFF:NFF + 1], HAM[:, 0:NFF], AX.X, ALU.max)
                        v.drain()
                        v.nop().then_inc(sem["v"], 1)
                    cnt["v"] += 1
                    vh1 = cnt["v"]

                    @blk.tensor
                    def _(t):
                        t.wait_ge(sem["v"], vh1)
                        t.transpose(PO[0:1, 0:128], HAM[:, NFF:NFF + 1],
                                    IDF[:, :]).then_inc(sem["t"], 1)
                    cnt["t"] += 1
                    th1 = cnt["t"]

                    @blk.vector
                    def _(v):
                        v.wait_ge(sem["t"], th1)
                        v.tensor_reduce(SC1[0:1, 0:1], PO[0:1, 0:128], AX.X, ALU.max)
                        v.drain()
                        v.nop().then_inc(sem["v"], 1)
                    cnt["v"] += 1
                    vh2 = cnt["v"]

                    @blk.gpsimd
                    def _(g):
                        g.wait_ge(sem["v"], vh2)
                        g.wait_ge(sem["lg"], cnt["lg"])
                        g.dma_start(out=hamax_io[0].ap()[0:1, 0:1],
                                    in_=SC1[0:1, 0:1]).then_inc(sem["lg"], 16)
                        g.wait_ge(sem["lg"], cnt["lg"] + 16)
                        g.collective_compute(
                            "AllReduce", ALU.max, replica_groups=RG8,
                            ins=[hamax_io[0].ap()[0:1, 0:1].opt()],
                            outs=[hamax_io[1].ap()[0:1, 0:1].opt()],
                        ).then_inc(sem["cc"], 1)
                        g.wait_ge(sem["cc"], cnt["cc"] + 1)
                        g.dma_start(out=SC1[0:1, 0:1],
                                    in_=hamax_io[1].ap()[0:1, 0:1]).then_inc(sem["lg"], 32 - 16)
                        g.wait_ge(sem["lg"], cnt["lg"] + 32)
                    cnt["lg"] += 32
                    cnt["cc"] += 1

                    @blk.vector
                    def _(v):
                        v.wait_ge(sem["lg"], cnt["lg"])
                        v.tensor_scalar(SC1[0:1, 0:1], SC1[0:1, 0:1], 1e-8, None, ALU.max)
                        v.drain()
                        v.reciprocal(SC1[0:1, 1:2], SC1[0:1, 0:1])
                        v.drain()
                        v.tensor_scalar(SC1[0:1, 1:2], SC1[0:1, 1:2], 32767.0, None, ALU.mult)
                        v.drain()
                        v.tensor_scalar(SC1[0:1, 2:3], SC1[0:1, 0:1], 1.0 / 32767.0, None, ALU.mult)
                        v.drain()
                        v.nop().then_inc(sem["v"], 1)
                    cnt["v"] += 1
                    vh3 = cnt["v"]

                    @blk.tensor
                    def _(t):
                        t.wait_ge(sem["v"], vh3)
                        t.matmul(PO[:, 0:4], ONE1F[:, :], SC1[0:1, 0:4],
                                 start=True, stop=True).then_inc(sem["t"], 1)
                    cnt["t"] += 1
                    th2 = cnt["t"]

                    @blk.vector
                    def _(v):
                        v.wait_ge(sem["t"], th2)
                        v.tensor_copy(SCL[:, :], PO[:, 0:4])
                        v.drain()
                        for fc in range(NFF):
                            v.tensor_scalar(TMP[:, :], HID[:, fc * T:(fc + 1) * T],
                                            SCL[:, 1:2], MAGIC, ALU.mult, ALU.add)
                            v.drain()
                            v.tensor_scalar(HID[:, fc * T:(fc + 1) * T], TMP[:, :],
                                            -MAGIC, SCL[:, 2:3], ALU.add, ALU.mult)
                            v.drain()
                        v.nop().then_inc(sem["v"], 1)
                    cnt["v"] += 1

                with nc.Block() as blk:
                    wo0 = {0: cnt["r0"], 1: cnt["r1"]}
                    wq0 = {0: cnt["p0"], 1: cnt["p1"]}
                    v81 = cnt["v"]

                    @blk.scalar
                    def _(s):
                        for p_ in ["r0", "r1"]:
                            s.wait_ge(sem[p_], cnt[p_])
                        for fc in range(NFF):
                            fp = fc % 2
                            s.wait_ge(sem["r%d" % fp], wo0[fp] + 16 * (fc // 2))
                            if fc >= 2:
                                s.wait_ge(sem["v"], v81 + (fc - 2) + 1)
                            s.dma_start(out=WST[:, fp * D:(fp + 1) * D],
                                        in_=DI["l0wout"].ap()[fc * 128:(fc + 1) * 128, :]
                                        ).then_inc(sem["r%d" % fp], 16)
                        for p_ in [0, 1]:
                            s.wait_ge(sem["r%d" % p_], wo0[p_] + 16 * 16)
                    cnt["r0"] += 16 * 16
                    cnt["r1"] += 16 * 16

                    @blk.vector
                    def _(v):
                        for fc in range(NFF):
                            fp = fc % 2
                            v.wait_ge(sem["r%d" % fp], wo0[fp] + 16 * (fc // 2 + 1))
                            if fc >= 2:
                                v.wait_ge(sem["p%d" % fp], wq0[fp] + 16 * (fc // 2))
                            v.tensor_reduce(QS[:, 0:1], WST[:, fp * D:(fp + 1) * D],
                                            AX.X, ALU.max, apply_absolute_value=True)
                            v.drain()
                            v.tensor_scalar(QS[:, 0:1], QS[:, 0:1], 1e-8, None, ALU.max)
                            v.drain()
                            v.reciprocal(QS[:, 1:2], QS[:, 0:1])
                            v.drain()
                            v.tensor_scalar(QS[:, 1:2], QS[:, 1:2], 127.0, None, ALU.mult)
                            v.drain()
                            v.tensor_scalar(QS[:, 2:3], QS[:, 0:1], 1.0 / 127.0, None, ALU.mult)
                            v.drain()
                            v.tensor_scalar(WQT[:, :], WST[:, fp * D:(fp + 1) * D],
                                            QS[:, 1:2], MAGIC, ALU.mult, ALU.add)
                            v.drain()
                            v.tensor_scalar(WQ[:, fp * D:(fp + 1) * D], WQT[:, :],
                                            -MAGIC, QS[:, 2:3], ALU.add,
                                            ALU.mult).then_inc(sem["v"], 1)
                    cnt["v"] += NFF

                    @blk.sync
                    def _(sy):
                        for p_ in ["p0", "p1"]:
                            sy.wait_ge(sem[p_], cnt[p_])
                        for fc in range(NFF):
                            fp = fc % 2
                            sy.wait_ge(sem["v"], v81 + fc + 1)
                            sy.dma_start(out=woq_d.ap()[fc * 128:(fc + 1) * 128, :],
                                         in_=WQ[:, fp * D:(fp + 1) * D]
                                         ).then_inc(sem["p%d" % fp], 16)
                        for p_ in [0, 1]:
                            sy.wait_ge(sem["p%d" % p_], wq0[p_] + 16 * 16)
                    cnt["p0"] += 16 * 16
                    cnt["p1"] += 16 * 16

                with nc.Block() as blk:
                    wo1 = {0: cnt["r0"], 1: cnt["r1"]}
                    x10 = {0: cnt["k0"], 1: cnt["k1"]}
                    o80 = {0: cnt["p2"], 1: cnt["p3"]}
                    t82 = cnt["t"]; v82 = cnt["v"]

                    @blk.scalar
                    def _(s):
                        for p_ in ["r0", "r1"]:
                            s.wait_ge(sem[p_], cnt[p_])
                        for i in range(NG * 2 * NFF):
                            fc = i % NFF
                            fp = i % 2
                            s.wait_ge(sem["r%d" % fp], wo1[fp] + 16 * (i // 2))
                            if i >= 2:
                                s.wait_ge(sem["t"], t82 + (i - 2) + 1)
                            s.dma_start(out=WQ[:, fp * D:(fp + 1) * D],
                                        in_=woq_d.ap()[fc * 128:(fc + 1) * 128, :]
                                        ).then_inc(sem["r%d" % fp], 16)
                        for p_ in [0, 1]:
                            s.wait_ge(sem["r%d" % p_], wo1[p_] + 16 * NG * NFF)
                    cnt["r0"] += 16 * NG * NFF
                    cnt["r1"] += 16 * NG * NFF

                    @blk.sync
                    def _(sy):
                        for p_ in ["k0", "k1", "p2", "p3"]:
                            sy.wait_ge(sem[p_], cnt[p_])
                        for i in range(NG * ND):
                            g4, o = divmod(i, ND)
                            fp = i % 2
                            if i >= 2:
                                j = i - 2
                                jg4, jo = divmod(j, ND)
                                jfp = j % 2
                                sy.wait_ge(sem["v"], v82 + j + 1)
                                sy.wait_ge(sem["p%d" % (2 + jfp)], o80[jfp] + 16 * (j // 2))
                                sy.dma_start(out=x2_d.ap()[jo * 128:(jo + 1) * 128,
                                                           jg4 * 512:(jg4 + 1) * 512],
                                             in_=OT2[:, jfp * 512:(jfp + 1) * 512]
                                             ).then_inc(sem["p%d" % (2 + jfp)], 16)
                            sy.wait_ge(sem["k%d" % fp], x10[fp] + 16 * (i // 2))
                            sy.dma_start(out=X1R[:, fp * 512:(fp + 1) * 512],
                                         in_=x1_d.ap()[o * 128:(o + 1) * 128,
                                                       g4 * 512:(g4 + 1) * 512]
                                         ).then_inc(sem["k%d" % fp], 16)
                        for j in range(NG * ND - 2, NG * ND):
                            jg4, jo = divmod(j, ND)
                            jfp = j % 2
                            sy.wait_ge(sem["v"], v82 + j + 1)
                            sy.wait_ge(sem["p%d" % (2 + jfp)], o80[jfp] + 16 * (j // 2))
                            sy.dma_start(out=x2_d.ap()[jo * 128:(jo + 1) * 128,
                                                       jg4 * 512:(jg4 + 1) * 512],
                                         in_=OT2[:, jfp * 512:(jfp + 1) * 512]
                                         ).then_inc(sem["p%d" % (2 + jfp)], 16)
                        for p_ in [0, 1]:
                            sy.wait_ge(sem["k%d" % p_], x10[p_] + 16 * NG * ND // 2)
                            sy.wait_ge(sem["p%d" % (2 + p_)], o80[p_] + 16 * NG * ND // 2)
                    cnt["k0"] += 16 * NG * ND // 2
                    cnt["k1"] += 16 * NG * ND // 2
                    cnt["p2"] += 16 * NG * ND // 2
                    cnt["p3"] += 16 * NG * ND // 2

                    @blk.tensor
                    def _(t):
                        for gg in range(NG * 2):
                            g4, oh = divmod(gg, 2)
                            if gg >= 1:
                                t.wait_ge(sem["v"], v82 + 4 * gg)
                            for fc in range(NFF):
                                i = gg * NFF + fc
                                fp = i % 2
                                t.wait_ge(sem["r%d" % fp], wo1[fp] + 16 * (i // 2 + 1))
                                for oo in range(4):
                                    o = oh * 4 + oo
                                    mm = t.matmul(PH[:, oo * 512:(oo + 1) * 512],
                                                  WQ[:, fp * D + o * 128:fp * D + (o + 1) * 128],
                                                  HID[:, fc * T + g4 * 512:fc * T + (g4 + 1) * 512],
                                                  start=(fc == 0), stop=(fc == NFF - 1))
                                mm.then_inc(sem["t"], 1)
                    cnt["t"] += NG * 2 * NFF

                    @blk.vector
                    def _(v):
                        for gg in range(NG * 2):
                            g4, oh = divmod(gg, 2)
                            v.wait_ge(sem["t"], t82 + (gg + 1) * NFF)
                            for oo in range(4):
                                o = oh * 4 + oo
                                i = gg * 4 + oo
                                fp = i % 2
                                v.wait_ge(sem["k%d" % fp], x10[fp] + 16 * (i // 2 + 1))
                                if i >= 2:
                                    v.wait_ge(sem["p%d" % (2 + fp)], o80[fp] + 16 * (i // 2))
                                v.scalar_tensor_tensor(OT2[:, fp * 512:(fp + 1) * 512],
                                                        PH[:, oo * 512:(oo + 1) * 512],
                                                        BOUT[:, o:o + 1],
                                                        X1R[:, fp * 512:(fp + 1) * 512],
                                                        ALU.add, ALU.add).then_inc(sem["v"], 1)
                    cnt["v"] += NG * 2 * 4



 
        if nstage >= 8:
            quant_ffn("f0")

        if nstage >= 9:
            layernorm(x2_d.ap(), "ln2", None)

        k1h_d = dint("k1h_d", [D, T], BF16)
        v1h_d = dint("v1h_d", [NT * 128, VW], BF16)
        k1f_d = dint("k1f_d", [2, D * T], BF16)
        v1f_d = dint("v1f_d", [2, NT * 128 * VW], BF16)
        ctf2_d = dint("ctf2_d", [D, T], BF16)

        if nstage >= 10:
            proj_fm(DI["l1wq"].ap(), DI["l1bq"], q1_d.ap(), "p9q")
            proj_fm(DI["l1wk"].ap(), DI["l1bk"], k1h_d.ap(), "p9k")
            proj_tm(DI["l1wv"].ap(), DI["l1bv"], v1h_d.ap(), "p9v")
            with nc.Block() as blk:
                @blk.gpsimd
                def _(g):
                    g.wait_ge(sem["lg"], cnt["lg"])
                    g.collective_compute(
                        "AllGather", ALU.bypass, replica_groups=RG2,
                        ins=[bass.AP(k1h_d, 0, [[D * T, 1], [1, D * T]]).opt()],
                        outs=[k1f_d.ap().opt()],
                    ).then_inc(sem["cc"], 1)
                    g.collective_compute(
                        "AllGather", ALU.bypass, replica_groups=RG2,
                        ins=[bass.AP(v1h_d, 0, [[NT * 128 * VW, 1], [1, NT * 128 * VW]]).opt()],
                        outs=[v1f_d.ap().opt()],
                    ).then_inc(sem["cc"], 1)
                cnt["cc"] += 2
            cc_kv = cnt["cc"]

        GT = E_(nc.sbuf_tensor("GT", [8, T], BF16)) if nstage >= 10 else None
        NM2 = E_(nc.sbuf_tensor("NM2", [128, 16], F32)) if nstage >= 10 else None

        def moe_route(tag):
            with ExitStack() as c2:
                E2 = c2.enter_context
                WRT = E2(nc.sbuf_tensor(tag + "WRT", [128, 64], BF16))
                IOE = E2(nc.sbuf_tensor(tag + "IOE", [128, NE], F32))
                RVT = E2(nc.sbuf_tensor(tag + "RVT", [128, NE], F32))
                LGT = E2(nc.sbuf_tensor(tag + "LGT", [128, 128], F32))
                OH1 = E2(nc.sbuf_tensor(tag + "OH1", [128, 128], F32))
                OH2 = E2(nc.sbuf_tensor(tag + "OH2", [128, 128], F32))
                MV = E2(nc.sbuf_tensor(tag + "MV", [128, 16], F32))
                E16 = E2(nc.sbuf_tensor(tag + "E16", [128, 48], F32))
                SC2 = E2(nc.sbuf_tensor(tag + "SC2", [128, 8], F32))
                SCR = E2(nc.sbuf_tensor(tag + "SCR", [128, 8], F32))
                OH1T = E2(nc.sbuf_tensor(tag + "OH1T", [8, T], F32))
                OH2T = E2(nc.sbuf_tensor(tag + "OH2T", [8, T], F32))
                SINC = E2(nc.sbuf_tensor(tag + "SINC", [8, T], F32))
                KF8 = E2(nc.sbuf_tensor(tag + "KF8", [8, T], F32))
                KFTS = E2(nc.sbuf_tensor(tag + "KFTS", [128, 8], F32))
                GTT = E2(nc.sbuf_tensor(tag + "GTT", [128, 8], F32))
                TRS = E2(nc.sbuf_tensor(tag + "TRS", [8, 2], F32))
                CP = E2(nc.sbuf_tensor(tag + "CP", [1, 24], F32))
                QN16 = E2(nc.sbuf_tensor(tag + "QN16", [1, 16], F32))
                M16 = E2(nc.sbuf_tensor(tag + "M16", [1, 16], F32))
                CB0 = E2(nc.sbuf_tensor(tag + "CB0", [1, 24], F32))
                CB1 = E2(nc.sbuf_tensor(tag + "CB1", [1, 24], F32))
                BASE = E2(nc.sbuf_tensor(tag + "BASE", [8, 2], F32))
                ODD = E2(nc.sbuf_tensor(tag + "ODD", [1, 1], F32))
                KCH = E2(nc.sbuf_tensor(tag + "KCH", [128, 2 * T], BF16))
                KSQ = E2(nc.sbuf_tensor(tag + "KSQ", [128, T], BF16))
                N2 = E2(nc.sbuf_tensor(tag + "N2", [2, 64], F32))
                PL = E2(nc.psum_tensor(tag + "PL", [128, 512], F32))
                PT8 = E2(nc.psum_tensor(tag + "PT8", [128, 1024], F32))
                PN = E2(nc.psum_tensor(tag + "PN", [2, T], F32))

                with nc.Block() as blk:
                    @blk.gpsimd
                    def _(g):
                        g.wait_ge(sem["lg"], cnt["lg"])
                        for c in range(ND):
                            g.dma_start(out=WRT[:, c * 8:(c + 1) * 8],
                                        in_=DI["l1wrT"].ap()[c * 128:(c + 1) * 128, :]
                                        ).then_inc(sem["lg"], 16)
                    cnt["lg"] += 16 * ND
                    lgr = cnt["lg"]

                    @blk.sync
                    def _(sy):
                        sy.wait_ge(sem["ld"], cnt["ld"])
                        sy.dma_start(out=IOE[:, :], in_=DI["iota_e"].ap()).then_inc(sem["ld"], 16)
                        sy.dma_start(out=RVT[:, :], in_=DI["revta"].ap()).then_inc(sem["ld"], 16)
                        sy.dma_start(out=ODD[:, :], in_=DI["odd1"].ap()).then_inc(sem["ld"], 16)
                    cnt["ld"] += 48
                    ldio = cnt["ld"]

                    tl0 = cnt["t"]

                    @blk.tensor
                    def _(t):
                        t.wait_ge(sem["lg"], lgr)
                        for tt in range(NT):
                            for c in range(ND):
                                mm = t.matmul(PL[:, (tt % 16) * 8:(tt % 16) * 8 + 8],
                                              XN[:, c * T + tt * 128:c * T + (tt + 1) * 128],
                                              WRT[:, c * 8:(c + 1) * 8],
                                              start=(c == 0), stop=(c == ND - 1))
                            mm.then_inc(sem["t"], 1)
                    cnt["t"] += NT

                    @blk.vector
                    def _(v):
                        v.wait_ge(sem["ld"], ldio)
                        v.wait_ge(sem["t"], tl0 + NT)
                        v.tensor_copy(LGT[:, :], PL[:, 0:128])
                        v.drain()
                        v.nop().then_inc(sem["v"], 1)
                    cnt["v"] += 1

                with nc.Block() as blk:
                    @blk.vector
                    def _(v):
                        for tt in range(NT):
                            L = LGT[:, tt * 8:(tt + 1) * 8]
                            v.tensor_reduce(MV[:, 0:1], L, AX.X, ALU.max)
                            v.drain()
                            v.tensor_scalar(SC2[:, :], L, MV[:, 0:1], None, ALU.is_ge)
                            v.drain()
                            v.tensor_tensor(SC2[:, :], SC2[:, :], RVT[:, :], ALU.mult)
                            v.drain()
                            v.tensor_reduce(MV[:, 1:2], SC2[:, :], AX.X, ALU.max)
                            v.drain()
                            v.tensor_scalar(OH1[:, tt * 8:(tt + 1) * 8], SC2[:, :],
                                            MV[:, 1:2], None, ALU.is_ge)
                            v.drain()
                            v.scalar_tensor_tensor(SCR[:, :], OH1[:, tt * 8:(tt + 1) * 8],
                                                   -BIG, L, ALU.mult, ALU.add)
                            v.drain()
                            v.tensor_reduce(MV[:, 2:3], SCR[:, :], AX.X, ALU.max)
                            v.drain()
                            v.tensor_scalar(SC2[:, :], SCR[:, :], MV[:, 2:3], None, ALU.is_ge)
                            v.drain()
                            v.tensor_tensor(SC2[:, :], SC2[:, :], RVT[:, :], ALU.mult)
                            v.drain()
                            v.tensor_reduce(MV[:, 3:4], SC2[:, :], AX.X, ALU.max)
                            v.drain()
                            v.tensor_scalar(OH2[:, tt * 8:(tt + 1) * 8], SC2[:, :],
                                            MV[:, 3:4], None, ALU.is_ge)
                            v.drain()
                            v.tensor_tensor(E16[:, tt:tt + 1], MV[:, 2:3], MV[:, 0:1],
                                            ALU.subtract)
                            v.drain()
                        v.nop().then_inc(sem["v"], 1)
                    cnt["v"] += 1
                    vt1 = cnt["v"]

                    @blk.scalar
                    def _(s):
                        s.wait_ge(sem["v"], vt1)
                        s.activation(E16[:, 16:32], E16[:, 0:16], AF.Exp)
                        s.drain()
                        s.nop().then_inc(sem["s"], 1)
                    cnt["s"] += 1
                    st1 = cnt["s"]

                    @blk.vector
                    def _(v):
                        v.wait_ge(sem["s"], st1)
                        v.tensor_scalar(E16[:, 32:48], E16[:, 16:32], 1.0, None, ALU.add)
                        v.drain()
                        v.reciprocal(E16[:, 0:16], E16[:, 32:48])
                        v.drain()
                        v.tensor_tensor(E16[:, 16:32], E16[:, 16:32], E16[:, 0:16], ALU.mult)
                        v.drain()
                        v.nop().then_inc(sem["v"], 1)
                    cnt["v"] += 1
                    vt2 = cnt["v"]

                with nc.Block() as blk:
                    to0 = cnt["t"]

                    @blk.tensor
                    def _(t):
                        t.wait_ge(sem["v"], vt2)
                        for tt in range(NT):
                            if tt >= 2:
                                t.wait_ge(sem["v"], vt2 + 2 * (tt - 1))
                            t.transpose(PT8[0:8, (tt % 2) * 512:(tt % 2) * 512 + 128],
                                        OH1[:, tt * 8:(tt + 1) * 8],
                                        IDF[:, :]).then_inc(sem["t"], 1)
                            t.transpose(PT8[0:8, (tt % 2) * 512 + 128:(tt % 2) * 512 + 256],
                                        OH2[:, tt * 8:(tt + 1) * 8],
                                        IDF[:, :]).then_inc(sem["t"], 1)
                    cnt["t"] += 2 * NT

                    @blk.vector
                    def _(v):
                        for tt in range(NT):
                            v.wait_ge(sem["t"], to0 + 2 * tt + 2)
                            v.tensor_copy(OH1T[0:8, tt * 128:(tt + 1) * 128],
                                          PT8[0:8, (tt % 2) * 512:(tt % 2) * 512 + 128]
                                          ).then_inc(sem["v"], 1)
                            v.tensor_copy(OH2T[0:8, tt * 128:(tt + 1) * 128],
                                          PT8[0:8, (tt % 2) * 512 + 128:(tt % 2) * 512 + 256]
                                          ).then_inc(sem["v"], 1)
                        v.drain()
                        v.tensor_tensor_scan(SINC[:, :], OH1T[:, :], OH2T[:, :], 0.0,
                                             ALU.add, ALU.add)
                        v.drain()
                        v.nop().then_inc(sem["v"], 1)
                    cnt["v"] += 2 * NT + 1
                    vs1 = cnt["v"]

                    @blk.tensor
                    def _(t):
                        t.wait_ge(sem["v"], vs1)
                        t.transpose(PT8[0:1, 0:8], SINC[0:8, T - 1:T],
                                    IDF[0:8, 0:8]).then_inc(sem["t"], 1)
                    cnt["t"] += 1
                    tc1 = cnt["t"]

                    @blk.vector
                    def _(v):
                        v.wait_ge(sem["t"], tc1)
                        v.tensor_copy(CP[0:1, 0:8], PT8[0:1, 0:8])
                        v.drain()
                        v.nop().then_inc(sem["v"], 1)
                    cnt["v"] += 1

                def norms2(src_dram, col0):
                    with nc.Block() as blk:
                        ks0 = cnt["ld"]
                        v0_ = cnt["v"]; t0_ = cnt["t"]

                        @blk.sync
                        def _(sy):
                            sy.wait_ge(sem["ld"], cnt["ld"])
                            for c in range(ND):
                                if c >= 2:
                                    sy.wait_ge(sem["v"], v0_ + 2 * (c - 2) + 1)
                                sy.dma_start(out=KCH[:, (c % 2) * T:(c % 2 + 1) * T],
                                             in_=src_dram[c * 128:(c + 1) * 128, :]
                                             ).then_inc(sem["ld"], 16)
                                sy.wait_ge(sem["ld"], ks0 + 16 * (c + 1))
                        cnt["ld"] += 16 * ND

                        @blk.vector
                        def _(v):
                            for c in range(ND):
                                p = c % 2
                                v.wait_ge(sem["ld"], ks0 + 16 * (c + 1))
                                if c >= 1:
                                    v.wait_ge(sem["t"], t0_ + c)
                                v.tensor_tensor(KSQ[:, :], KCH[:, p * T:(p + 1) * T],
                                                KCH[:, p * T:(p + 1) * T], ALU.mult)
                                v.drain()
                                v.nop().then_inc(sem["v"], 1)
                                v.wait_ge(sem["t"], t0_ + c + 1)
                                v.tensor_reduce(N2[0:2, col0 + c:col0 + c + 1], PN[0:2, :],
                                                AX.X, ALU.max)
                                v.drain()
                                v.nop().then_inc(sem["v"], 1)
                        cnt["v"] += 2 * ND

                        @blk.tensor
                        def _(t):
                            for c in range(ND):
                                t.wait_ge(sem["v"], v0_ + 2 * c + 1)
                                for g4 in range(NG):
                                    mm = t.matmul(PN[0:2, g4 * 512:(g4 + 1) * 512],
                                                  ONEPAIR[:, :],
                                                  KSQ[:, g4 * 512:(g4 + 1) * 512],
                                                  start=True, stop=True)
                                mm.then_inc(sem["t"], 1)
                        cnt["t"] += ND

                norms2(k1h_d.ap(), 0)
                norms2(q1_d.ap(), 8)

                with nc.Block() as blk:
                    @blk.tensor
                    def _(t):
                        t.transpose(PT8[0:8, 0:2], N2[0:2, 0:8], IDF[0:2, 0:2]).then_inc(sem["t"], 1)
                        t.transpose(PT8[0:8, 512:514], N2[0:2, 8:16],
                                    IDF[0:2, 0:2]).then_inc(sem["t"], 1)
                    cnt["t"] += 2
                    tf1 = cnt["t"]

                    @blk.vector
                    def _(v):
                        v.wait_ge(sem["t"], tf1)
                        v.tensor_copy(TRS[0:8, 0:2], PT8[0:8, 0:2])
                        v.drain()
                        v.nop().then_inc(sem["v"], 1)
                    cnt["v"] += 1
                    vf1 = cnt["v"]

                    @blk.sync
                    def _(sy):
                        sy.wait_ge(sem["v"], vf1)
                        sy.wait_ge(sem["ld"], cnt["ld"])
                        sy.dma_start(out=CP[0:1, 8:24], in_=TRS[0:8, 0:2]).then_inc(sem["ld"], 16)
                    cnt["ld"] += 16
                    ldf1 = cnt["ld"]

                    @blk.vector
                    def _(v):
                        v.wait_ge(sem["ld"], ldf1)
                        v.tensor_copy(TRS[0:8, 0:2], PT8[0:8, 512:514])
                        v.drain()
                        v.nop().then_inc(sem["v"], 1)
                    cnt["v"] += 1
                    vf2 = cnt["v"]

                    @blk.sync
                    def _(sy):
                        sy.wait_ge(sem["v"], vf2)
                        sy.dma_start(out=QN16[0:1, 0:16], in_=TRS[0:8, 0:2]).then_inc(sem["ld"], 16)
                        sy.wait_ge(sem["ld"], ldf1 + 16)
                    cnt["ld"] += 16

                    @blk.gpsimd
                    def _(g):
                        g.wait_ge(sem["ld"], cnt["ld"])
                        g.wait_ge(sem["lg"], cnt["lg"])
                        g.dma_start(out=cnt_in.ap()[0:1, :], in_=CP[0:1, :]).then_inc(sem["lg"], 16)
                        g.wait_ge(sem["lg"], cnt["lg"] + 16)
                        g.collective_compute(
                            "AllGather", ALU.bypass, replica_groups=RG2,
                            ins=[cnt_in.ap().opt()], outs=[cnt_out.ap().opt()],
                        ).then_inc(sem["cc"], 1)
                        g.wait_ge(sem["cc"], cnt["cc"] + 1)
                        g.dma_start(out=CB0[0:1, :], in_=cnt_out.ap()[0:1, :]).then_inc(sem["lg"], 16)
                        g.dma_start(out=CB1[0:1, :], in_=cnt_out.ap()[1:2, :]).then_inc(sem["lg"], 16)
                        g.wait_ge(sem["lg"], cnt["lg"] + 48)
                    cnt["lg"] += 48
                    cnt["cc"] += 1
                    lgc1 = cnt["lg"]

                    @blk.vector
                    def _(v):
                        v.wait_ge(sem["lg"], lgc1)
                        v.tensor_tensor(M16[0:1, :], CB0[0:1, 8:24], CB1[0:1, 8:24], ALU.max)
                        v.drain()
                        v.tensor_tensor(M16[0:1, :], M16[0:1, :], QN16[0:1, :], ALU.mult)
                        v.drain()
                        v.nop().then_inc(sem["v"], 1)
                    cnt["v"] += 1
                    vm1 = cnt["v"]

                    @blk.scalar
                    def _(s):
                        s.wait_ge(sem["v"], vm1)
                        s.activation(M16[0:1, :], M16[0:1, :], AF.Sqrt)
                        s.drain()
                        s.mul(M16[0:1, :], M16[0:1, :], -0.125)
                        s.drain()
                        s.nop().then_inc(sem["s"], 1)
                    cnt["s"] += 1
                    sm1 = cnt["s"]

                    @blk.tensor
                    def _(t):
                        t.wait_ge(sem["s"], sm1)
                        t.matmul(PT8[:, 0:16], ONE1F[:, :], M16[0:1, :],
                                 start=True, stop=True).then_inc(sem["t"], 1)
                        t.wait_ge(sem["v"], cnt["v"] + 1)
                        t.transpose(PT8[0:8, 512:513], CP[0:1, 0:8],
                                    IDF[0:1, 0:1]).then_inc(sem["t"], 1)
                    cnt["t"] += 2
                    tm1 = cnt["t"]

                    @blk.vector
                    def _(v):
                        v.wait_ge(sem["lg"], lgc1)
                        v.tensor_scalar(CP[0:1, 0:8], CB0[0:1, 0:8], ODD[0:1, 0:1],
                                        None, ALU.mult)
                        v.drain()
                        v.nop().then_inc(sem["v"], 1)
                        v.wait_ge(sem["t"], tm1)
                        v.tensor_copy(NM2[:, :], PT8[:, 0:16])
                        v.drain()
                        v.tensor_scalar(BASE[0:8, 0:1], PT8[0:8, 512:513], -1538.0,
                                        None, ALU.add)
                        v.drain()
                        v.nop().then_inc(sem["v"], 1)
                    cnt["v"] += 2
                    vb2 = cnt["v"]

                with nc.Block() as blk:
                    @blk.vector
                    def _(v):
                        v.tensor_scalar(KF8[:, :], SINC[:, :], BASE[0:8, 0:1], None, ALU.add)
                        v.drain()
                        v.tensor_scalar(KF8[:, :], KF8[:, :], 0.0, None, ALU.is_lt)
                        v.drain()
                        v.nop().then_inc(sem["v"], 1)
                    cnt["v"] += 1
                    vk1 = cnt["v"]
                    tg0 = cnt["t"]

                    @blk.tensor
                    def _(t):
                        t.wait_ge(sem["v"], vk1)
                        for tt in range(NT):
                            if tt >= 1:
                                t.wait_ge(sem["v"], vk1 + 2 * tt)
                            t.transpose(PT8[:, (tt % 2) * 512:(tt % 2) * 512 + 8],
                                        KF8[0:8, tt * 128:(tt + 1) * 128],
                                        IDF[0:8, 0:8]).then_inc(sem["t"], 1)
                            t.wait_ge(sem["v"], vk1 + 2 * tt + 1)
                            t.transpose(PT8[0:8, (tt % 2) * 512 + 128:(tt % 2) * 512 + 256],
                                        GTT[:, :], IDF[:, :]).then_inc(sem["t"], 1)
                    cnt["t"] += 2 * NT

                    @blk.vector
                    def _(v):
                        for tt in range(NT):
                            v.wait_ge(sem["t"], tg0 + 2 * tt + 1)
                            v.tensor_copy(KFTS[:, :], PT8[:, (tt % 2) * 512:(tt % 2) * 512 + 8])
                            v.drain()
                            v.tensor_tensor(SC2[:, :], OH1[:, tt * 8:(tt + 1) * 8],
                                            KFTS[:, :], ALU.mult)
                            v.drain()
                            v.tensor_scalar(SC2[:, :], SC2[:, :], E16[:, tt:tt + 1],
                                            None, ALU.mult)
                            v.drain()
                            v.tensor_tensor(SCR[:, :], OH2[:, tt * 8:(tt + 1) * 8],
                                            KFTS[:, :], ALU.mult)
                            v.drain()
                            v.tensor_scalar(SCR[:, :], SCR[:, :], E16[:, 16 + tt:17 + tt],
                                            None, ALU.mult)
                            v.drain()
                            v.tensor_tensor(GTT[:, :], SC2[:, :], SCR[:, :], ALU.add)
                            v.drain()
                            v.nop().then_inc(sem["v"], 1)
                            v.wait_ge(sem["t"], tg0 + 2 * tt + 2)
                            with nc.allow_low_precision(reason="bf16 gate broadcast"):
                                v.tensor_copy(GT[0:8, tt * 128:(tt + 1) * 128],
                                              PT8[0:8, (tt % 2) * 512 + 128:(tt % 2) * 512 + 256])
                            v.drain()
                            v.nop().then_inc(sem["v"], 1)
                    cnt["v"] += 2 * NT

        if nstage >= 10:
            moe_route("m0")

        we_bf = dint("we_bf", [NE * D, D], BF16)

        def moe_experts(tag):
            with ExitStack() as c2:
                E2 = c2.enter_context
                XEG = E2(nc.sbuf_tensor(tag + "XEG", [128, NE * ND * 512], BF16))
                WED = E2(nc.sbuf_tensor(tag + "WED", [128, 2 * 1024], BF16))
                GROW = E2(nc.sbuf_tensor(tag + "GROW", [1, 2 * T], BF16))
                GBS = E2(nc.sbuf_tensor(tag + "GBS", [128, 2 * 512], BF16))
                GBA = E2(nc.sbuf_tensor(tag + "GBA", [128, NE * NG * 512], BF16))
                OTM = E2(nc.sbuf_tensor(tag + "OTM", [128, ND * 512], F32))
                PM = E2(nc.psum_tensor(tag + "PM", [128, ND * 512], F32))

                with nc.Block() as blk:
                    @blk.gpsimd
                    def _(g):
                        g.wait_ge(sem["lg"], cnt["lg"])
                        for e in range(NE):
                            g.dma_start(out=we_bf.ap()[e * D:(e + 1) * D, :],
                                        in_=DI["l1we"].ap()[e, :, :]).then_inc(sem["lg"], 16)
                        g.wait_ge(sem["lg"], cnt["lg"] + 16 * NE)
                    cnt["lg"] += 16 * NE

                NJ = NE * NG
                with nc.Block() as blk:
                    pg0 = {0: cnt["r0"], 1: cnt["r1"]}
                    tm0 = cnt["t"]
                    vm0 = cnt["v"]

                    @blk.sync
                    def _(sy):
                        for p_ in ["r0", "r1"]:
                            sy.wait_ge(sem[p_], cnt[p_])
                        for j in range(NJ):
                            e, jg = divmod(j, NG)
                            fp = j % 2
                            sy.wait_ge(sem["r%d" % fp], pg0[fp] + 16 * (j // 2))
                            if j >= 2:
                                sy.wait_ge(sem["t"], tm0 + (j - 2) + 1)
                            sy.dma_start(out=GROW[0:1, fp * T:fp * T + 512],
                                         in_=bass.AP(GT, e * T + jg * 512,
                                                     [[T, 1], [1, 512]])
                                         ).then_inc(sem["r%d" % fp], 16)
                        for p_ in [0, 1]:
                            sy.wait_ge(sem["r%d" % p_], pg0[p_] + 16 * NJ // 2)
                    cnt["r0"] += 16 * NJ // 2
                    cnt["r1"] += 16 * NJ // 2

                    @blk.tensor
                    def _(t):
                        for j in range(NJ):
                            fp = j % 2
                            t.wait_ge(sem["r%d" % fp], pg0[fp] + 16 * (j // 2 + 1))
                            if j >= 2:
                                t.wait_ge(sem["v"], vm0 + (j - 2) + 1)
                            t.matmul(PM[:, (j % 2) * 512:(j % 2 + 1) * 512], ONE1B[:, :],
                                     GROW[0:1, fp * T:fp * T + 512],
                                     start=True, stop=True).then_inc(sem["t"], 1)
                    cnt["t"] += NJ

                    @blk.vector
                    def _(v):
                        for j in range(NJ):
                            v.wait_ge(sem["t"], tm0 + j + 1)
                            v.tensor_copy(GBA[:, j * 512:(j + 1) * 512],
                                          PM[:, (j % 2) * 512:(j % 2 + 1) * 512]
                                          ).then_inc(sem["v"], 1)
                    cnt["v"] += NJ

                for g4 in range(NG):
                    with nc.Block() as blk:
                        we0 = {0: cnt["r0"], 1: cnt["r1"]}
                        tx0 = cnt["t"]
                        vx0 = cnt["v"]

                        @blk.vector
                        def _(v):
                            for e in range(NE):
                                for di in range(ND):
                                    mmv = v.tensor_tensor(
                                        XEG[:, (e * ND + di) * 512:(e * ND + di + 1) * 512],
                                        XN[:, di * T + g4 * 512:di * T + (g4 + 1) * 512],
                                        GBA[:, (e * NG + g4) * 512:(e * NG + g4 + 1) * 512],
                                        ALU.mult)
                                mmv.then_inc(sem["v"], 1)
                        cnt["v"] += NE

                        @blk.sync
                        def _(sy):
                            for p_ in ["r0", "r1"]:
                                sy.wait_ge(sem[p_], cnt[p_])
                            for i in range(NE * ND):
                                e, di = divmod(i, ND)
                                fp = i % 2
                                sy.wait_ge(sem["r%d" % fp], we0[fp] + 16 * (i // 2))
                                if i >= 2:
                                    sy.wait_ge(sem["t"], tx0 + (i - 2) + 1)
                                sy.dma_start(out=WED[:, fp * 1024:(fp + 1) * 1024],
                                             in_=we_bf.ap()[e * D + di * 128:e * D + (di + 1) * 128, :]
                                             ).then_inc(sem["r%d" % fp], 16)
                            for p_ in [0, 1]:
                                sy.wait_ge(sem["r%d" % p_], we0[p_] + 16 * NE * ND // 2)
                        cnt["r0"] += 16 * NE * ND // 2
                        cnt["r1"] += 16 * NE * ND // 2

                        @blk.tensor
                        def _(t):
                            for i in range(NE * ND):
                                e, di = divmod(i, ND)
                                fp = i % 2
                                t.wait_ge(sem["r%d" % fp], we0[fp] + 16 * (i // 2 + 1))
                                t.wait_ge(sem["v"], vx0 + e + 1)
                                for o in range(ND):
                                    mm = t.matmul(PM[:, o * 512:(o + 1) * 512],
                                                  WED[:, fp * 1024 + o * 128:fp * 1024 + (o + 1) * 128],
                                                  XEG[:, i * 512:(i + 1) * 512],
                                                  start=(i == 0), stop=(i == NE * ND - 1))
                                mm.then_inc(sem["t"], 1)
                        cnt["t"] += NE * ND
                        tx1 = cnt["t"]

                        @blk.vector
                        def _(v):
                            v.wait_ge(sem["t"], tx1)
                            for o in range(ND):
                                mmv = v.tensor_copy(OTM[:, o * 512:(o + 1) * 512],
                                                    PM[:, o * 512:(o + 1) * 512])
                            mmv.then_inc(sem["v"], 1)
                        cnt["v"] += 1
                        vx1 = cnt["v"]

                        @blk.sync
                        def _(sy):
                            sy.wait_ge(sem["v"], vx1)
                            sy.wait_ge(sem["p0"], cnt["p0"])
                            for o in range(ND):
                                sy.dma_start(out=moe_d.ap()[o * 128:(o + 1) * 128,
                                                            g4 * 512:(g4 + 1) * 512],
                                             in_=OTM[:, o * 512:(o + 1) * 512]
                                             ).then_inc(sem["p0"], 16)
                            sy.wait_ge(sem["p0"], cnt["p0"] + 16 * ND)
                        cnt["p0"] += 16 * ND

        if nstage >= 11:
            moe_experts("e0")

        def l1_attn(tag):
            with ExitStack() as c2:
                E2 = c2.enter_context
                KC2 = E2(nc.sbuf_tensor(tag + "KC2", [128, 2 * 2 * T], BF16))
                QC2 = E2(nc.sbuf_tensor(tag + "QC2", [128, 2 * T], BF16))
                VH2 = E2(nc.sbuf_tensor(tag + "VH2", [128, 2 * 2 * VW], BF16))
                EXT2 = E2(nc.sbuf_tensor(tag + "EXT2", [128, 3 * 1024], BF16))
                CTG3 = E2(nc.sbuf_tensor(tag + "CTG3", [128, 2 * T], BF16))
                SH3 = E2(nc.sbuf_tensor(tag + "SH3", [1, 2 * T], F32))
                S128b = E2(nc.sbuf_tensor(tag + "S128b", [128, 256], F32))
                R128b = E2(nc.sbuf_tensor(tag + "R128b", [128, 256], BF16))
                RBr2 = E2(nc.sbuf_tensor(tag + "RBr2", [1, 2 * T], BF16))
                CTF2 = E2(nc.sbuf_tensor(tag + "CTF2", [128, 2 * T], BF16))
                PS2 = E2(nc.psum_tensor(tag + "PS2", [128, 3 * 1024], F32))
                PC2 = E2(nc.psum_tensor(tag + "PC2", [128, 2 * 512], F32))

                NT2 = 2 * NT
                t0 = cnt["t"]; v0 = cnt["v"]; s0 = cnt["s"]
                kq0 = {0: cnt["k0"], 1: cnt["k1"]}
                vh0 = {0: cnt["r2"], 1: cnt["r3"]}
                pc0 = {0: cnt["p0"], 1: cnt["p1"]}
                sf0 = {0: cnt["p2"], 1: cnt["p3"]}
                TI_Q = 32
                SI_Q = 16

                with nc.Block() as blk:
                    @blk.gpsimd
                    def _(g):
                        g.wait_ge(sem["cc"], cc_kv)
                        g.nop().then_inc(sem["cc"], 1)
                    cnt["cc"] += 1
                    cc_r = cnt["cc"]

                    @blk.sync
                    def _(sy):
                        for p_ in ["k0", "k1", "r2", "r3", "p0", "p1", "p2", "p3"]:
                            sy.wait_ge(sem[p_], cnt[p_])
                        sy.wait_ge(sem["cc"], cc_r)
                        for h in range(H):
                            c = h // 2
                            cp = c % 2
                            hp = h % 2
                            if h % 2 == 0:
                                ksem = "k%d" % cp
                                sy.wait_ge(sem[ksem], kq0[cp] + 48 * (c // 2))
                                if c >= 2:
                                    sy.wait_ge(sem["t"], t0 + TI_Q * NG * (2 * (c - 2) + 2))
                                for r in range(2):
                                    sy.dma_start(
                                        out=KC2[:, cp * 2 * T + r * T:cp * 2 * T + (r + 1) * T],
                                        in_=bass.AP(k1f_d, r * D * T + c * 128 * T,
                                                    [[T, 128], [1, T]])
                                        ).then_inc(sem[ksem], 16)
                                sy.dma_start(out=QC2[:, cp * T:(cp + 1) * T],
                                             in_=q1_d.ap()[c * 128:(c + 1) * 128, :]
                                             ).then_inc(sem[ksem], 16)
                            vsem = "r%d" % (2 + hp)
                            sy.wait_ge(sem[vsem], vh0[hp] + 32 * (h // 2))
                            if h >= 2:
                                sy.wait_ge(sem["t"], t0 + TI_Q * NG * (h - 1))
                            for r in range(2):
                                sy.dma_start(
                                    out=VH2[:, hp * 2 * VW + r * VW:hp * 2 * VW + (r + 1) * VW],
                                    in_=bass.AP(v1f_d, r * NT * 128 * VW + h * 65,
                                                [[VW, 128], [128 * VW, NT], [1, 65]])
                                    ).then_inc(sem[vsem], 16)
                            psem = "p%d" % hp
                            sy.wait_ge(sem[psem], pc0[hp] + 16 * (h // 2))
                            sy.wait_ge(sem["v"], v0 + 5 * (h + 1))
                            sy.dma_start(out=ctxb_d.ap()[h * 65:(h + 1) * 65, :],
                                         in_=CTG3[0:65, hp * T:(hp + 1) * T]
                                         ).then_inc(sem[psem], 16)
                            fsem = "p%d" % (2 + hp)
                            sy.wait_ge(sem[fsem], sf0[hp] + 16 * (h // 2))
                            sy.wait_ge(sem["v"], v0 + 5 * (h + 1))
                            sy.dma_start(out=bass.AP(S128b, h * 16, [[256, 128], [1, 16]]),
                                         in_=SH3[0:1, hp * T:(hp + 1) * T]
                                         ).then_inc(sem[fsem], 16)
                        for hh2, sname, base_, amt in [(0, "k0", kq0[0], 48 * 4),
                                                       (0, "k1", kq0[1], 48 * 4),
                                                       (0, "r2", vh0[0], 32 * 8),
                                                       (0, "r3", vh0[1], 32 * 8),
                                                       (0, "p0", pc0[0], 16 * 8),
                                                       (0, "p1", pc0[1], 16 * 8),
                                                       (0, "p2", sf0[0], 16 * 8),
                                                       (0, "p3", sf0[1], 16 * 8)]:
                            sy.wait_ge(sem[sname], base_ + amt)
                    cnt["k0"] += 48 * 4
                    cnt["k1"] += 48 * 4
                    cnt["r2"] += 32 * 8
                    cnt["r3"] += 32 * 8
                    for p_ in range(4):
                        cnt["p%d" % p_] += 16 * 8

                    @blk.tensor
                    def _(t):
                        for h in range(H):
                            c = h // 2
                            cp = c % 2
                            hp = h % 2
                            base = 64 * hp
                            t.wait_ge(sem["k%d" % cp], kq0[cp] + 48 * (c // 2 + 1))
                            t.wait_ge(sem["r%d" % (2 + hp)], vh0[hp] + 32 * (h // 2 + 1))
                            if h >= 1:
                                t.wait_ge(sem["v"], v0 + 5 * h)
                            for qg in range(NG):
                                if qg >= 2:
                                    t.wait_ge(sem["v"], v0 + 5 * h + qg - 1)

                                def scores_u(u):
                                    iq = SI_Q * (h * NG + qg) + u
                                    if iq >= 3:
                                        t.wait_ge(sem["s"], s0 + iq - 2)
                                    for k4 in range(2):
                                        kt = u * 2 + k4
                                        mm = t.matmul(
                                            PS2[:, (iq % 3) * 1024 + k4 * 512:
                                                (iq % 3) * 1024 + (k4 + 1) * 512],
                                            KC2[base:base + 64,
                                                cp * 2 * T + kt * 128:cp * 2 * T + (kt + 1) * 128],
                                            QC2[base:base + 64,
                                                cp * T + qg * 512:cp * T + (qg + 1) * 512],
                                            start=True, stop=True)
                                    mm.then_inc(sem["t"], 1)

                                scores_u(0)
                                scores_u(1)
                                for u in range(16):
                                    iq = SI_Q * (h * NG + qg) + u
                                    if u + 2 < 16:
                                        scores_u(u + 2)
                                    t.wait_ge(sem["s"], s0 + iq + 1)
                                    for k4 in range(2):
                                        kt = u * 2 + k4
                                        mm = t.matmul(
                                            PC2[0:65, (qg % 2) * 512:(qg % 2 + 1) * 512],
                                            VH2[:, hp * 2 * VW + kt * 65:hp * 2 * VW + (kt + 1) * 65],
                                            EXT2[:, (iq % 3) * 1024 + k4 * 512:
                                                 (iq % 3) * 1024 + (k4 + 1) * 512],
                                            start=(u == 0 and k4 == 0),
                                            stop=(u == 15 and k4 == 1))
                                    mm.then_inc(sem["t"], 1)
                    cnt["t"] += H * NG * TI_Q

                    @blk.scalar
                    def _(s):
                        for h in range(H):
                            for qg in range(NG):
                                for u in range(16):
                                    iq = SI_Q * (h * NG + qg) + u
                                    jsc = u if u < 2 else 2 * u - 2
                                    s.wait_ge(sem["t"], t0 + TI_Q * (h * NG + qg) + jsc + 1)
                                    s.activation(EXT2[:, (iq % 3) * 1024:(iq % 3 + 1) * 1024],
                                                 PS2[:, (iq % 3) * 1024:(iq % 3 + 1) * 1024],
                                                 AF.Exp, bias=NM2[:, h:h + 1],
                                                 scale=0.125).then_inc(sem["s"], 1)
                    cnt["s"] += H * NG * SI_Q

                    @blk.vector
                    def _(v):
                        for h in range(H):
                            hp = h % 2
                            if h >= 2:
                                v.wait_ge(sem["p%d" % hp], pc0[hp] + 16 * (h // 2))
                                v.wait_ge(sem["p%d" % (2 + hp)], sf0[hp] + 16 * (h // 2))
                            for qg in range(NG):
                                v.wait_ge(sem["t"], t0 + TI_Q * (h * NG + qg + 1))
                                v.tensor_copy(
                                    CTG3[0:65, hp * T + qg * 512:hp * T + (qg + 1) * 512],
                                    PC2[0:65, (qg % 2) * 512:(qg % 2 + 1) * 512])
                                v.tensor_copy(SH3[0:1, hp * T + qg * 512:hp * T + (qg + 1) * 512],
                                              PC2[64:65, (qg % 2) * 512:(qg % 2 + 1) * 512]
                                              ).then_inc(sem["v"], 1)
                            v.nop().then_inc(sem["v"], 1)
                    cnt["v"] += H * (NG + 1)

                with nc.Block() as blk:
                    @blk.vector
                    def _(v):
                        v.tensor_scalar(S128b[:, :], S128b[:, :], 1e-30, None, ALU.max)
                        v.drain()
                        with nc.allow_low_precision(reason="bf16 normalizer broadcast"):
                            v.reciprocal(R128b[:, :], S128b[:, :])
                        v.drain()
                        v.nop().then_inc(sem["v"], 1)
                    cnt["v"] += 1

                t1 = cnt["t"]; v1 = cnt["v"]
                pr0 = {0: cnt["r0"], 1: cnt["r1"]}
                pf0 = {0: cnt["p0"], 1: cnt["p1"]}
                with nc.Block() as blk:
                    @blk.sync
                    def _(sy):
                        for p_ in ["r0", "r1", "p0", "p1"]:
                            sy.wait_ge(sem[p_], cnt[p_])
                        for h in range(H):
                            hp = h % 2
                            sy.wait_ge(sem["r%d" % hp], pr0[hp] + 16 * (h // 2))
                            if h >= 2:
                                sy.wait_ge(sem["t"], t1 + h - 1)
                            sy.dma_start(out=RBr2[0:1, hp * T:(hp + 1) * T],
                                         in_=bass.AP(R128b, h * 16, [[256, 128], [1, 16]])
                                         ).then_inc(sem["r%d" % hp], 16)
                            sy.wait_ge(sem["p%d" % hp], pf0[hp] + 32 * (h // 2))
                            if h >= 2:
                                sy.wait_ge(sem["v"], v1 + h - 1)
                            sy.dma_start(out=CTG3[0:65, hp * T:(hp + 1) * T],
                                         in_=ctxb_d.ap()[h * 65:(h + 1) * 65, :]
                                         ).then_inc(sem["p%d" % hp], 16)
                            sy.wait_ge(sem["v"], v1 + h + 1)
                            sy.dma_start(out=ctf2_d.ap()[h * 64:(h + 1) * 64, :],
                                         in_=CTF2[0:64, hp * T:(hp + 1) * T]
                                         ).then_inc(sem["p%d" % hp], 16)
                        sy.wait_ge(sem["r0"], pr0[0] + 16 * 8)
                        sy.wait_ge(sem["r1"], pr0[1] + 16 * 8)
                        sy.wait_ge(sem["p0"], pf0[0] + 32 * 8)
                        sy.wait_ge(sem["p1"], pf0[1] + 32 * 8)
                    for hh in range(H):
                        cnt["r%d" % (hh % 2)] += 16
                        cnt["p%d" % (hh % 2)] += 32

                    @blk.tensor
                    def _(t):
                        for h in range(H):
                            hp = h % 2
                            t.wait_ge(sem["r%d" % hp], pr0[hp] + 16 * (h // 2 + 1))
                            if h >= 1:
                                t.wait_ge(sem["v"], v1 + h)
                            for g4 in range(NG):
                                mm = t.matmul(PS2[:, g4 * 512:(g4 + 1) * 512], ONE1B[:, :],
                                              RBr2[0:1, hp * T + g4 * 512:hp * T + (g4 + 1) * 512],
                                              start=True, stop=True)
                            mm.then_inc(sem["t"], 1)
                    cnt["t"] += H

                    @blk.vector
                    def _(v):
                        for h in range(H):
                            hp = h % 2
                            v.wait_ge(sem["t"], t1 + h + 1)
                            v.wait_ge(sem["p%d" % hp], pf0[hp] + 32 * (h // 2) + 16)
                            v.tensor_tensor(CTF2[0:64, hp * T:(hp + 1) * T],
                                            CTG3[0:64, hp * T:(hp + 1) * T],
                                            PS2[0:64, 0:T], ALU.mult).then_inc(sem["v"], 1)
                    cnt["v"] += H

        if nstage >= 12:
            l1_attn("a1")

        if nstage >= 13:
            with ExitStack() as c5:
                CTFS2 = c5.enter_context(nc.sbuf_tensor("CTFS2", [128, ND * T], BF16))
                with nc.Block() as blk:
                    @blk.sync
                    def _(sy):
                        sy.wait_ge(sem["ld"], cnt["ld"])
                        for c in range(ND):
                            sy.dma_start(out=CTFS2[:, c * T:(c + 1) * T],
                                         in_=ctf2_d.ap()[c * 128:(c + 1) * 128, :]
                                         ).then_inc(sem["ld"], 16)
                        cnt["ld"] += 16 * ND
                        sy.wait_ge(sem["ld"], cnt["ld"])
                proj_fm(DI["l1wo"].ap(), DI["l1bo"], out_d.ap(), "pfo",
                        src_sb=CTFS2, res_dram=x2_d.ap(), res2_dram=moe_d.ap(),
                        out_f32=True)


        if nstage < 13:
            with nc.Block() as blk:
                @blk.gpsimd
                def _(g):
                    g.wait_ge(sem["sg"], cnt["sg"])
                    for c in range(ND):
                        g.dma_start(out=out_d.ap()[c * 128:(c + 1) * 128, :],
                                    in_=XN[:, c * T:(c + 1) * T]).then_inc(sem["sg"], 16)
                    cnt["sg"] += 16 * ND
                    g.wait_ge(sem["sg"], cnt["sg"])

    return nc


def _masks(half):
    t0 = half * T
    L = np.zeros((128, 128), np.float32)
    for b0 in (0, 64):
        L[b0 + 0, :] = 1.0
        L[b0 + 1, 0:64] = BIG
        L[b0 + 2, 64:128] = BIG
        L[b0 + 3, :] = -BIG
    gq = np.zeros(T, np.float32)
    for pos in GLOBALS:
        if t0 <= pos < t0 + T:
            gq[pos - t0] = 1.0
    qmod = np.arange(T) % 128
    R = np.zeros((128, T), np.float32)
    for b0 in (0, 64):
        R[b0 + 0, :] = -BIG
        R[b0 + 1, :] = (qmod < 64)
        R[b0 + 2, :] = (qmod >= 64)
        R[b0 + 3, :] = gq
    GL = np.zeros((128, 128), np.float32)
    for b0 in (0, 64):
        for g in range(8):
            GL[b0 + g, g] = -BIG
        GL[b0 + 8, 0:8] = -BIG
    GR = np.zeros((128, T), np.float32)
    for b0 in (0, 64):
        for g, pos in enumerate(GLOBALS):
            blk0 = (pos // 64) * 64
            if t0 <= blk0 < t0 + T:
                GR[b0 + g, blk0 - t0:blk0 - t0 + 64] = 1.0
        GR[b0 + 8, :] = gq
    SC = np.zeros((8, T), np.float32)
    for g, pos in enumerate(GLOBALS):
        if t0 <= pos < t0 + T:
            SC[g, pos - t0] = 1.0
    return {"maskl_lhs": _bf(L), "maskl_rhs": _bf(R), "maskg_lhs": _bf(GL),
            "maskg_rhs": _bf(GR), "scat_s": _bf(SC),
            "odd1": np.full((1, 1), float(half), np.float32)}


def _host_prep(inputs):
    x = np.asarray(inputs["L0_attn_x"], np.float32)
    shared = {
        "identf": np.eye(128, dtype=np.float32),
        "identb": _bf(np.eye(128, dtype=np.float32)),
        "ones1b": _bf(np.ones((1, 128))),
        "ones1f": np.ones((1, 128), np.float32),
        "onescol": _bf(np.ones((128, 1))),
        "onespair": _bf(np.stack([np.r_[np.ones(64), np.zeros(64)],
                                  np.r_[np.zeros(64), np.ones(64)]], 1)),
        "iota_e": np.tile(np.arange(NE, dtype=np.float32), (128, 1)),
        "revta": np.tile(8.0 - np.arange(NE, dtype=np.float32), (128, 1)),
        "l0wq": np.asarray(inputs["L0_attn_Wq"], np.float32),
        "l0wk": np.asarray(inputs["L0_attn_Wk"], np.float32),
        "l0wv": np.asarray(inputs["L0_attn_Wv"], np.float32),
        "l0wo": np.asarray(inputs["L0_attn_Wo"], np.float32),
        "l0bq": np.asarray(inputs["L0_attn_bq"], np.float32),
        "l0bk": np.asarray(inputs["L0_attn_bk"], np.float32),
        "l0bv": np.asarray(inputs["L0_attn_bv"], np.float32),
        "l0bo": np.asarray(inputs["L0_attn_bo"], np.float32),
        "l0winT": np.ascontiguousarray(np.asarray(inputs["L0_ffn_W_in"], np.float32).T),
        "l0bin": np.asarray(inputs["L0_ffn_b_in"], np.float32),
        "l0wout": np.asarray(inputs["L0_ffn_W_out"], np.float32),
        "l0bout": np.asarray(inputs["L0_ffn_b_out"], np.float32),
        "l1wq": np.ascontiguousarray(np.asarray(inputs["L1_attn_q_w"], np.float32).T),
        "l1wk": np.ascontiguousarray(np.asarray(inputs["L1_attn_k_w"], np.float32).T),
        "l1wv": np.ascontiguousarray(np.asarray(inputs["L1_attn_v_w"], np.float32).T),
        "l1wo": np.ascontiguousarray(np.asarray(inputs["L1_attn_out_w"], np.float32).T),
        "l1bq": np.asarray(inputs["L1_attn_q_b"], np.float32),
        "l1bk": np.asarray(inputs["L1_attn_k_b"], np.float32),
        "l1bv": np.asarray(inputs["L1_attn_v_b"], np.float32),
        "l1bo": np.asarray(inputs["L1_attn_out_b"], np.float32),
        "l1wrT": np.ascontiguousarray(np.asarray(inputs["L1_ffn_Wr"], np.float32).T),
        "l1we": np.asarray(inputs["L1_ffn_We"], np.float32),
    }
    maps = []
    for c in range(8):
        b, half = c // 2, c % 2
        m = dict(shared)
        m["xf"] = np.ascontiguousarray(x[b, half * T:(half + 1) * T, :].T)
        m["xg"] = np.ascontiguousarray(x[b, GLOBALS, :])
        m.update(_masks(half))
        maps.append(m)
    return maps


def kernel(**inputs):
    maps = _host_prep(inputs)
    nc = build_nc(taps=_TAPS)
    res = run_bass_kernel_spmd(nc, maps, core_ids=list(range(8)))
    full = np.zeros((B, S, D), np.float32)
    for c in range(8):
        b, half = c // 2, c % 2
        full[b, half * T:(half + 1) * T, :] = np.asarray(
            res.results[c]["out"], np.float32).T
    return full
